# version 06
# baseline (speedup 1.0000x reference)
import contextlib
import math
import numpy as np
import concourse.bass as bass
import concourse.mybir as mybir
from concourse.bass_utils import run_bass_kernel_spmd

F32 = mybir.dt.float32
BF16 = mybir.dt.bfloat16
I32 = mybir.dt.int32
U8 = mybir.dt.uint8
AF = mybir.ActivationFunctionType
ALU = mybir.AluOpType
AX = mybir.AxisListType

NCORES = 8
B, L, D = 8, 2048, 1024
TOK = 2048
DCH = D // 128
TB = 512
NTB = TOK // TB

LN_EPS = 1e-5


class Prog:
    ENGINES = ("sync", "act", "dve", "pe", "pool")
    BLOCKFN = {"sync": "sync", "act": "scalar", "dve": "vector",
               "pe": "tensor", "pool": "gpsimd"}

    def __init__(self, nc):
        self.nc = nc
        self.ops = []
        self.count = 0

    def op(self, engine, fn, wait=None):
        w = self.count if wait is None else wait
        self.ops.append((engine, fn, 1, w))
        self.count += 1

    def dma(self, engine, fn, wait=None):
        w = self.count if wait is None else wait
        self.ops.append((engine, fn, 16, w))
        self.count += 16

    def mark(self):
        return self.count

    EPOCH = 40000

    def run(self, block, sems):
        if not isinstance(sems, (list, tuple)):
            sems = [sems]
        bounds = [0]
        cur = 0
        starts = []
        for (e, fn, inc, w) in self.ops:
            if cur + inc - bounds[-1] > self.EPOCH:
                bounds.append(cur)
            starts.append(cur)
            cur += inc
        nep = len(bounds)
        assert nep <= len(sems), f"need {nep} semaphores, got {len(sems)}"
        totals = [(bounds[i + 1] if i + 1 < nep else self.count) - bounds[i]
                  for i in range(nep)]

        def epoch_of(cnt):
            ei = 0
            while ei + 1 < nep and cnt >= bounds[ei + 1]:
                ei += 1
            return ei

        for engine_name in self.ENGINES:
            eng_ops = [(fn, inc, w, st) for (e, fn, inc, w), st
                       in zip(self.ops, starts) if e == engine_name]
            if not eng_ops and engine_name != "sync":
                continue

            def section(eng, eng_ops=eng_ops, engine_name=engine_name):
                fenced = -1
                last_w = {}
                for fn, inc, w, st in eng_ops:
                    ep = epoch_of(st)
                    epw = epoch_of(w)
                    for j in range(fenced + 1, epw):
                        eng.wait_ge(sems[j], totals[j])
                    fenced = max(fenced, epw - 1)
                    rel = w - bounds[epw]
                    if rel > 0 and rel > last_w.get(epw, -1):
                        eng.wait_ge(sems[epw], rel)
                        last_w[epw] = rel
                    fn(eng).then_inc(sems[ep], inc)
                if engine_name == "sync":
                    for j in range(nep):
                        if totals[j] > max(last_w.get(j, -1), 0) or j > fenced:
                            eng.wait_ge(sems[j], totals[j])

            getattr(block, self.BLOCKFN[engine_name])(section)


_SB_UID = [0]


class SB:
    def __init__(self, nc, ctx):
        self.nc = nc
        self.ctx = ctx

    def _u(self, name):
        _SB_UID[0] += 1
        return f"{name}_{_SB_UID[0]}"

    def t(self, name, shape, dtype):
        return self.ctx.enter_context(self.nc.sbuf_tensor(self._u(name), shape, dtype))

    def p(self, name, shape, dtype=F32):
        return self.ctx.enter_context(self.nc.psum_tensor(self._u(name), shape, dtype))



def emit_softplus(p, out, in_, scratch, limit=None):
    p.op("act", lambda e: e.activation(scratch, in_, AF.Abs))
    p.op("act", lambda e: e.activation(scratch, scratch, AF.Exp, scale=-1.0))
    p.op("act", lambda e: e.activation(scratch, scratch, AF.Ln, bias=1.0))
    p.op("dve", lambda e: e.tensor_scalar_max(out, in_, 0.0))
    p.op("dve", lambda e: e.tensor_add(out, out, scratch))
    if limit is not None:
        p.op("dve", lambda e: e.tensor_scalar_min(out, out, float(limit)))


def emit_ln_stats(p, x_chunks, sq, ps_s, ps_q, ones_col, mrow, rrow,
                  m_b, r_b, ones_row, ps_bc, scr_row, eps=LN_EPS, n=D,
                  ntb=NTB, tb_sz=TB):
    nch = len(x_chunks)
    for tb in range(ntb):
        sl = slice(tb * tb_sz, (tb + 1) * tb_sz)
        for k in range(nch):
            p.op("pe", lambda e, sl=sl, k=k: e.matmul(
                ps_s[:, :], ones_col, x_chunks[k][:, sl],
                start=(k == 0), stop=(k == nch - 1)))
        p.op("act", lambda e, sl=sl: e.activation(
            mrow[:, sl], ps_s[:, :], AF.Copy, scale=1.0 / n))
        for k in range(nch):
            p.op("act", lambda e, sl=sl, k=k: e.activation(
                sq[:, 0:tb_sz], x_chunks[k][:, sl], AF.Square))
            p.op("pe", lambda e, k=k: e.matmul(
                ps_q[:, :], ones_col, sq[:, 0:tb_sz],
                start=(k == 0), stop=(k == nch - 1)))
        p.op("act", lambda e, sl=sl: e.activation(
            rrow[:, sl], ps_q[:, :], AF.Copy, scale=1.0 / n))
    p.op("dve", lambda e: e.tensor_tensor(scr_row, mrow, mrow, ALU.mult))
    p.op("dve", lambda e: e.tensor_sub(rrow, rrow, scr_row))
    p.op("dve", lambda e: e.tensor_scalar_add(rrow, rrow, float(eps)))
    p.op("act", lambda e: e.activation(rrow, rrow, AF.Sqrt))
    p.op("dve", lambda e: e.reciprocal(rrow, rrow))
    for tb in range(ntb):
        sl = slice(tb * tb_sz, (tb + 1) * tb_sz)
        p.op("pe", lambda e, sl=sl: e.matmul(
            ps_bc[:, :], ones_row, mrow[:, sl], start=True, stop=True))
        p.op("dve", lambda e, sl=sl: e.tensor_copy(m_b[:, sl], ps_bc[:, :]))
        p.op("pe", lambda e, sl=sl: e.matmul(
            ps_bc[:, :], ones_row, rrow[:, sl], start=True, stop=True))
        p.op("dve", lambda e, sl=sl: e.tensor_copy(r_b[:, sl], ps_bc[:, :]))


def emit_ln_rows(p, x_chunks, sq, ps_s, ps_q, ones_col, mrow, rrow, scr_row,
                 eps=LN_EPS, n=D, ntb=NTB, tb_sz=TB):
    nch = len(x_chunks)
    for tb in range(ntb):
        sl = slice(tb * tb_sz, (tb + 1) * tb_sz)
        for k in range(nch):
            p.op("pe", lambda e, sl=sl, k=k: e.matmul(
                ps_s[:, :], ones_col, x_chunks[k][:, sl],
                start=(k == 0), stop=(k == nch - 1)))
        p.op("act", lambda e, sl=sl: e.activation(
            mrow[:, sl], ps_s[:, :], AF.Copy, scale=1.0 / n))
        for k in range(nch):
            p.op("act", lambda e, sl=sl, k=k: e.activation(
                sq[:, 0:tb_sz], x_chunks[k][:, sl], AF.Square))
            p.op("pe", lambda e, k=k: e.matmul(
                ps_q[:, :], ones_col, sq[:, 0:tb_sz],
                start=(k == 0), stop=(k == nch - 1)))
        p.op("act", lambda e, sl=sl: e.activation(
            rrow[:, sl], ps_q[:, :], AF.Copy, scale=1.0 / n))
    p.op("dve", lambda e: e.tensor_tensor(scr_row, mrow, mrow, ALU.mult))
    p.op("dve", lambda e: e.tensor_sub(rrow, rrow, scr_row))
    p.op("dve", lambda e: e.tensor_scalar_add(rrow, rrow, float(eps)))
    p.op("act", lambda e: e.activation(rrow, rrow, AF.Sqrt))
    p.op("dve", lambda e: e.reciprocal(rrow, rrow))



def emit_l0a(nc, p, io):
    ECH = 16

    xT = io["xT"]
    w_inT = io["w_inT"]
    w_stateT = io["w_stateT"]
    w_dtT = io["w_dtT"]
    b_dt = io["b_dt"]
    a_log = io["a_log"]
    w_cT = io["w_cT"]
    dvec = io["dvec"]
    w_outT = io["w_outT"]
    ones_col = io["ones_col"]
    ones_row = io["ones_row"]

    o_xn = io["o_xn"]
    o_s0 = io["o_s0"]

    w0 = p.mark()
    with contextlib.ExitStack() as ctx:
        sb = SB(nc, ctx)
        xt_h = ctx.enter_context(nc.sbuf_tensor("xt_f", [128, DCH * TOK], F32))
        xt = xt_h.ap()
        ug_h = xt_h.bitcast(BF16)
        ug = ug_h.ap()
        xnbf = sb.t("xnbf", [128, DCH * TOK], BF16)
        w_in_sb = sb.t("w_in_sb", [128, DCH * 2048], BF16)
        w_out_sb = sb.t("w_out_sb", [128, DCH * D], BF16)
        w_state_sb = sb.t("w_state_sb", [128, DCH * 128], BF16)
        w_dt_sb = sb.t("w_dt_sb", [128, DCH * 128], BF16)
        w_c_sb = sb.t("w_c_sb", [128, D], BF16)
        onec = sb.t("onec", [128, 1], F32)
        oner = sb.t("oner", [1, 128], F32)
        rrow = sb.t("rrow", [1, TOK], F32)
        scr1 = sb.t("scr1", [128, TOK], F32)
        scr2 = sb.t("scr2", [128, TOK], F32)
        b_t32 = sb.t("b_t32", [128, TOK], F32)
        dt32 = sb.t("dt32", [128, TOK], F32)
        decay = sb.t("decay", [128, TOK], F32)
        hsbf = sb.t("hsbf", [128, TOK], BF16)
        cols = sb.t("cols", [128, 4], F32)
        dcol = sb.t("dcol", [128, DCH], F32)
        ps_s = sb.p("ps_s", [1, TB], F32)
        ps_q = sb.p("ps_q", [1, TB], F32)
        ps_bc = sb.p("ps_bc", [128, TB], F32)
        ps_a = sb.p("ps_a", [128, TB], F32)
        ps_b = sb.p("ps_b", [128, TB], F32)


        xch = [xt[:, k * TOK:(k + 1) * TOK] for k in range(DCH)]
        m_b, r_b = b_t32, dt32
        mrow_t = decay
        mrow = mrow_t
        sout = scr2

        for k in range(DCH):
            p.dma("sync", lambda e, k=k: e.dma_start(
                out=xch[k], in_=xT[k * 128:(k + 1) * 128, :]), wait=w0)
        p.dma("sync", lambda e: e.dma_start(out=onec[:], in_=ones_col), wait=w0)
        p.dma("sync", lambda e: e.dma_start(out=oner[:], in_=ones_row), wait=w0)
        p.dma("sync", lambda e: e.dma_start(out=cols[:, 0:1], in_=b_dt), wait=w0)
        p.dma("sync", lambda e: e.dma_start(out=cols[:, 1:2], in_=a_log), wait=w0)
        p.dma("sync", lambda e: e.dma_start(out=dcol[:], in_=dvec), wait=w0)
        for k in range(DCH):
            p.dma("pool", lambda e, k=k: e.dma_start(
                out=w_in_sb[:, k * 2048:(k + 1) * 2048],
                in_=w_inT[k * 128:(k + 1) * 128, :]), wait=w0)
            p.dma("pool", lambda e, k=k: e.dma_start(
                out=w_out_sb[:, k * D:(k + 1) * D],
                in_=w_outT[k * 128:(k + 1) * 128, :]), wait=w0)
            p.dma("pool", lambda e, k=k: e.dma_start(
                out=w_state_sb[:, k * 128:(k + 1) * 128],
                in_=w_stateT[k * 128:(k + 1) * 128, :]), wait=w0)
            p.dma("pool", lambda e, k=k: e.dma_start(
                out=w_dt_sb[:, k * 128:(k + 1) * 128],
                in_=w_dtT[k * 128:(k + 1) * 128, :]), wait=w0)
        p.dma("pool", lambda e: e.dma_start(out=w_c_sb[:], in_=w_cT), wait=w0)

        emit_ln_stats(p, xch, scr2, ps_s, ps_q, onec[:], mrow[0:1, 0:TOK], rrow[:],
                      m_b[:], r_b[:], oner[:], ps_bc, scr1[0:1, 0:TOK])
        for k in range(DCH):
            p.op("dve", lambda e, k=k: e.tensor_sub(xch[k], xch[k], m_b[:]))
            p.op("dve", lambda e, k=k: e.tensor_mul(xch[k], xch[k], r_b[:]))
            p.op("dve", lambda e, k=k: e.tensor_copy(
                xnbf[:, k * TOK:(k + 1) * TOK], xch[k]))
            p.dma("sync", lambda e, k=k: e.dma_start(
                out=o_xn[k * 128:(k + 1) * 128, :], in_=xch[k]))

        xnch = [xnbf[:, k * TOK:(k + 1) * TOK] for k in range(DCH)]

        for ec in range(ECH):
            for tb in range(NTB):
                for k in range(DCH):
                    p.op("pe", lambda e, ec=ec, tb=tb, k=k: e.matmul(
                        ps_a[:, :],
                        w_in_sb[:, k * 2048 + ec * 128:k * 2048 + (ec + 1) * 128],
                        xnch[k][:, tb * TB:(tb + 1) * TB],
                        start=(k == 0), stop=(k == DCH - 1)))
                p.op("act", lambda e, ec=ec, tb=tb: e.activation(
                    ug[:, ec * TOK + tb * TB:ec * TOK + (tb + 1) * TB],
                    ps_a[:, :], AF.Copy))

        uch = [ug[:, k * TOK:(k + 1) * TOK] for k in range(DCH)]
        gch = [ug[:, (DCH + k) * TOK:(DCH + k + 1) * TOK] for k in range(DCH)]

        for tb in range(NTB):
            sl = slice(tb * TB, (tb + 1) * TB)
            for k in range(DCH):
                p.op("pe", lambda e, sl=sl, k=k: e.matmul(
                    ps_a[:, :], w_state_sb[:, k * 128:(k + 1) * 128],
                    uch[k][:, sl], start=(k == 0), stop=(k == DCH - 1)))
            p.op("act", lambda e, sl=sl: e.activation(
                b_t32[:, sl], ps_a[:, :], AF.Copy))
            for k in range(DCH):
                p.op("pe", lambda e, sl=sl, k=k: e.matmul(
                    ps_b[:, :], w_dt_sb[:, k * 128:(k + 1) * 128],
                    uch[k][:, sl], start=(k == 0), stop=(k == DCH - 1)))
            p.op("act", lambda e, sl=sl: e.activation(
                dt32[:, sl], ps_b[:, :], AF.Identity, bias=cols[:, 0:1]))

        emit_softplus(p, dt32[:], dt32[:], scr1[:], limit=10.0)
        emit_softplus(p, cols[:, 2:3], cols[:, 1:2], cols[:, 3:4])
        p.op("dve", lambda e: e.tensor_scalar_add(cols[:, 2:3], cols[:, 2:3], 1e-4))
        p.op("dve", lambda e: e.tensor_scalar_mul(scr1[:], dt32[:], cols[:, 2:3]))
        p.op("act", lambda e: e.activation(decay[:], scr1[:], AF.Exp, scale=-1.0))
        p.op("dve", lambda e: e.tensor_scalar(scr1[:], decay[:], -1.0, 1.0,
                                              ALU.mult, ALU.add))
        p.op("dve", lambda e: e.tensor_mul(scr1[:], scr1[:], b_t32[:]))
        p.op("dve", lambda e: e.tensor_tensor_scan(
            scr2[:], decay[:], scr1[:], 0.0, ALU.mult, ALU.add))
        p.op("dve", lambda e: e.tensor_copy(hsbf[:], scr2[:]))

        zch = uch
        for k in range(DCH):
            for tb in range(NTB):
                sl = slice(tb * TB, (tb + 1) * TB)
                p.op("pe", lambda e, k=k, sl=sl: e.matmul(
                    ps_a[:, :], w_c_sb[:, k * 128:(k + 1) * 128],
                    hsbf[:, sl], start=True, stop=True))
                p.op("act", lambda e, k=k, sl=sl: e.activation(
                    scr1[:, sl], gch[k][:, sl], AF.Silu))
                p.op("dve", lambda e, k=k, sl=sl: e.tensor_mul(
                    scr2[:, sl], ps_a[:, :], scr1[:, sl]))
            p.op("dve", lambda e, k=k: e.tensor_scalar(
                b_t32[:], xnch[k], dcol[:, k:k + 1], None, ALU.mult))
            p.op("dve", lambda e, k=k: e.tensor_add(scr2[:], scr2[:], b_t32[:]))
            p.op("dve", lambda e, k=k: e.tensor_copy(zch[k], scr2[:]))

        for ec in range(DCH):
            for tb in range(NTB):
                sl = slice(tb * TB, (tb + 1) * TB)
                for k in range(DCH):
                    p.op("pe", lambda e, ec=ec, sl=sl, k=k: e.matmul(
                        ps_b[:, :],
                        w_out_sb[:, k * D + ec * 128:k * D + (ec + 1) * 128],
                        zch[k][:, sl], start=(k == 0), stop=(k == DCH - 1)))
                p.op("act", lambda e, sl=sl: e.activation(
                    sout[:, sl], ps_b[:, :], AF.Copy))
            p.dma("sync", lambda e, ec=ec: e.dma_start(
                out=o_s0[ec * 128:(ec + 1) * 128, :], in_=sout[:]))



def build_l0a():
    nc = bass.Bass("TRN2", target_bir_lowering=False)
    io = {}
    io["xT"] = nc.dram_tensor("xT", [D, TOK], F32, kind="ExternalInput").ap()
    io["w_inT"] = nc.dram_tensor("w_inT", [D, 2048], F32, kind="ExternalInput").ap()
    io["w_stateT"] = nc.dram_tensor("w_stateT", [D, 128], F32, kind="ExternalInput").ap()
    io["w_dtT"] = nc.dram_tensor("w_dtT", [D, 128], F32, kind="ExternalInput").ap()
    io["b_dt"] = nc.dram_tensor("b_dt", [128, 1], F32, kind="ExternalInput").ap()
    io["a_log"] = nc.dram_tensor("a_log", [128, 1], F32, kind="ExternalInput").ap()
    io["w_cT"] = nc.dram_tensor("w_cT", [128, D], F32, kind="ExternalInput").ap()
    io["dvec"] = nc.dram_tensor("dvec", [128, DCH], F32, kind="ExternalInput").ap()
    io["w_outT"] = nc.dram_tensor("w_outT", [D, D], F32, kind="ExternalInput").ap()
    io["ones_col"] = nc.dram_tensor("ones_col", [128, 1], F32, kind="ExternalInput").ap()
    io["ones_row"] = nc.dram_tensor("ones_row", [1, 128], F32, kind="ExternalInput").ap()
    io["o_xn"] = nc.dram_tensor("o_xn", [D, TOK], F32, kind="ExternalOutput").ap()
    io["o_s0"] = nc.dram_tensor("o_s0", [D, TOK], F32, kind="ExternalOutput").ap()
    import contextlib as _cl
    with _cl.ExitStack() as _c:
        block = _c.enter_context(nc.Block())
        sem = [_c.enter_context(nc.semaphore(f"chain{i}")) for i in range(4)]
        p = Prog(nc)
        emit_l0a(nc, p, io)
        p.run(block, sem)
    return nc


def _prep_l0a_inputs(inputs):
    c = np.ascontiguousarray
    w = {}
    w["w_inT"] = c(inputs["L0_attn_W_in"].T)
    w["w_stateT"] = c(inputs["L0_attn_W_state"].T)
    w["w_dtT"] = c(inputs["L0_attn_W_dt"].T)
    w["b_dt"] = c(np.asarray(inputs["L0_attn_b_dt"], np.float32).reshape(128, 1))
    w["a_log"] = c(np.asarray(inputs["L0_attn_A_log"], np.float32).reshape(128, 1))
    w["w_cT"] = c(inputs["L0_attn_W_C"].T)
    w["dvec"] = c(np.asarray(inputs["L0_attn_D"], np.float32).reshape(DCH, 128).T)
    w["w_outT"] = c(inputs["L0_attn_W_out"].T)
    w["ones_col"] = np.ones((128, 1), np.float32)
    w["ones_row"] = np.ones((1, 128), np.float32)
    return {k: np.asarray(v, np.float32) for k, v in w.items()}


def run_l0a(inputs):
    x = np.asarray(inputs["L0_attn_x"], np.float32)
    w = _prep_l0a_inputs(inputs)
    nc = build_l0a()
    in_maps = []
    for b in range(NCORES):
        m = dict(w)
        m["xT"] = np.ascontiguousarray(x[b].T)
        in_maps.append(m)
    res = run_bass_kernel_spmd(nc, in_maps, core_ids=list(range(NCORES)))
    xn = np.stack([r["o_xn"] for r in res.results])
    s0 = np.stack([r["o_s0"] for r in res.results])
    return xn, s0



LCAP = 1536
NGRP = LCAP // 512
TRASH_IDX = 8 * LCAP
TRASH_PARTS = TOK * 2


def emit_l0b2(nc, p, io):

    ln2tok = io["ln2tok"]
    h1T = io["h1T"]
    topi = io["topi"]
    topp = io["topp"]
    counts_all = io["counts_all"]
    sel_g = io["sel_g"]
    sel_l = io["sel_l"]
    moe_w1 = io["moe_w1"]
    moe_w2 = io["moe_w2"]
    ones_row8 = io["ones_row8"]
    eiota = io["eiota"]
    tokhalf = io["tokhalf"]
    jiota = io["jiota"]
    ident_bf = io["ident_bf"]
    ident_f32 = io["ident_f32"]

    o_g = io["o_g"]

    ln2pad = nc.dram_tensor("b2_ln2pad", [TOK + 1, D], BF16).ap()
    idxbuf = nc.dram_tensor("b2_idxbuf", [8 * LCAP + 1, 4], F32).ap()
    parts = nc.dram_tensor("b2_parts", [2 * TOK + 1, D], F32).ap()

    w0 = p.mark()
    with contextlib.ExitStack() as ctx:
        sb = SB(nc, ctx)
        wspace_h = ctx.enter_context(
            nc.sbuf_tensor("wspace_f", [128, 2 * DCH * 2048], BF16))
        w1e = wspace_h.ap()[:, 0:DCH * 2048]
        w2e = wspace_h.ap()[:, DCH * 2048:2 * DCH * 2048]
        h1e = sb.t("h1e", [128, 16 * 512], BF16)
        xg = sb.t("xg", [128, D], BF16)
        xgT = sb.t("xgT", [128, DCH * 512], BF16)
        y2c = sb.t("y2c", [128, 512], F32)
        ytok = sb.t("ytok", [128, 4 * D], F32)
        moT = wspace_h.bitcast(F32).ap()
        zero_t = sb.t("zero_t", [128, D], F32)
        h1st = sb.t("h1st", [128, TOK], F32)
        oh = sb.t("oh", [EXP, 512], F32)
        sc8 = sb.t("sc8", [EXP, 512], F32)
        incl = sb.t("incl", [EXP, 512], F32)
        zero8 = sb.t("zero8", [EXP, 512], F32)
        arow = sb.t("arow", [1, 512], F32)
        prow_r = sb.t("prow_r", [1, 512], F32)
        lrank = sb.t("lrank", [1, 512], F32)
        grow = sb.t("grow", [1, 512], F32)
        lloc = sb.t("lloc", [1, 512], F32)
        erow = sb.t("erow", [1, 512], F32)
        dstrow = sb.t("dstrow", [1, 512], F32)
        vrow = sb.t("vrow", [1, 512], F32)
        tokrow = sb.t("tokrow", [1, 512], F32)
        outrow = sb.t("outrow", [1, 512], F32)
        offsT = sb.t("offsT", [EXP, EXP], F32)
        loffsT = sb.t("loffsT", [EXP, EXP], F32)
        csb = sb.t("csb", [64, EXP], F32)
        selg_sb = sb.t("selg_sb", [64, EXP], F32)
        sell_sb = sb.t("sell_sb", [64, EXP], F32)
        onesr8 = sb.t("onesr8", [1, EXP], F32)
        eic = sb.t("eic", [EXP, 1], F32)
        one8c = sb.t("one8c", [EXP, 1], F32)
        thalf = sb.t("thalf", [1, 512], F32)
        jio = sb.t("jio", [1, 512], F32)
        identb = sb.t("identb", [128, 128], BF16)
        identf = sb.t("identf", [128, 128], F32)
        paycol = sb.t("paycol", [128, 4], F32)
        dsti = sb.t("dsti", [128, 1], I32)
        gcol = sb.t("gcol", [128, 4], F32)
        gidx = sb.t("gidx", [128, 1], I32)
        pidx = sb.t("pidx", [128, 1], I32)
        ps_mm = sb.p("ps_mm", [128, TB], F32)
        ps_tp = sb.p("ps_tp", [128, 128], BF16)
        ps_tp32 = sb.p("ps_tp32", [128, 128], F32)
        ps_row = sb.p("ps_row", [1, 512], F32)
        ps_off = sb.p("ps_off", [EXP, EXP], F32)


        p.dma("sync", lambda e: e.dma_start(out=csb[:], in_=counts_all), wait=w0)
        p.dma("sync", lambda e: e.dma_start(out=selg_sb[:], in_=sel_g), wait=w0)
        p.dma("sync", lambda e: e.dma_start(out=sell_sb[:], in_=sel_l), wait=w0)
        p.dma("sync", lambda e: e.dma_start(out=onesr8[:], in_=ones_row8), wait=w0)
        p.dma("sync", lambda e: e.dma_start(out=eic[:], in_=eiota), wait=w0)
        p.dma("sync", lambda e: e.dma_start(out=thalf[:], in_=tokhalf), wait=w0)
        p.dma("sync", lambda e: e.dma_start(out=jio[:], in_=jiota), wait=w0)
        p.dma("sync", lambda e: e.dma_start(out=identb[:], in_=ident_bf), wait=w0)
        p.dma("sync", lambda e: e.dma_start(out=identf[:], in_=ident_f32), wait=w0)
        p.op("dve", lambda e: e.memset(zero8[:], 0.0), wait=w0)
        p.op("dve", lambda e: e.memset(zero_t[:], 0.0), wait=w0)
        p.op("dve", lambda e: e.memset(one8c[:], 1.0), wait=w0)
        p.dma("sync", lambda e: e.dma_start(out=ln2pad[0:TOK, :], in_=ln2tok), wait=w0)
        zt = p.mark()
        p.dma("sync", lambda e: e.dma_start(
            out=ln2pad[TOK:TOK + 1, :], in_=zero_t[0:1, :].bitcast(BF16)[:, 0:D]),
            wait=zt)
        for r in range(32):
            p.dma("sync", lambda e, r=r: e.dma_start(
                out=parts[r * 128:(r + 1) * 128, :], in_=zero_t[:]), wait=zt)
        p.dma("sync", lambda e: e.dma_start(
            out=parts[2 * TOK:2 * TOK + 1, :], in_=zero_t[0:1, :]), wait=zt)
        p.op("dve", lambda e: e.memset(paycol[:], 0.0), wait=zt)
        p.op("dve", lambda e: e.memset(paycol[:, 0:1], float(TOK)))
        p.op("dve", lambda e: e.memset(paycol[:, 2:3], float(TRASH_PARTS)))
        ii = p.mark()
        for r in range(8 * LCAP // 128 + 1):
            n = 128 if r < 8 * LCAP // 128 else 1
            p.dma("sync", lambda e, r=r, n=n: e.dma_start(
                out=idxbuf[r * 128:r * 128 + n, :], in_=paycol[0:n, :]), wait=ii)

        p.op("pe", lambda e: e.matmul(ps_off[:, :], csb[:], selg_sb[:],
                                      start=True, stop=True))
        p.op("act", lambda e: e.activation(offsT[:], ps_off[:, :], AF.Copy))
        p.op("pe", lambda e: e.matmul(ps_off[:, :], csb[:], sell_sb[:],
                                      start=True, stop=True))
        p.op("act", lambda e: e.activation(loffsT[:], ps_off[:, :], AF.Copy))

        tflat = topi.rearrange("a b -> (a b)").unsqueeze(0)
        pflat = topp.rearrange("a b -> (a b)").unsqueeze(0)
        ps8 = ps_mm[0:EXP, 0:512]
        for s in range(8):
            p.dma("sync", lambda e, s=s: e.dma_start(
                out=arow[:], in_=tflat[:, s * 512:(s + 1) * 512]))
            p.dma("sync", lambda e, s=s: e.dma_start(
                out=prow_r[:], in_=pflat[:, s * 512:(s + 1) * 512]))
            p.op("pe", lambda e: e.matmul(ps8, onesr8[:], arow[:],
                                          start=True, stop=True))
            p.op("dve", lambda e: e.tensor_scalar(
                oh[:], ps8, eic[:], None, ALU.is_equal))
            p.op("dve", lambda e: e.tensor_scalar(
                sc8[:], zero8[:], 0.0, 1.0, ALU.mult, ALU.add))
            p.op("dve", lambda e: e.tensor_tensor_scan(
                incl[:], sc8[:], oh[:], 0.0, ALU.mult, ALU.add))
            p.op("dve", lambda e: e.tensor_sub(incl[:], incl[:], oh[:]))
            p.op("dve", lambda e: e.tensor_mul(sc8[:], oh[:], incl[:]))
            p.op("pe", lambda e: e.matmul(ps_row[:, :], one8c[:], sc8[:],
                                          start=True, stop=True))
            p.op("act", lambda e: e.activation(lrank[:], ps_row[:, :], AF.Copy))
            for colv, dest in ((offsT, grow), (loffsT, lloc)):
                p.op("dve", lambda e, colv=colv, s=s: e.tensor_scalar_mul(
                    sc8[:], oh[:], colv[:, s:s + 1]))
                p.op("pe", lambda e: e.matmul(ps_row[:, :], one8c[:], sc8[:],
                                              start=True, stop=True))
                p.op("act", lambda e, dest=dest: e.activation(
                    dest[:], ps_row[:, :], AF.Copy))
            p.op("dve", lambda e: e.tensor_scalar_mul(sc8[:], oh[:], eic[:]))
            p.op("pe", lambda e: e.matmul(ps_row[:, :], one8c[:], sc8[:],
                                          start=True, stop=True))
            p.op("act", lambda e: e.activation(erow[:], ps_row[:, :], AF.Copy))
            p.op("dve", lambda e: e.tensor_add(grow[:], grow[:], lrank[:]))
            p.op("dve", lambda e: e.tensor_scalar(
                vrow[:], grow[:], float(CAP), None, ALU.is_lt))
            p.op("dve", lambda e: e.tensor_add(lloc[:], lloc[:], lrank[:]))
            p.op("dve", lambda e: e.tensor_scalar(
                sc8[0:1, :], lloc[:], float(LCAP), None, ALU.is_lt))
            p.op("dve", lambda e: e.tensor_mul(vrow[:], vrow[:], sc8[0:1, :]))
            p.op("dve", lambda e: e.tensor_scalar_mul(dstrow[:], erow[:], float(LCAP)))
            p.op("dve", lambda e: e.tensor_add(dstrow[:], dstrow[:], lloc[:]))
            p.op("dve", lambda e: e.tensor_scalar_add(
                dstrow[:], dstrow[:], -float(TRASH_IDX)))
            p.op("dve", lambda e: e.tensor_mul(dstrow[:], dstrow[:], vrow[:]))
            p.op("dve", lambda e: e.tensor_scalar_add(
                dstrow[:], dstrow[:], float(TRASH_IDX)))
            p.op("dve", lambda e, s=s: e.tensor_scalar_add(
                tokrow[:], thalf[:], float(s * 256)))
            p.op("dve", lambda e, s=s: e.tensor_scalar_add(
                outrow[:], jio[:], float(s * 512)))
            for ch in range(4):
                sl = slice(ch * 128, (ch + 1) * 128)
                for row, col in ((tokrow, 0), (prow_r, 1), (outrow, 2),
                                 (dstrow, 3)):
                    p.op("pe", lambda e, row=row, sl=sl: e.transpose(
                        ps_tp32[:, 0:1], row[:, sl], identf[0:1, 0:1]))
                    p.op("dve", lambda e, col=col: e.tensor_copy(
                        paycol[:, col:col + 1], ps_tp32[:, 0:1]))
                p.op("dve", lambda e: e.tensor_copy(dsti[:], paycol[:, 3:4]))
                p.dma("pool", lambda e: e.indirect_dma_start(
                    idxbuf, bass.IndirectOffsetOnAxis(dsti[:], 0),
                    paycol[:, 0:4], None))

        for ex in range(EXP):
            for k in range(DCH):
                p.dma("pool", lambda e, ex=ex, k=k: e.dma_start(
                    out=w1e[:, k * 2048:(k + 1) * 2048],
                    in_=moe_w1[ex, k * 128:(k + 1) * 128, :]))
            for k in range(16):
                p.dma("pool", lambda e, ex=ex, k=k: e.dma_start(
                    out=w2e[:, k * D:(k + 1) * D],
                    in_=moe_w2[ex, k * 128:(k + 1) * 128, :]))
            for g in range(NGRP):
                base = ex * LCAP + g * 512
                for sub in range(4):
                    p.dma("sync", lambda e, base=base, sub=sub: e.dma_start(
                        out=gcol[:, :], in_=idxbuf[base + sub * 128:
                                                   base + (sub + 1) * 128, :]))
                    p.op("dve", lambda e: e.tensor_copy(gidx[:], gcol[:, 0:1]))
                    p.dma("pool", lambda e: e.indirect_dma_start(
                        xg[:], None, ln2pad, bass.IndirectOffsetOnAxis(gidx[:], 0)))
                    for k in range(DCH):
                        p.op("pe", lambda e, k=k: e.transpose(
                            ps_tp[:, :], xg[:, k * 128:(k + 1) * 128], identb[:]))
                        p.op("dve", lambda e, k=k, sub=sub: e.tensor_copy(
                            xgT[:, k * 512 + sub * 128:k * 512 + sub * 128 + 128],
                            ps_tp[:, :]))
                for fc in range(16):
                    for k in range(DCH):
                        p.op("pe", lambda e, fc=fc, k=k: e.matmul(
                            ps_mm[:, :],
                            w1e[:, k * 2048 + fc * 128:k * 2048 + (fc + 1) * 128],
                            xgT[:, k * 512:(k + 1) * 512],
                            start=(k == 0), stop=(k == DCH - 1)))
                    p.op("act", lambda e, fc=fc: e.activation(
                        h1e[:, fc * 512:(fc + 1) * 512], ps_mm[:, :],
                        AF.Gelu_apprx_tanh))
                for hc in range(DCH):
                    for fc in range(16):
                        p.op("pe", lambda e, hc=hc, fc=fc: e.matmul(
                            ps_mm[:, :],
                            w2e[:, fc * D + hc * 128:fc * D + (hc + 1) * 128],
                            h1e[:, fc * 512:(fc + 1) * 512],
                            start=(fc == 0), stop=(fc == 15)))
                    p.op("act", lambda e: e.activation(y2c[:], ps_mm[:, :], AF.Copy))
                    for sub in range(4):
                        p.op("pe", lambda e, sub=sub: e.transpose(
                            ps_tp32[:, :], y2c[:, sub * 128:(sub + 1) * 128],
                            identf[:]))
                        p.op("dve", lambda e, sub=sub, hc=hc: e.tensor_copy(
                            ytok[:, sub * D + hc * 128:sub * D + (hc + 1) * 128],
                            ps_tp32[:, :]))
                for sub in range(4):
                    p.dma("sync", lambda e, base=base, sub=sub: e.dma_start(
                        out=gcol[:, :], in_=idxbuf[base + sub * 128:
                                                   base + (sub + 1) * 128, :]))
                    p.op("dve", lambda e, sub=sub: e.tensor_scalar_mul(
                        ytok[:, sub * D:(sub + 1) * D],
                        ytok[:, sub * D:(sub + 1) * D], gcol[:, 1:2]))
                    p.op("dve", lambda e: e.tensor_copy(pidx[:], gcol[:, 2:3]))
                    p.dma("pool", lambda e, sub=sub: e.indirect_dma_start(
                        parts, bass.IndirectOffsetOnAxis(pidx[:], 0),
                        ytok[:, sub * D:(sub + 1) * D], None))

        for ti in range(16):
            pview = parts[0:2 * TOK, :].rearrange("(t two) d -> t two d", two=2)
            p.dma("sync", lambda e, ti=ti, pview=pview: e.dma_start(
                out=ytok[:, 0:D], in_=pview[ti * 128:(ti + 1) * 128, 0, :]))
            p.dma("sync", lambda e, ti=ti, pview=pview: e.dma_start(
                out=ytok[:, D:2 * D], in_=pview[ti * 128:(ti + 1) * 128, 1, :]))
            p.op("dve", lambda e: e.tensor_add(
                ytok[:, 0:D], ytok[:, 0:D], ytok[:, D:2 * D]))
            for k in range(DCH):
                p.op("pe", lambda e, k=k: e.transpose(
                    ps_tp32[:, :], ytok[:, k * 128:(k + 1) * 128], identf[:]))
                p.op("dve", lambda e, k=k, ti=ti: e.tensor_copy(
                    moT[:, k * TOK + ti * 128:k * TOK + ti * 128 + 128],
                    ps_tp32[:, :]))
        for k in range(DCH):
            p.dma("sync", lambda e, k=k: e.dma_start(
                out=h1st[:], in_=h1T[k * 128:(k + 1) * 128, :]))
            p.op("dve", lambda e, k=k: e.tensor_add(
                h1st[:], h1st[:], moT[:, k * TOK:(k + 1) * TOK]))
            p.op("act", lambda e: e.activation(h1st[:], h1st[:], AF.Silu))
            p.dma("sync", lambda e, k=k: e.dma_start(
                out=o_g[k * 128:(k + 1) * 128, :], in_=h1st[:]))



def build_l0b2():
    nc = bass.Bass("TRN2", target_bir_lowering=False)
    io = {}
    io["ln2tok"] = nc.dram_tensor("ln2tok", [TOK, D], BF16, kind="ExternalInput").ap()
    io["h1T"] = nc.dram_tensor("h1T", [D, TOK], F32, kind="ExternalInput").ap()
    io["topi"] = nc.dram_tensor("topi", [TOK, 2], F32, kind="ExternalInput").ap()
    io["topp"] = nc.dram_tensor("topp", [TOK, 2], F32, kind="ExternalInput").ap()
    io["counts_all"] = nc.dram_tensor("counts_all", [64, EXP], F32, kind="ExternalInput").ap()
    io["sel_g"] = nc.dram_tensor("sel_g", [64, EXP], F32, kind="ExternalInput").ap()
    io["sel_l"] = nc.dram_tensor("sel_l", [64, EXP], F32, kind="ExternalInput").ap()
    io["moe_w1"] = nc.dram_tensor("moe_w1", [EXP, D, 2048], F32, kind="ExternalInput").ap()
    io["moe_w2"] = nc.dram_tensor("moe_w2", [EXP, 2048, D], F32, kind="ExternalInput").ap()
    io["ones_row8"] = nc.dram_tensor("ones_row8", [1, EXP], F32, kind="ExternalInput").ap()
    io["eiota"] = nc.dram_tensor("eiota", [EXP, 1], F32, kind="ExternalInput").ap()
    io["tokhalf"] = nc.dram_tensor("tokhalf", [1, 512], F32, kind="ExternalInput").ap()
    io["jiota"] = nc.dram_tensor("jiota", [1, 512], F32, kind="ExternalInput").ap()
    io["ident_bf"] = nc.dram_tensor("ident_bf", [128, 128], BF16, kind="ExternalInput").ap()
    io["ident_f32"] = nc.dram_tensor("ident_f32", [128, 128], F32, kind="ExternalInput").ap()
    io["o_g"] = nc.dram_tensor("o_g", [D, TOK], F32, kind="ExternalOutput").ap()
    io["ln2pad"] = nc.dram_tensor("ln2pad", [TOK + 1, D], BF16).ap()
    io["idxbuf"] = nc.dram_tensor("idxbuf", [8 * LCAP + 1, 4], F32).ap()
    io["parts"] = nc.dram_tensor("parts", [2 * TOK + 1, D], F32).ap()
    import contextlib as _cl
    with _cl.ExitStack() as _c:
        block = _c.enter_context(nc.Block())
        sem = [_c.enter_context(nc.semaphore(f"chain{i}")) for i in range(4)]
        p = Prog(nc)
        emit_l0b2(nc, p, io)
        p.run(block, sem)
    return nc


def _prep_l0b1_inputs(inputs):
    c = np.ascontiguousarray
    f = np.float32

    def cols(v):
        return c(np.asarray(v, f).reshape(DCH, 128).T)

    w = {
        "qkv_wT": c(inputs["L0_ffn_qkv_weight"].T),
        "proj_wT": c(inputs["L0_ffn_proj_weight"].T),
        "router_w": c(np.asarray(inputs["L0_ffn_router_weight"], f)),
        "ln1w": cols(inputs["L0_ffn_ln1_weight"]),
        "ln1b": cols(inputs["L0_ffn_ln1_bias"]),
        "ln2w": cols(inputs["L0_ffn_ln2_weight"]),
        "ln2b": cols(inputs["L0_ffn_ln2_bias"]),
        "ones_col": np.ones((128, 1), f),
        "ones_row": np.ones((1, 128), f),
        "ident_f32": np.eye(128, dtype=f),
        "iota8": np.tile(np.arange(EXP, dtype=f), (128, 1)),
    }
    import ml_dtypes
    w["ident_bf"] = np.eye(128).astype(ml_dtypes.bfloat16)
    return {k: (np.asarray(v, f) if k != "ident_bf" else v) for k, v in w.items()}


def _reshard_to_axis1(xT_stack):
    out = []
    for cc in range(NCORES):
        cols = [xT_stack[s][:, cc * 256:(cc + 1) * 256] for s in range(8)]
        out.append(np.ascontiguousarray(np.concatenate(cols, axis=1)))
    return out


def run_l0b1(inputs, xnT_stack):
    w = _prep_l0b1_inputs(inputs)
    shards = _reshard_to_axis1(xnT_stack)
    nc = build_l0b1()
    in_maps = []
    for cc in range(NCORES):
        m = dict(w)
        m["xnT"] = shards[cc]
        in_maps.append(m)
    res = run_bass_kernel_spmd(nc, in_maps, core_ids=list(range(NCORES)))
    return res.results



def _prep_l0b2_inputs(inputs, core_id):
    f = np.float32
    sel_g = np.zeros((64, EXP), f)
    sel_l = np.zeros((64, EXP), f)
    for c in range(8):
        for sp in range(8):
            r = c * 8 + sp
            for s in range(8):
                if sp < s or (sp == s and c < core_id):
                    sel_g[r, s] = 1.0
                if c == core_id and sp < s:
                    sel_l[r, s] = 1.0
    import ml_dtypes
    return {
        "sel_g": sel_g, "sel_l": sel_l,
        "moe_w1": np.asarray(inputs["L0_ffn_moe_w1"], f),
        "moe_w2": np.asarray(inputs["L0_ffn_moe_w2"], f),
        "ones_row8": np.ones((1, EXP), f),
        "eiota": np.arange(EXP, dtype=f).reshape(EXP, 1),
        "tokhalf": (np.arange(512) // 2).astype(f).reshape(1, 512),
        "jiota": np.arange(512, dtype=f).reshape(1, 512),
        "ident_bf": np.eye(128).astype(ml_dtypes.bfloat16),
        "ident_f32": np.eye(128, dtype=f),
    }


def run_l0b2(inputs, b1_results):
    counts_all = np.stack([np.asarray(r["o_counts"], np.float32)
                           for r in b1_results]).reshape(64, EXP)
    nc = build_l0b2()
    in_maps = []
    for cc in range(NCORES):
        m = _prep_l0b2_inputs(inputs, cc)
        m["counts_all"] = counts_all
        m["ln2tok"] = np.asarray(b1_results[cc]["o_ln2tok"])
        m["h1T"] = np.asarray(b1_results[cc]["o_h1"], np.float32)
        m["topi"] = np.asarray(b1_results[cc]["o_topi"], np.float32)
        m["topp"] = np.asarray(b1_results[cc]["o_topp"], np.float32)
        in_maps.append(m)
    res = run_bass_kernel_spmd(nc, in_maps, core_ids=list(range(NCORES)))
    return res.results



NHEADS = 16
C1 = math.sqrt(64.0 * 3.0)
BSC = math.sqrt(64.0 * 2.0)
WJ = 2048
SLABR = 2048 * WJ


def emit_l1a(nc, p, io):

    xT = io["xT"]
    s0T = io["s0T"]
    gT = io.get("gT")
    wq = io["wq"]
    wk = io["wk"]
    wv = io["wv"]
    wo = io["wo"]
    wpk = io["wpk"]
    wpq = io["wpq"]
    bq_c = io["bq_c"]
    bk_c = io["bk_c"]
    bo_c = io["bo_c"]
    bv_r = io["bv_r"]
    bpk_r = io["bpk_r"]
    bpq_r = io["bpq_r"]
    rel_emb = io["rel_emb"]
    lng_r = io["lng_r"]
    lnb_r = io["lnb_r"]
    expand1 = io["expand1"]
    expand2 = io["expand2"]
    ones_col = io["ones_col"]
    ones_row = io["ones_row"]
    ones_rowb = io["ones_rowb"]
    ident_bf = io["ident_bf"]

    o_x1 = io["o_x1"]
    o_xn1 = io["o_xn1"]
    o_s1 = io["o_s1"]

    slab1 = nc.dram_tensor("slab1", [NHEADS, 2048, WJ], BF16)
    slab2 = nc.dram_tensor("slab2", [NHEADS, 2048, WJ], BF16)

    w0 = p.mark()
    with contextlib.ExitStack() as ctx:
        sb = SB(nc, ctx)
        regA_h = ctx.enter_context(
            nc.sbuf_tensor("regA_f", [128, 2 * DCH * TOK], BF16))
        x1f = regA_h.bitcast(F32).ap()
        qT = regA_h.ap()[:, 0:DCH * TOK]
        kT = regA_h.ap()[:, DCH * TOK:2 * DCH * TOK]
        xnbf_h = ctx.enter_context(
            nc.sbuf_tensor("xnbf_f", [128, DCH * TOK], BF16))
        xnbf = xnbf_h.ap()
        relst = xnbf_h.bitcast(F32).ap()[:, 0:1024]
        rebf = xnbf_h.ap()[:, 2048:2048 + 4 * 1024]
        reT = xnbf_h.ap()[:, 6144:6144 + 8 * 512]
        gbc = xnbf_h.bitcast(F32).ap()[:, 5120:5120 + 1024]
        bbc = xnbf_h.bitcast(F32).ap()[:, 6144:6144 + 1024]
        vtok_h = ctx.enter_context(
            nc.sbuf_tensor("vtok_f", [128, 16 * (NHEADS * 65)], BF16))
        vtok = vtok_h.ap()
        mrow = vtok_h.bitcast(F32).ap()[:, 0:TOK]
        rrow = vtok_h.bitcast(F32).ap()[:, TOK:2 * TOK]
        scrow = vtok_h.bitcast(F32).ap()[:, 2 * TOK:3 * TOK]
        regB = sb.t("regB", [128, DCH * TOK], BF16)
        ph23_h = ctx.enter_context(
            nc.sbuf_tensor("ph23_f", [128, 4 * WJ], BF16))
        exp_sb = ph23_h.ap()
        wstage = sb.t("wstage", [128, 2 * 1024], BF16)
        pkw = sb.t("pkw", [128, 4 * 1024], BF16)
        stage = sb.t("stage", [128, WJ], BF16)
        m_bc = sb.t("m_bc", [128, TB], F32)
        r_bc = sb.t("r_bc", [128, TB], F32)
        scr1 = sb.t("scr1", [128, TB], F32)
        pkedgec = sb.t("pkedgec", [128, 2 * NHEADS], BF16)
        pqedgec = sb.t("pqedgec", [128, 2 * NHEADS], BF16)
        kpec = sb.t("kpec", [128, 32], F32)
        qpe0 = ph23_h.ap()[0:1, 0:TOK]
        qpe1 = ph23_h.ap()[0:1, TOK:2 * TOK]
        qpbc = ph23_h.ap()[:, 2 * TOK:2 * TOK + 1024]
        etile = ph23_h.ap()[:, 2 * TOK + 1024:2 * TOK + 2048]
        ptile = ph23_h.ap()[:, 2 * TOK + 2048:2 * TOK + 2560]
        recv = ph23_h.bitcast(F32).ap()[:, TOK + 1280:TOK + 1792]
        onec = sb.t("onec", [128, 1], F32)
        oner = sb.t("oner", [1, 128], F32)
        onerb = sb.t("onerb", [1, 128], BF16)
        identb = sb.t("identb", [128, 128], BF16)
        bcols = sb.t("bcols", [128, 3 * DCH], F32)
        brows = sb.t("brows", [1, 3 * D], BF16)
        ps_a = sb.p("ps_a", [128, TB], F32)
        ps_b = sb.p("ps_b", [128, TB], F32)
        ps_pv = sb.p("ps_pv", [128, TB], F32)
        ps_tp = sb.p("ps_tp", [128, 128], BF16)
        ps_r1 = sb.p("ps_r1", [1, TB], F32)
        ps_r2 = sb.p("ps_r2", [1, TB], F32)


        x1ch = [x1f[:, k * TOK:(k + 1) * TOK] for k in range(DCH)]
        xnch = [xnbf[:, k * TOK:(k + 1) * TOK] for k in range(DCH)]

        p.dma("sync", lambda e: e.dma_start(out=onec[:], in_=ones_col), wait=w0)
        p.dma("sync", lambda e: e.dma_start(out=oner[:], in_=ones_row), wait=w0)
        p.dma("sync", lambda e: e.dma_start(out=onerb[:], in_=ones_rowb), wait=w0)
        p.dma("sync", lambda e: e.dma_start(out=identb[:], in_=ident_bf), wait=w0)
        p.dma("sync", lambda e: e.dma_start(out=bcols[:, 0:DCH], in_=bq_c), wait=w0)
        p.dma("sync", lambda e: e.dma_start(out=bcols[:, DCH:2 * DCH], in_=bk_c), wait=w0)
        p.dma("sync", lambda e: e.dma_start(out=bcols[:, 2 * DCH:3 * DCH], in_=bo_c), wait=w0)
        p.dma("sync", lambda e: e.dma_start(out=brows[:, 0:D], in_=bv_r), wait=w0)
        p.dma("sync", lambda e: e.dma_start(out=brows[:, D:2 * D], in_=bpk_r), wait=w0)
        p.dma("sync", lambda e: e.dma_start(out=brows[:, 2 * D:3 * D], in_=bpq_r), wait=w0)

        for k in range(DCH):
            p.dma("sync", lambda e, k=k: e.dma_start(
                out=x1ch[k], in_=s0T[k * 128:(k + 1) * 128, :]), wait=w0)
        if "ag_g" in io:
            ridx_sb = sb.t("ridx_sb", [128, 1], I32)
            p.dma("sync", lambda e: e.dma_start(out=ridx_sb[:], in_=io["ridx"]))
            agf = io["ag_g"]
            for k in range(DCH):
                for cc in range(8):
                    p.dma("pool", lambda e, k=k, cc=cc: e.indirect_dma_start(
                        scr1[:, 0:256], None, agf,
                        bass.IndirectOffsetOnAxis(ridx_sb[:], 0),
                        element_offset=(cc * 1024 + k * 128) * 2048))
                    p.op("dve", lambda e, k=k, cc=cc: e.tensor_mul(
                        x1ch[k][:, cc * 256:(cc + 1) * 256],
                        x1ch[k][:, cc * 256:(cc + 1) * 256], scr1[:, 0:256]))
        else:
            for k in range(DCH):
                for tb in range(NTB):
                    p.dma("sync", lambda e, k=k, tb=tb: e.dma_start(
                        out=relst[:, 0:TB],
                        in_=gT[k * 128:(k + 1) * 128, tb * TB:(tb + 1) * TB]))
                    p.op("dve", lambda e, k=k, tb=tb: e.tensor_mul(
                        x1ch[k][:, tb * TB:(tb + 1) * TB],
                        x1ch[k][:, tb * TB:(tb + 1) * TB], relst[:, 0:TB]))
        for k in range(DCH):
            for tb in range(NTB):
                p.dma("sync", lambda e, k=k, tb=tb: e.dma_start(
                    out=relst[:, 0:TB],
                    in_=xT[k * 128:(k + 1) * 128, tb * TB:(tb + 1) * TB]))
                p.op("dve", lambda e, k=k, tb=tb: e.tensor_add(
                    x1ch[k][:, tb * TB:(tb + 1) * TB],
                    x1ch[k][:, tb * TB:(tb + 1) * TB], relst[:, 0:TB]))
            p.dma("sync", lambda e, k=k: e.dma_start(
                out=o_x1[k * 128:(k + 1) * 128, :], in_=x1ch[k]))
        emit_ln_rows(p, x1ch, scr1, ps_r1, ps_r2, onec[:], mrow[0:1, 0:TOK],
                     rrow[0:1, 0:TOK], scrow[0:1, 0:TOK])
        for tb in range(NTB):
            sl = slice(tb * TB, (tb + 1) * TB)
            p.op("pe", lambda e, sl=sl: e.matmul(
                ps_a[:, :], oner[:], mrow[0:1, sl], start=True, stop=True))
            p.op("dve", lambda e: e.tensor_copy(m_bc[:], ps_a[:, :]))
            p.op("pe", lambda e, sl=sl: e.matmul(
                ps_a[:, :], oner[:], rrow[0:1, sl], start=True, stop=True))
            p.op("dve", lambda e: e.tensor_copy(r_bc[:], ps_a[:, :]))
            for k in range(DCH):
                p.op("dve", lambda e, k=k, sl=sl: e.tensor_sub(
                    scr1[:], x1ch[k][:, sl], m_bc[:]))
                p.op("dve", lambda e, k=k: e.tensor_mul(scr1[:], scr1[:], r_bc[:]))
                p.dma("sync", lambda e, k=k, sl=sl: e.dma_start(
                    out=o_xn1[k * 128:(k + 1) * 128, sl], in_=scr1[:]))
                p.op("dve", lambda e, k=k, sl=sl: e.tensor_copy(
                    xnch[k][:, sl], scr1[:]))

        for name_w, dst, bco in ((wq, qT, 0), (wk, kT, DCH)):
            for ec in range(DCH):
                for kk in range(DCH):
                    p.dma("pool", lambda e, name_w=name_w, ec=ec, kk=kk: e.dma_start(
                        out=wstage[:, kk * 128:(kk + 1) * 128],
                        in_=name_w[kk * 128:(kk + 1) * 128, ec * 128:(ec + 1) * 128]))
                for tb in range(NTB):
                    for kk in range(DCH):
                        p.op("pe", lambda e, ec=ec, tb=tb, kk=kk: e.matmul(
                            ps_a[:, :], wstage[:, kk * 128:(kk + 1) * 128],
                            xnch[kk][:, tb * TB:(tb + 1) * TB],
                            start=(kk == 0), stop=(kk == DCH - 1)))
                    p.op("act", lambda e, dst=dst, ec=ec, tb=tb, bco=bco: e.activation(
                        dst[:, ec * TOK + tb * TB:ec * TOK + (tb + 1) * TB],
                        ps_a[:, :], AF.Identity,
                        bias=bcols[:, bco + ec:bco + ec + 1]))
        for kk in range(DCH):
            p.dma("pool", lambda e, kk=kk: e.dma_start(
                out=regB[:, kk * 1024:(kk + 1) * 1024],
                in_=wv[kk * 128:(kk + 1) * 128, :]), wait=None)
        for ti in range(16):
            p.op("dve", lambda e, ti=ti: e.memset(
                vtok[:, ti * 1040:(ti + 1) * 1040].rearrange(
                    "p (h c) -> p h c", c=65)[:, :, 64:65], 1.0))
        for ti in range(16):
            for blk2 in range(2):
                for kk in range(DCH):
                    p.op("pe", lambda e, kk=kk, ti=ti, blk2=blk2: e.matmul(
                        ps_b[:, :], xnch[kk][:, ti * 128:(ti + 1) * 128],
                        regB[:, kk * 1024 + blk2 * 512:kk * 1024 + (blk2 + 1) * 512],
                        start=(kk == 0), stop=False, skip_group_check=True))
                p.op("pe", lambda e, blk2=blk2: e.matmul(
                    ps_b[:, :], onerb[:],
                    brows[0:1, blk2 * 512:(blk2 + 1) * 512],
                    start=False, stop=True, skip_group_check=True))
                p.op("act", lambda e, ti=ti, blk2=blk2: e.activation(
                    vtok[:, ti * 1040:(ti + 1) * 1040].rearrange(
                        "p (h c) -> p h c", c=65)[:, blk2 * 8:(blk2 + 1) * 8, 0:64],
                    ps_b[:, :].rearrange("p (h c) -> p h c", c=64), AF.Copy))

        for srcdram, dst in ((lng_r, gbc), (lnb_r, bbc)):
            p.dma("sync", lambda e, srcdram=srcdram: e.dma_start(
                out=relst[0:1, 0:D], in_=srcdram))
            for bl in range(2):
                p.op("pe", lambda e, bl=bl: e.matmul(
                    ps_a[:, :], oner[:],
                    relst[0:1, bl * 512:(bl + 1) * 512],
                    start=True, stop=True))
                p.op("dve", lambda e, dst=dst, bl=bl: e.tensor_copy(
                    dst[:, bl * 512:(bl + 1) * 512], ps_a[:, :]))
        for rt in range(4):
            p.dma("sync", lambda e, rt=rt: e.dma_start(
                out=relst[:, :], in_=rel_emb[rt * 128:(rt + 1) * 128, :]))
            for hf in range(2):
                p.op("dve", lambda e, hf=hf: e.bn_stats(
                    kpec[:, hf * 6:(hf + 1) * 6], relst[:, hf * 512:(hf + 1) * 512]))
            p.op("dve", lambda e: e.bn_aggr(kpec[:, 12:14], kpec[:, 0:12]))
            p.op("dve", lambda e: e.tensor_scalar_add(
                kpec[:, 13:14], kpec[:, 13:14], LN_EPS))
            p.op("act", lambda e: e.activation(kpec[:, 13:14], kpec[:, 13:14], AF.Sqrt))
            p.op("dve", lambda e: e.reciprocal(kpec[:, 13:14], kpec[:, 13:14]))
            p.op("dve", lambda e: e.tensor_scalar(
                relst[:, :], relst[:, :], kpec[:, 12:13], kpec[:, 13:14],
                ALU.subtract, ALU.mult))
            p.op("dve", lambda e: e.tensor_mul(relst[:, :], relst[:, :], gbc[:, :]))
            p.op("dve", lambda e, rt=rt: e.tensor_tensor(
                rebf[:, rt * 1024:(rt + 1) * 1024], relst[:, :], bbc[:, :], ALU.add))
            for kk in range(DCH):
                p.op("pe", lambda e, rt=rt, kk=kk: e.transpose(
                    ps_tp[:, :], rebf[:, rt * 1024 + kk * 128:rt * 1024 + (kk + 1) * 128],
                    identb[:]))
                p.op("dve", lambda e, rt=rt, kk=kk: e.tensor_copy(
                    reT[:, kk * 512 + rt * 128:kk * 512 + (rt + 1) * 128], ps_tp[:, :]))

        for side in range(2):
            w_side = wpk if side == 0 else wpq
            brow_off = D if side == 0 else 2 * D
            slab = slab1 if side == 0 else slab2
            expnd = expand1 if side == 0 else expand2
            actT = qT if side == 0 else kT
            edgec = pkedgec if side == 0 else pqedgec
            for wc in range(4):
                p.dma("pool", lambda e, expnd=expnd, wc=wc: e.dma_start(
                    out=exp_sb[:, wc * WJ:(wc + 1) * WJ],
                    in_=expnd[wc * 128:(wc + 1) * 128, :]))
            for kk in range(DCH):
                p.dma("pool", lambda e, w_side=w_side, kk=kk: e.dma_start(
                    out=regB[:, kk * 1024:(kk + 1) * 1024],
                    in_=w_side[kk * 128:(kk + 1) * 128, :]))
            for wt in range(4):
                for bl in range(2):
                    for kk in range(DCH):
                        p.op("pe", lambda e, wt=wt, bl=bl, kk=kk: e.matmul(
                            ps_a[:, :],
                            reT[:, kk * 512 + wt * 128:kk * 512 + (wt + 1) * 128],
                            regB[:, kk * 1024 + bl * 512:kk * 1024 + (bl + 1) * 512],
                            start=(kk == 0), stop=False, skip_group_check=True))
                    p.op("pe", lambda e, bl=bl, brow_off=brow_off: e.matmul(
                        ps_a[:, :], onerb[:],
                        brows[0:1, brow_off + bl * 512:brow_off + (bl + 1) * 512],
                        start=False, stop=True, skip_group_check=True))
                    p.op("act", lambda e, wt=wt, bl=bl: e.activation(
                        pkw[:, wt * 1024 + bl * 512:wt * 1024 + (bl + 1) * 512],
                        ps_a[:, :], AF.Copy))
            for jt in range(16):
                for dbl in range(2):
                    for wc in range(4):
                        p.op("pe", lambda e, jt=jt, dbl=dbl, wc=wc: e.matmul(
                            ps_a[:, :],
                            exp_sb[:, wc * WJ + jt * 128:wc * WJ + (jt + 1) * 128],
                            pkw[:, wc * 1024 + dbl * 512:wc * 1024 + (dbl + 1) * 512],
                            start=(wc == 0), stop=(wc == 3)))
                    p.op("act", lambda e: e.activation(
                        stage[:, 0:512], ps_a[:, :], AF.Copy))
                    for sub in range(4):
                        p.op("pe", lambda e, sub=sub: e.transpose(
                            ps_tp[:, :], stage[:, sub * 128:(sub + 1) * 128],
                            identb[:]))
                        dchunk = dbl * 4 + sub
                        p.op("dve", lambda e, dchunk=dchunk, jt=jt: e.tensor_copy(
                            regB[:, dchunk * WJ + jt * 128:dchunk * WJ + (jt + 1) * 128]
                            if False else
                            regB[:, dchunk * 2048 + jt * 128:dchunk * 2048 + (jt + 1) * 128],
                            ps_tp[:, :]))
            for dchunk in range(DCH):
                for eix, jcol in ((0, 0), (1, WJ - 1)):
                    p.op("dve", lambda e, dchunk=dchunk, eix=eix, jcol=jcol,
                         edgec=edgec: e.tensor_copy(
                        edgec[:, dchunk * 2 + eix:dchunk * 2 + eix + 1],
                        regB[:, dchunk * 2048 + jcol:dchunk * 2048 + jcol + 1]))
            for h in range(NHEADS):
                ch, ro = h // 2, (h % 2) * 64
                for st in range(16):
                    lhs = actT[ro:ro + 64,
                               ch * TOK + st * 128:ch * TOK + (st + 1) * 128]
                    for jb in range(4):
                        p.op("pe", lambda e, lhs=lhs, ch=ch, ro=ro, jb=jb: e.matmul(
                            ps_a[:, :], lhs,
                            regB[ro:ro + 64,
                                 ch * 2048 + jb * 512:ch * 2048 + (jb + 1) * 512],
                            start=True, stop=True))
                        p.op("act", lambda e, jb=jb: e.activation(
                            stage[:, jb * 512:(jb + 1) * 512], ps_a[:, :],
                            AF.Exp, scale=1.0 / BSC))
                    p.dma("sync", lambda e, slab=slab, h=h, st=st: e.dma_start(
                        out=slab[h, st * 128:(st + 1) * 128, :], in_=stage[:, :]))

        slab1f = slab1.ap().rearrange("h a b -> (h a b)")
        slab2f = slab2.ap().rearrange("h a b -> (h a b)")
        for h in range(NHEADS):
            ch, ro = h // 2, (h % 2) * 64

            def qsl(c0, w, ch=ch, ro=ro):
                return qT[ro:ro + 64, ch * TOK + c0:ch * TOK + c0 + w]

            def ksl(c0, w, ch=ch, ro=ro):
                return kT[ro:ro + 64, ch * TOK + c0:ch * TOK + c0 + w]

            for eix, dstrow in ((0, qpe1), (1, qpe0)):
                for tb in range(NTB):
                    p.op("pe", lambda e, eix=eix, tb=tb, ch=ch, ro=ro, qsl=qsl:
                         e.matmul(
                             ps_r1[:, :],
                             pkedgec[ro:ro + 64, ch * 2 + eix:ch * 2 + eix + 1],
                             qsl(tb * TB, TB), start=True, stop=True))
                    p.op("act", lambda e, dstrow=dstrow, tb=tb: e.activation(
                        dstrow[0:1, tb * TB:(tb + 1) * TB], ps_r1[:, :],
                        AF.Exp, scale=1.0 / BSC))
            for tt in range(16):
                p.op("pe", lambda e, tt=tt, ch=ch, ro=ro, ksl=ksl: e.matmul(
                    ps_b[:, 0:2],
                    ksl(tt * 128, 128),
                    pqedgec[ro:ro + 64, ch * 2:ch * 2 + 2],
                    start=True, stop=True))
                p.op("act", lambda e, tt=tt: e.activation(
                    kpec[:, tt * 2:tt * 2 + 2], ps_b[:, 0:2],
                    AF.Copy, scale=1.0 / BSC))
            for sc in range(4):
                s0 = sc * 512
                p.op("pe", lambda e, s0=s0: e.matmul(
                    ps_b[:, :], onerb[:], qpe0[0:1, s0:s0 + 512],
                    start=True, stop=True))
                p.op("dve", lambda e: e.tensor_copy(qpbc[:, 0:512], ps_b[:, :]))
                p.op("pe", lambda e, s0=s0: e.matmul(
                    ps_b[:, :], onerb[:], qpe1[0:1, s0:s0 + 512],
                    start=True, stop=True))
                p.op("dve", lambda e: e.tensor_copy(qpbc[:, 512:1024], ps_b[:, :]))
                for tt in range(16):
                    t0 = tt * 128
                    DDiag = t0 - s0
                    p.op("pe", lambda e, tt=tt, s0=s0, ksl=ksl, qsl=qsl: e.matmul(
                        ps_a[:, :], ksl(tt * 128, 128), qsl(s0, 512),
                        start=True, stop=True))
                    if DDiag >= 1024:
                        p.op("act", lambda e, tt=tt: e.activation(
                            ptile[:], ps_a[:, :], AF.Exp,
                            bias=kpec[:, tt * 2:tt * 2 + 1], scale=1.0 / C1))
                        p.op("dve", lambda e: e.tensor_mul(
                            ptile[:], ptile[:], qpbc[:, 0:512]))
                    elif DDiag <= -640:
                        p.op("act", lambda e, tt=tt: e.activation(
                            ptile[:], ps_a[:, :], AF.Exp,
                            bias=kpec[:, tt * 2 + 1:tt * 2 + 2], scale=1.0 / C1))
                        p.op("dve", lambda e: e.tensor_mul(
                            ptile[:], ptile[:], qpbc[:, 512:1024]))
                    else:
                        p.op("act", lambda e: e.activation(
                            ptile[:], ps_a[:, :], AF.Exp, scale=1.0 / C1))
                        off1 = h * SLABR + s0 * (WJ - 1) + t0 + (WJ // 2 - 1)
                        src1 = bass.AP(slab1f.tensor, off1, [[WJ - 1, 512], [1, 128]])
                        p.dma("sync", lambda e, src1=src1: e.dma_start_transpose(
                            etile[:, 0:512], src1))
                        off2 = h * SLABR + t0 * (WJ - 1) + s0 + (WJ // 2 - 1)
                        src2 = bass.AP(slab2f.tensor, off2, [[WJ - 1, 128], [1, 512]])
                        p.dma("sync", lambda e, src2=src2: e.dma_start(
                            out=etile[:, 512:1024], in_=src2))
                        p.op("dve", lambda e: e.tensor_mul(
                            ptile[:], ptile[:], etile[:, 0:512]))
                        p.op("dve", lambda e: e.tensor_mul(
                            ptile[:], ptile[:], etile[:, 512:1024]))
                    p.op("pe", lambda e, tt=tt, h=h: e.matmul(
                        ps_pv[0:65, :],
                        vtok[:, tt * 1040 + h * 65:tt * 1040 + (h + 1) * 65],
                        ptile[:], start=(tt == 0), stop=(tt == 15),
                        skip_group_check=True))
                p.op("dve", lambda e: e.reciprocal(recv[0:1, 0:512], ps_pv[64:65, :]))
                p.op("pe", lambda e: e.matmul(
                    ps_b[0:64, :], oner[0:1, 0:64], recv[0:1, 0:512],
                    start=True, stop=True))
                p.op("dve", lambda e: e.tensor_copy(recv[0:64, :], ps_b[0:64, :]))
                p.op("dve", lambda e, sc=sc, ch=ch, ro=ro: e.tensor_tensor(
                    regB[ro:ro + 64, ch * TOK + sc * 512:ch * TOK + (sc + 1) * 512],
                    ps_pv[0:64, :], recv[0:64, :], ALU.mult))

        for ec in range(DCH):
            for kk in range(DCH):
                p.dma("pool", lambda e, ec=ec, kk=kk: e.dma_start(
                    out=wstage[:, kk * 128:(kk + 1) * 128],
                    in_=wo[kk * 128:(kk + 1) * 128, ec * 128:(ec + 1) * 128]))
            for tb in range(NTB):
                for kk in range(DCH):
                    p.op("pe", lambda e, ec=ec, tb=tb, kk=kk: e.matmul(
                        ps_a[:, :], wstage[:, kk * 128:(kk + 1) * 128],
                        regB[:, kk * TOK + tb * TB:kk * TOK + (tb + 1) * TB],
                        start=(kk == 0), stop=(kk == DCH - 1)))
                p.op("act", lambda e, ec=ec, tb=tb: e.activation(
                    scr1[:], ps_a[:, :], AF.Identity,
                    bias=bcols[:, 2 * DCH + ec:2 * DCH + ec + 1]))
                p.dma("sync", lambda e, ec=ec, tb=tb: e.dma_start(
                    out=o_s1[ec * 128:(ec + 1) * 128, tb * TB:(tb + 1) * TB],
                    in_=scr1[:]))



def build_l1a():
    nc = bass.Bass("TRN2", target_bir_lowering=False)
    io = {}
    io["xT"] = nc.dram_tensor("xT", [D, TOK], F32, kind="ExternalInput").ap()
    io["s0T"] = nc.dram_tensor("s0T", [D, TOK], F32, kind="ExternalInput").ap()
    io["gT"] = nc.dram_tensor("gT", [D, TOK], F32, kind="ExternalInput").ap()
    io["wq"] = nc.dram_tensor("wq", [D, D], F32, kind="ExternalInput").ap()
    io["wk"] = nc.dram_tensor("wk", [D, D], F32, kind="ExternalInput").ap()
    io["wv"] = nc.dram_tensor("wv", [D, D], F32, kind="ExternalInput").ap()
    io["wo"] = nc.dram_tensor("wo", [D, D], F32, kind="ExternalInput").ap()
    io["wpk"] = nc.dram_tensor("wpk", [D, D], F32, kind="ExternalInput").ap()
    io["wpq"] = nc.dram_tensor("wpq", [D, D], F32, kind="ExternalInput").ap()
    io["bq_c"] = nc.dram_tensor("bq_c", [128, DCH], F32, kind="ExternalInput").ap()
    io["bk_c"] = nc.dram_tensor("bk_c", [128, DCH], F32, kind="ExternalInput").ap()
    io["bo_c"] = nc.dram_tensor("bo_c", [128, DCH], F32, kind="ExternalInput").ap()
    io["bv_r"] = nc.dram_tensor("bv_r", [1, D], BF16, kind="ExternalInput").ap()
    io["bpk_r"] = nc.dram_tensor("bpk_r", [1, D], BF16, kind="ExternalInput").ap()
    io["bpq_r"] = nc.dram_tensor("bpq_r", [1, D], BF16, kind="ExternalInput").ap()
    io["rel_emb"] = nc.dram_tensor("rel_emb", [512, D], F32, kind="ExternalInput").ap()
    io["lng_r"] = nc.dram_tensor("lng_r", [1, D], F32, kind="ExternalInput").ap()
    io["lnb_r"] = nc.dram_tensor("lnb_r", [1, D], F32, kind="ExternalInput").ap()
    io["expand1"] = nc.dram_tensor("expand1", [512, WJ], BF16, kind="ExternalInput").ap()
    io["expand2"] = nc.dram_tensor("expand2", [512, WJ], BF16, kind="ExternalInput").ap()
    io["ones_col"] = nc.dram_tensor("ones_col", [128, 1], F32, kind="ExternalInput").ap()
    io["ones_row"] = nc.dram_tensor("ones_row", [1, 128], F32, kind="ExternalInput").ap()
    io["ones_rowb"] = nc.dram_tensor("ones_rowb", [1, 128], BF16, kind="ExternalInput").ap()
    io["ident_bf"] = nc.dram_tensor("ident_bf", [128, 128], BF16, kind="ExternalInput").ap()
    io["o_x1"] = nc.dram_tensor("o_x1", [D, TOK], F32, kind="ExternalOutput").ap()
    io["o_xn1"] = nc.dram_tensor("o_xn1", [D, TOK], F32, kind="ExternalOutput").ap()
    io["o_s1"] = nc.dram_tensor("o_s1", [D, TOK], F32, kind="ExternalOutput").ap()
    io["o_dslab1"] = nc.dram_tensor("o_dslab1", [128, WJ], BF16, kind="ExternalOutput").ap()
    io["o_dslab2"] = nc.dram_tensor("o_dslab2", [128, WJ], BF16, kind="ExternalOutput").ap()
    io["o_det"] = nc.dram_tensor("o_det", [128, 1024], BF16, kind="ExternalOutput").ap()
    io["o_dpt"] = nc.dram_tensor("o_dpt", [128, 512], BF16, kind="ExternalOutput").ap()
    io["o_dctx"] = nc.dram_tensor("o_dctx", [128, TOK], BF16, kind="ExternalOutput").ap()
    io["o_dk"] = nc.dram_tensor("o_dk", [128, TOK], BF16, kind="ExternalOutput").ap()
    io["o_dpv"] = nc.dram_tensor("o_dpv", [128, 512], F32, kind="ExternalOutput").ap()
    import contextlib as _cl
    with _cl.ExitStack() as _c:
        block = _c.enter_context(nc.Block())
        sem = [_c.enter_context(nc.semaphore(f"chain{i}")) for i in range(4)]
        p = Prog(nc)
        emit_l1a(nc, p, io)
        p.run(block, sem)
    return nc



def _bucket_f32(rel):
    rel = np.asarray(rel)
    mid = 128
    sign = np.sign(rel).astype(np.float32)
    abs_pos = np.where((rel < mid) & (rel > -mid), mid - 1,
                       np.abs(rel)).astype(np.float32)
    log_pos = np.ceil(
        np.log(abs_pos / np.float32(mid)) /
        np.float32(np.log(np.float32(511) / np.float32(mid))) *
        np.float32(mid - 1)).astype(np.float32) + mid
    bucket = np.where(abs_pos <= mid, rel.astype(np.float32), log_pos * sign)
    return bucket.astype(np.int32)


def _expand_mats():
    jp = np.arange(WJ)
    r = 1023 - jp
    b = _bucket_f32(r)
    g1 = np.clip(b + 256, 0, 511)
    g2 = np.clip(256 - b, 0, 511)
    e1 = (np.arange(512)[:, None] == g1[None, :])
    e2 = (np.arange(512)[:, None] == g2[None, :])
    import ml_dtypes
    return e1.astype(ml_dtypes.bfloat16), e2.astype(ml_dtypes.bfloat16)


def _prep_l1a_inputs(inputs):
    c = np.ascontiguousarray
    f = np.float32
    import ml_dtypes

    def cols(v):
        return c(np.asarray(v, f).reshape(DCH, 128).T)

    e1, e2 = _expand_mats()
    w = {
        "wq": c(np.asarray(inputs["L1_attn_Wq"], f)),
        "wk": c(np.asarray(inputs["L1_attn_Wk"], f)),
        "wv": c(np.asarray(inputs["L1_attn_Wv"], f)),
        "wo": c(np.asarray(inputs["L1_attn_Wo"], f)),
        "wpk": c(np.asarray(inputs["L1_attn_Wpk"], f)),
        "wpq": c(np.asarray(inputs["L1_attn_Wpq"], f)),
        "bq_c": cols(inputs["L1_attn_bq"]),
        "bk_c": cols(inputs["L1_attn_bk"]),
        "bo_c": cols(inputs["L1_attn_bo"]),
        "bv_r": np.asarray(inputs["L1_attn_bv"], f).reshape(1, D).astype(ml_dtypes.bfloat16),
        "bpk_r": np.asarray(inputs["L1_attn_bpk"], f).reshape(1, D).astype(ml_dtypes.bfloat16),
        "bpq_r": np.asarray(inputs["L1_attn_bpq"], f).reshape(1, D).astype(ml_dtypes.bfloat16),
        "rel_emb": c(np.asarray(inputs["L1_attn_rel_emb"], f)),
        "lng_r": np.asarray(inputs["L1_attn_ln_g"], f).reshape(1, D),
        "lnb_r": np.asarray(inputs["L1_attn_ln_b"], f).reshape(1, D),
        "expand1": c(e1),
        "expand2": c(e2),
        "ones_col": np.ones((128, 1), f),
        "ones_row": np.ones((1, 128), f),
        "ones_rowb": np.ones((1, 128), dtype=ml_dtypes.bfloat16),
        "ident_bf": np.eye(128).astype(ml_dtypes.bfloat16),
    }
    return w


def run_l1a(inputs, xT_stack, s0T_stack, gT_stack):
    w = _prep_l1a_inputs(inputs)
    nc = build_l1a()
    in_maps = []
    for b in range(NCORES):
        m = dict(w)
        m["xT"] = np.ascontiguousarray(xT_stack[b])
        m["s0T"] = np.ascontiguousarray(s0T_stack[b])
        m["gT"] = np.ascontiguousarray(gT_stack[b])
        in_maps.append(m)
    res = run_bass_kernel_spmd(nc, in_maps, core_ids=list(range(NCORES)))
    return res.results



NBLK = 16
FF = 4096
FCH = FF // 128


def emit_l1f(nc, p, io):

    x1T = io["x1T"]
    xn1T = io["xn1T"]
    s1T = io["s1T"]
    w_router = io["w_router"]
    b_router = io["b_router"]
    w1T = io["w1T"]
    b1c = io["b1c"]
    w2T = io["w2T"]
    b2c = io["b2c"]
    ones_row = io["ones_row"]
    ident_f32 = io["ident_f32"]

    o_out = io["o_out"]
    maskdram = nc.dram_tensor("f_maskdram", [NBLK, TOK], F32).ap()

    w0 = p.mark()
    with contextlib.ExitStack() as ctx:
        sb = SB(nc, ctx)
        xnbf = sb.t("xnbf", [128, DCH * TOK], BF16)
        hbuf = sb.t("hbuf", [128, FCH * 1024], BF16)
        mrow0 = sb.t("mrow0", [1, 1024], F32)
        lgs = sb.t("lgs", [128, TOK], F32)
        mbc = sb.t("mbc", [128, 1024], F32)
        stg = sb.t("stg", [128, TOK], F32)
        stg2 = sb.t("stg2", [128, TOK], F32)
        w1st = sb.t("w1st", [128, 1024], BF16)
        w2st = sb.t("w2st", [128, FCH * 128], BF16)
        rw = sb.t("rw", [128, DCH * NBLK], F32)
        brc = sb.t("brc", [128, 1], F32)
        b1cs = sb.t("b1cs", [128, FCH], F32)
        b2cs = sb.t("b2cs", [128, DCH], F32)
        oner = sb.t("oner", [1, 128], F32)
        identf = sb.t("identf", [128, 128], F32)
        ptile = sb.t("ptile", [128, NBLK], F32)
        mk = sb.t("mk", [128, NBLK], F32)
        eq = sb.t("eq", [128, NBLK], F32)
        tc = sb.t("tc", [128, 2], F32)
        sbst = sb.t("sbst", [128, TB], BF16)
        ps_lg = sb.p("ps_lg", [128, TOK], F32)
        ps_a = sb.p("ps_a", [128, TB], F32)
        ps_tp32 = sb.p("ps_tp32", [128, 128], F32)
        ps_bc = sb.p("ps_bc", [128, TB], F32)


        p.dma("sync", lambda e: e.dma_start(out=oner[:], in_=ones_row), wait=w0)
        p.dma("sync", lambda e: e.dma_start(out=identf[:], in_=ident_f32), wait=w0)
        p.dma("sync", lambda e: e.dma_start(out=brc[0:NBLK, :], in_=b_router), wait=w0)
        p.dma("sync", lambda e: e.dma_start(out=b1cs[:], in_=b1c), wait=w0)
        p.dma("sync", lambda e: e.dma_start(out=b2cs[:], in_=b2c), wait=w0)
        for k in range(DCH):
            p.dma("sync", lambda e, k=k: e.dma_start(
                out=rw[:, k * NBLK:(k + 1) * NBLK],
                in_=w_router[k * 128:(k + 1) * 128, :]), wait=w0)

        for k in range(DCH):
            p.dma("sync", lambda e, k=k: e.dma_start(
                out=stg[:], in_=xn1T[k * 128:(k + 1) * 128, :]))
            p.op("dve", lambda e, k=k: e.tensor_copy(
                xnbf[:, k * TOK:(k + 1) * TOK], stg[:]))
            for tb in range(NTB):
                p.op("pe", lambda e, k=k, tb=tb: e.matmul(
                    ps_lg[0:NBLK, tb * TB:(tb + 1) * TB],
                    rw[:, k * NBLK:(k + 1) * NBLK],
                    stg[:, tb * TB:(tb + 1) * TB],
                    start=(k == 0), stop=(k == DCH - 1),
                    skip_group_check=True))
        for tb in range(NTB):
            p.op("act", lambda e, tb=tb: e.activation(
                lgs[0:NBLK, tb * TB:(tb + 1) * TB],
                ps_lg[0:NBLK, tb * TB:(tb + 1) * TB],
                AF.Identity, bias=brc[0:NBLK, :]))
        for ti in range(16):
            p.op("pe", lambda e, ti=ti: e.transpose(
                ps_tp32[:, 0:NBLK], lgs[0:NBLK, ti * 128:(ti + 1) * 128],
                identf[0:NBLK, 0:NBLK]))
            p.op("dve", lambda e: e.tensor_copy(ptile[:], ps_tp32[:, 0:NBLK]))
            p.op("dve", lambda e: e.memset(mk[:], 0.0))
            for rnd in range(4):
                p.op("dve", lambda e: e.tensor_reduce(
                    tc[:, 0:1], ptile[:], AX.X, ALU.max))
                p.op("dve", lambda e: e.tensor_scalar(
                    eq[:], ptile[:], tc[:, 0:1], None, ALU.is_equal))
                p.op("dve", lambda e: e.tensor_add(mk[:], mk[:], eq[:]))
                if rnd < 3:
                    p.op("dve", lambda e: e.tensor_scalar_mul(eq[:], eq[:], -1e30))
                    p.op("dve", lambda e: e.tensor_add(ptile[:], ptile[:], eq[:]))
            p.op("pe", lambda e, ti=ti: e.transpose(
                ps_tp32[0:NBLK, 0:128], mk[:], identf[:]))
            p.op("dve", lambda e, ti=ti: e.tensor_copy(
                lgs[0:NBLK, ti * 128:(ti + 1) * 128], ps_tp32[0:NBLK, 0:128]))
        p.dma("sync", lambda e: e.dma_start(out=maskdram, in_=lgs[0:NBLK, :]))

        for half in range(2):
            hoff = half * 1024
            for blkb in range(NBLK):
                p.dma("sync", lambda e, blkb=blkb, hoff=hoff: e.dma_start(
                    out=mrow0[:], in_=maskdram[blkb:blkb + 1, hoff:hoff + 1024]))
                for bl in range(2):
                    p.op("pe", lambda e, bl=bl: e.matmul(
                        ps_bc[:, :], oner[:], mrow0[0:1, bl * TB:(bl + 1) * TB],
                        start=True, stop=True))
                    p.op("dve", lambda e, bl=bl: e.tensor_copy(
                        mbc[:, bl * TB:(bl + 1) * TB], ps_bc[:, :]))
                for fc in (2 * blkb, 2 * blkb + 1):
                    for kk in range(DCH):
                        p.dma("pool", lambda e, fc=fc, kk=kk: e.dma_start(
                            out=w1st[:, kk * 128:(kk + 1) * 128],
                            in_=w1T[kk * 128:(kk + 1) * 128, fc * 128:(fc + 1) * 128]))
                    for bl in range(2):
                        for kk in range(DCH):
                            p.op("pe", lambda e, fc=fc, kk=kk, hoff=hoff, bl=bl:
                                 e.matmul(
                                     ps_a[:, :], w1st[:, kk * 128:(kk + 1) * 128],
                                     xnbf[:, kk * TOK + hoff + bl * TB:
                                          kk * TOK + hoff + (bl + 1) * TB],
                                     start=(kk == 0), stop=(kk == DCH - 1)))
                        p.op("act", lambda e, fc=fc, bl=bl: e.activation(
                            sbst[:, :], ps_a[:, :], AF.Relu,
                            bias=b1cs[:, fc:fc + 1]))
                        p.op("dve", lambda e, fc=fc, bl=bl: e.tensor_tensor(
                            hbuf[:, fc * 1024 + bl * TB:fc * 1024 + (bl + 1) * TB],
                            sbst[:, :], mbc[:, bl * TB:(bl + 1) * TB], ALU.mult))
            for ec in range(DCH):
                for kk in range(FCH):
                    p.dma("pool", lambda e, ec=ec, kk=kk: e.dma_start(
                        out=w2st[:, kk * 128:(kk + 1) * 128],
                        in_=w2T[kk * 128:(kk + 1) * 128, ec * 128:(ec + 1) * 128]))
                p.dma("sync", lambda e, ec=ec, hoff=hoff: e.dma_start(
                    out=stg[:, 0:1024], in_=x1T[ec * 128:(ec + 1) * 128,
                                                hoff:hoff + 1024]))
                p.dma("sync", lambda e, ec=ec, hoff=hoff: e.dma_start(
                    out=stg2[:, 0:1024], in_=s1T[ec * 128:(ec + 1) * 128,
                                                 hoff:hoff + 1024]))
                for bl in range(2):
                    for kk in range(FCH):
                        p.op("pe", lambda e, ec=ec, kk=kk, bl=bl: e.matmul(
                            ps_a[:, :], w2st[:, kk * 128:(kk + 1) * 128],
                            hbuf[:, kk * 1024 + bl * TB:kk * 1024 + (bl + 1) * TB],
                            start=(kk == 0), stop=(kk == FCH - 1)))
                    p.op("act", lambda e, ec=ec, bl=bl: e.activation(
                        stg2[:, 1024 + bl * TB:1024 + (bl + 1) * TB]
                        if False else stg[:, 1024 + bl * TB:1024 + (bl + 1) * TB],
                        ps_a[:, :], AF.Identity, bias=b2cs[:, ec:ec + 1]))
                    p.op("act", lambda e, bl=bl: e.activation(
                        stg[:, 1024 + bl * TB:1024 + (bl + 1) * TB],
                        stg[:, 1024 + bl * TB:1024 + (bl + 1) * TB], AF.Silu))
                    p.op("dve", lambda e, bl=bl: e.tensor_mul(
                        stg2[:, 1024 + bl * TB:1024 + (bl + 1) * TB]
                        if False else stg2[:, bl * TB:(bl + 1) * TB],
                        stg2[:, bl * TB:(bl + 1) * TB],
                        stg[:, 1024 + bl * TB:1024 + (bl + 1) * TB]))
                    p.op("dve", lambda e, bl=bl: e.tensor_add(
                        stg[:, bl * TB:(bl + 1) * TB],
                        stg[:, bl * TB:(bl + 1) * TB],
                        stg2[:, bl * TB:(bl + 1) * TB]))
                p.dma("sync", lambda e, ec=ec, hoff=hoff: e.dma_start(
                    out=o_out[ec * 128:(ec + 1) * 128, hoff:hoff + 1024],
                    in_=stg[:, 0:1024]))



def build_l1f():
    nc = bass.Bass("TRN2", target_bir_lowering=False)
    io = {}
    io["x1T"] = nc.dram_tensor("x1T", [D, TOK], F32, kind="ExternalInput").ap()
    io["xn1T"] = nc.dram_tensor("xn1T", [D, TOK], F32, kind="ExternalInput").ap()
    io["s1T"] = nc.dram_tensor("s1T", [D, TOK], F32, kind="ExternalInput").ap()
    io["w_router"] = nc.dram_tensor("w_router", [D, NBLK], F32, kind="ExternalInput").ap()
    io["b_router"] = nc.dram_tensor("b_router", [NBLK, 1], F32, kind="ExternalInput").ap()
    io["w1T"] = nc.dram_tensor("w1T", [D, FF], F32, kind="ExternalInput").ap()
    io["b1c"] = nc.dram_tensor("b1c", [128, FCH], F32, kind="ExternalInput").ap()
    io["w2T"] = nc.dram_tensor("w2T", [FF, D], F32, kind="ExternalInput").ap()
    io["b2c"] = nc.dram_tensor("b2c", [128, DCH], F32, kind="ExternalInput").ap()
    io["ones_row"] = nc.dram_tensor("ones_row", [1, 128], F32, kind="ExternalInput").ap()
    io["ident_f32"] = nc.dram_tensor("ident_f32", [128, 128], F32, kind="ExternalInput").ap()
    io["o_out"] = nc.dram_tensor("o_out", [D, TOK], F32, kind="ExternalOutput").ap()
    io["maskdram"] = nc.dram_tensor("maskdram", [NBLK, TOK], F32).ap()
    import contextlib as _cl
    with _cl.ExitStack() as _c:
        block = _c.enter_context(nc.Block())
        sem = [_c.enter_context(nc.semaphore(f"chain{i}")) for i in range(4)]
        p = Prog(nc)
        emit_l1f(nc, p, io)
        p.run(block, sem)
    return nc



def _prep_l1f_inputs(inputs):
    c = np.ascontiguousarray
    f = np.float32

    def cols(v, nch):
        return c(np.asarray(v, f).reshape(nch, 128).T)

    return {
        "w_router": c(np.asarray(inputs["L1_ffn_w_router"], f)),
        "b_router": c(np.asarray(inputs["L1_ffn_b_router"], f).reshape(NBLK, 1)),
        "w1T": c(np.asarray(inputs["L1_ffn_w1"], f).T),
        "b1c": cols(inputs["L1_ffn_b1"], FCH),
        "w2T": c(np.asarray(inputs["L1_ffn_w2"], f).T),
        "b2c": cols(inputs["L1_ffn_b2"], DCH),
        "ones_row": np.ones((1, 128), f),
        "ident_f32": np.eye(128, dtype=f),
    }


def run_l1f(inputs, x1T_stack, xn1T_stack, s1T_stack):
    w = _prep_l1f_inputs(inputs)
    nc = build_l1f()
    in_maps = []
    for b in range(NCORES):
        m = dict(w)
        m["x1T"] = np.ascontiguousarray(x1T_stack[b])
        m["xn1T"] = np.ascontiguousarray(xn1T_stack[b])
        m["s1T"] = np.ascontiguousarray(s1T_stack[b])
        in_maps.append(m)
    res = run_bass_kernel_spmd(nc, in_maps, core_ids=list(range(NCORES)))
    return res.results


RG8 = [list(range(NCORES))]
DEBUG_FUSED = False


def build_fused():
    nc = bass.Bass("TRN2", target_bir_lowering=False)
    E_IN, E_OUT = "ExternalInput", "ExternalOutput"
    d = {}

    def inp(name, shape, dt=F32):
        d[name] = nc.dram_tensor(name, shape, dt, kind=E_IN).ap()
        return d[name]

    inp("xT", [D, TOK])
    inp("ridx", [128, 1], I32)
    inp("sel_g", [64, EXP])
    inp("sel_l", [64, EXP])
    inp("w_inT", [D, 2048]); inp("w_stateT", [D, 128]); inp("w_dtT", [D, 128])
    inp("b_dt", [128, 1]); inp("a_log", [128, 1]); inp("w_cT", [128, D])
    inp("dvec", [128, DCH]); inp("w_outT", [D, D])
    inp("qkv_wT", [D, QKV_O]); inp("proj_wT", [D, D]); inp("router_w", [D, EXP])
    inp("ln1w", [128, DCH]); inp("ln1b", [128, DCH])
    inp("ln2w", [128, DCH]); inp("ln2b", [128, DCH])
    inp("moe_w1", [EXP, D, 2048]); inp("moe_w2", [EXP, 2048, D])
    inp("ones_row8", [1, EXP]); inp("eiota", [EXP, 1])
    inp("tokhalf", [1, 512]); inp("jiota", [1, 512])
    inp("wq", [D, D]); inp("wk", [D, D]); inp("wv", [D, D]); inp("wo", [D, D])
    inp("wpk", [D, D]); inp("wpq", [D, D])
    inp("bq_c", [128, DCH]); inp("bk_c", [128, DCH]); inp("bo_c", [128, DCH])
    inp("bv_r", [1, D], BF16); inp("bpk_r", [1, D], BF16); inp("bpq_r", [1, D], BF16)
    inp("rel_emb", [512, D]); inp("lng_r", [1, D]); inp("lnb_r", [1, D])
    inp("expand1", [512, WJ], BF16); inp("expand2", [512, WJ], BF16)
    inp("w_router", [D, NBLK]); inp("b_router", [NBLK, 1])
    inp("w1T", [D, FF]); inp("b1c", [128, FCH])
    inp("w2T", [FF, D]); inp("b2c", [128, DCH])
    inp("ones_col", [128, 1]); inp("ones_row", [1, 128])
    inp("ones_rowb", [1, 128], BF16)
    inp("ident_bf", [128, 128], BF16); inp("ident_f32", [128, 128])
    inp("iota8", [128, EXP])

    xn_i = nc.dram_tensor("xn_i", [D, TOK], F32)
    s0_i = nc.dram_tensor("s0_i", [D, TOK], F32)
    ag_xn = nc.dram_tensor("ag_xn", [NCORES * D, TOK], F32, addr_space="Shared")
    h1_i = nc.dram_tensor("h1_i", [D, TOK], F32)
    ln2tok_i = nc.dram_tensor("ln2tok_i", [TOK, D], BF16)
    topi_i = nc.dram_tensor("topi_i", [TOK, 2], F32)
    topp_i = nc.dram_tensor("topp_i", [TOK, 2], F32)
    cnt_i = nc.dram_tensor("cnt_i", [EXP, EXP], F32)
    cnt_all = nc.dram_tensor("cnt_all", [64, EXP], F32, addr_space="Shared")
    g_i = nc.dram_tensor("g_i", [D, TOK], F32)
    ag_g = nc.dram_tensor("ag_g", [NCORES * D, TOK], F32, addr_space="Shared")
    x1_i = nc.dram_tensor("x1_i", [D, TOK], F32)
    xn1_i = nc.dram_tensor("xn1_i", [D, TOK], F32)
    s1_i = nc.dram_tensor("s1_i", [D, TOK], F32)
    o_out = nc.dram_tensor("o_out", [D, TOK], F32, kind=E_OUT).ap()

    io_l0a = dict(d)
    io_l0a.update({"o_xn": xn_i.ap(), "o_s0": s0_i.ap()})
    io_b1 = dict(d)
    io_b1.update({"ag_xn": ag_xn.ap().rearrange("a b -> (a b)").unsqueeze(1),
                  "o_h1": h1_i.ap(), "o_ln2tok": ln2tok_i.ap(),
                  "o_topi": topi_i.ap(), "o_topp": topp_i.ap(),
                  "o_counts": cnt_i.ap()})
    io_b2 = dict(d)
    io_b2.update({"ln2tok": ln2tok_i.ap(), "h1T": h1_i.ap(),
                  "topi": topi_i.ap(), "topp": topp_i.ap(),
                  "counts_all": cnt_all.ap(), "o_g": g_i.ap()})
    io_l1a = dict(d)
    io_l1a.update({"s0T": s0_i.ap(), "ag_g": ag_g.ap().rearrange("a b -> (a b)").unsqueeze(1),
                   "o_x1": x1_i.ap(), "o_xn1": xn1_i.ap(), "o_s1": s1_i.ap()})
    io_l1f = dict(d)
    io_l1f.update({"x1T": x1_i.ap(), "xn1T": xn1_i.ap(), "s1T": s1_i.ap(),
                   "o_out": o_out})

    with contextlib.ExitStack() as ctx:
        block = ctx.enter_context(nc.Block())
        sem = [ctx.enter_context(nc.semaphore(f"chain{i}")) for i in range(8)]
        cc_sems = [ctx.enter_context(nc.semaphore(f"ccs{i}")) for i in range(3)]
        p = Prog(nc)

        def coll(ins_t, outs_t, ded):
            def fn(e):
                e.collective_compute(
                    "AllGather", ALU.bypass, RG8,
                    [ins_t.ap().opt()], [outs_t.ap().opt()]).then_inc(ded, 1)
                e.wait_ge(ded, 1)
                return e.nop()
            p.op("pool", fn)

        emit_l0a(nc, p, io_l0a)
        coll(xn_i, ag_xn, cc_sems[0])
        emit_l0b1(nc, p, io_b1)
        coll(cnt_i, cnt_all, cc_sems[1])
        emit_l0b2(nc, p, io_b2)
        coll(g_i, ag_g, cc_sems[2])
        emit_l1a(nc, p, io_l1a)
        emit_l1f(nc, p, io_l1f)
        if DEBUG_FUSED:
            for nm, t in (("d_xn", xn_i), ("d_s0", s0_i), ("d_h1", h1_i),
                          ("d_g", g_i), ("d_x1", x1_i), ("d_s1", s1_i),
                          ("d_xn1", xn1_i)):
                dst = nc.dram_tensor(nm, list(t.ap().shape), F32,
                                     kind="ExternalOutput").ap()
                p.dma("sync", lambda e, dst=dst, t=t: e.dma_start(
                    out=dst, in_=t.ap()))
            dag = nc.dram_tensor("d_agxn", [NCORES * D, TOK], F32,
                                 kind="ExternalOutput").ap()
            p.dma("sync", lambda e, dag=dag: e.dma_start(out=dag, in_=ag_xn.ap()))
        p.run(block, sem)
    return nc


def kernel(**inputs):
    inputs = {k: np.asarray(v) for k, v in inputs.items()}
    w = {}
    w.update(_prep_l0a_inputs(inputs))
    w.update(_prep_l0b1_inputs(inputs))
    w.update(_prep_l1a_inputs(inputs))
    w.update(_prep_l1f_inputs(inputs))
    b2c = _prep_l0b2_inputs(inputs, 0)
    for kk in ("moe_w1", "moe_w2", "ones_row8", "eiota", "tokhalf", "jiota"):
        w[kk] = b2c[kk]
    x = np.asarray(inputs["L0_attn_x"], np.float32)

    nc = build_fused()
    in_maps = []
    for b in range(NCORES):
        m = dict(w)
        sels = _prep_l0b2_inputs(inputs, b)
        m["sel_g"] = sels["sel_g"]
        m["sel_l"] = sels["sel_l"]
        m["xT"] = np.ascontiguousarray(x[b].T)
        m["ridx"] = (np.arange(128, dtype=np.int64) * TOK +
                     b * 256).astype(np.int32).reshape(128, 1)
        in_maps.append(m)
    res = run_bass_kernel_spmd(nc, in_maps, core_ids=list(range(NCORES)))
    out = np.stack([np.asarray(r["o_out"], np.float32).T for r in res.results])
    return np.ascontiguousarray(out)


def kernel_multilaunch(**inputs):
    inputs = {k: np.asarray(v) for k, v in inputs.items()}
    xnT, s0T = run_l0a(inputs)
    b1res = run_l0b1(inputs, xnT)
    b2res = run_l0b2(inputs, b1res)
    gT = np.empty((NCORES, D, TOK), np.float32)
    for cc in range(NCORES):
        gc = np.asarray(b2res[cc]["o_g"], np.float32)
        for s in range(8):
            gT[s][:, cc * 256:(cc + 1) * 256] = gc[:, s * 256:(s + 1) * 256]
    x = np.asarray(inputs["L0_attn_x"], np.float32)
    xT = np.ascontiguousarray(np.swapaxes(x, 1, 2))
    l1a = run_l1a(inputs, xT, s0T, gT)
    x1T = np.stack([np.asarray(r["o_x1"], np.float32) for r in l1a])
    xn1T = np.stack([np.asarray(r["o_xn1"], np.float32) for r in l1a])
    s1T = np.stack([np.asarray(r["o_s1"], np.float32) for r in l1a])
    l1f = run_l1f(inputs, x1T, xn1T, s1T)
    out = np.stack([np.asarray(r["o_out"], np.float32).T for r in l1f])
    return np.ascontiguousarray(out)



NH, NKV, EXP, TOPK = 16, 4, 8, 2
HD = 64
QKV_O = 1536
CAP = 5120


def emit_tp128(p, ps_tp, src, ident, dst):
    p.op("pe", lambda e: e.transpose(ps_tp[:, :], src, ident))
    p.op("dve", lambda e: e.tensor_copy(dst, ps_tp[:, :]))


def emit_l0b1(nc, p, io):
    QCH = QKV_O // 128

    xnT = io.get("xnT")
    qkv_wT = io["qkv_wT"]
    proj_wT = io["proj_wT"]
    router_w = io["router_w"]
    ln1w = io["ln1w"]
    ln1b = io["ln1b"]
    ln2w = io["ln2w"]
    ln2b = io["ln2b"]
    ones_col = io["ones_col"]
    ones_row = io["ones_row"]
    ident_bf = io["ident_bf"]
    ident_f32 = io["ident_f32"]
    iota8 = io["iota8"]

    o_h1 = io["o_h1"]
    o_ln2tok = io["o_ln2tok"]
    o_topi = io["o_topi"]
    o_topp = io["o_topp"]
    o_counts = io["o_counts"]

    w0 = p.mark()
    with contextlib.ExitStack() as ctx:
        sb = SB(nc, ctx)
        xh = sb.t("xh", [128, DCH * TOK], F32)
        lnbf = sb.t("lnbf", [128, DCH * TOK], BF16)
        qkv_tok = sb.t("qkv_tok", [128, 16 * QKV_O], BF16)
        wstage = sb.t("wstage", [128, 2 * 1024], BF16)
        rw_sb = sb.t("rw_sb", [128, DCH * EXP], F32)
        rwp_sb = sb.t("rwp_sb", [128, DCH * EXP], F32)
        rwb_sb = sb.t("rwb_sb", [128, DCH * EXP], F32)
        c12 = sb.t("c12", [1, 2 * EXP], F32)
        c12b = sb.t("c12b", [128, 2 * EXP], F32)
        stage = sb.t("stage", [128, TB], BF16)
        kexp = sb.t("kexp", [128, NH * HD], BF16)
        vexp = sb.t("vexp", [128, NH * HD], BF16)
        prod = sb.t("prod", [128, NH * HD], F32)
        scox = sb.t("scox", [128, NH * 8], F32)
        rsum16 = sb.t("rsum16", [128, NH], F32)
        aoacc = sb.t("aoacc", [128, NH * HD], F32)
        aobf = sb.t("aobf", [128, NH * HD], BF16)
        onec = sb.t("onec", [128, 1], F32)
        oner = sb.t("oner", [1, 128], F32)
        identb = sb.t("identb", [128, 128], BF16)
        identf = sb.t("identf", [128, 128], F32)
        iota8_sb = sb.t("iota8_sb", [128, EXP], F32)
        lncols = sb.t("lncols", [128, 4 * DCH], F32)
        mrow_t = sb.t("mrow_t", [1, TOK], F32)
        rrow = sb.t("rrow", [1, TOK], F32)
        scrow = sb.t("scrow", [1, TOK], F32)
        m_bc = sb.t("m_bc", [128, TB], F32)
        r_bc = sb.t("r_bc", [128, TB], F32)
        scr1 = sb.t("scr1", [128, TB], F32)
        scr2 = sb.t("scr2", [128, TB], F32)
        logits = sb.t("logits", [128, TOK], F32)
        ptile = sb.t("ptile", [128, EXP], F32)
        tcols = sb.t("tcols", [128, 8], F32)
        cnt_row = sb.t("cnt_row", [1, EXP * EXP], F32)
        ps_mm = sb.p("ps_mm", [128, TB], F32)
        ps_tp = sb.p("ps_tp", [128, 128], BF16)
        ps_tp32 = sb.p("ps_tp32", [128, 128], F32)
        ps_bc = sb.p("ps_bc", [128, TB], F32)
        ps_r1 = sb.p("ps_r1", [1, TB], F32)
        ps_r2 = sb.p("ps_r2", [1, TB], F32)
        ps_c = sb.p("ps_c", [1, EXP], F32)


        xch = [xh[:, k * TOK:(k + 1) * TOK] for k in range(DCH)]
        lnch = [lnbf[:, k * TOK:(k + 1) * TOK] for k in range(DCH)]
        mrow = mrow_t

        if "ag_xn" in io:
            ridx_sb = sb.t("ridx_sb", [128, 1], I32)
            p.dma("sync", lambda e: e.dma_start(out=ridx_sb[:], in_=io["ridx"]),
                  wait=w0)
            rmark = p.mark()
            agf = io["ag_xn"]
            for k in range(DCH):
                for s in range(8):
                    p.dma("pool", lambda e, k=k, s=s: e.indirect_dma_start(
                        xch[k][:, s * 256:(s + 1) * 256], None, agf,
                        bass.IndirectOffsetOnAxis(ridx_sb[:], 0),
                        element_offset=(s * 1024 + k * 128) * 2048),
                        wait=rmark)
        else:
            for k in range(DCH):
                p.dma("sync", lambda e, k=k: e.dma_start(
                    out=xch[k], in_=xnT[k * 128:(k + 1) * 128, :]), wait=w0)
        p.dma("sync", lambda e: e.dma_start(out=onec[:], in_=ones_col), wait=w0)
        p.dma("sync", lambda e: e.dma_start(out=oner[:], in_=ones_row), wait=w0)
        p.dma("sync", lambda e: e.dma_start(out=identb[:], in_=ident_bf), wait=w0)
        p.dma("sync", lambda e: e.dma_start(out=identf[:], in_=ident_f32), wait=w0)
        p.dma("sync", lambda e: e.dma_start(out=iota8_sb[:], in_=iota8), wait=w0)
        p.dma("sync", lambda e: e.dma_start(out=lncols[:, 0:DCH], in_=ln1w), wait=w0)
        p.dma("sync", lambda e: e.dma_start(out=lncols[:, DCH:2 * DCH], in_=ln1b), wait=w0)
        p.dma("sync", lambda e: e.dma_start(out=lncols[:, 2 * DCH:3 * DCH], in_=ln2w), wait=w0)
        p.dma("sync", lambda e: e.dma_start(out=lncols[:, 3 * DCH:4 * DCH], in_=ln2b), wait=w0)
        for k in range(DCH):
            p.dma("sync", lambda e, k=k: e.dma_start(
                out=rw_sb[:, k * EXP:(k + 1) * EXP],
                in_=router_w[k * 128:(k + 1) * 128, :]), wait=w0)

        def ln_apply(wcol_off, bcol_off):
            for tb in range(NTB):
                sl = slice(tb * TB, (tb + 1) * TB)
                p.op("pe", lambda e, sl=sl: e.matmul(
                    ps_bc[:, :], oner[:], mrow[0:1, sl], start=True, stop=True))
                p.op("dve", lambda e: e.tensor_copy(m_bc[:], ps_bc[:, :]))
                p.op("pe", lambda e, sl=sl: e.matmul(
                    ps_bc[:, :], oner[:], rrow[0:1, sl], start=True, stop=True))
                p.op("dve", lambda e: e.tensor_copy(r_bc[:], ps_bc[:, :]))
                for k in range(DCH):
                    p.op("dve", lambda e, k=k, sl=sl: e.tensor_sub(
                        scr1[:], xch[k][:, sl], m_bc[:]))
                    p.op("dve", lambda e, k=k: e.tensor_mul(scr1[:], scr1[:], r_bc[:]))
                    p.op("dve", lambda e, k=k, sl=sl: e.tensor_scalar(
                        lnch[k][:, sl], scr1[:],
                        lncols[:, wcol_off + k:wcol_off + k + 1],
                        lncols[:, bcol_off + k:bcol_off + k + 1],
                        ALU.mult, ALU.add))

        emit_ln_rows(p, xch, scr2, ps_r1, ps_r2, onec[:], mrow[0:1, 0:TOK],
                     rrow[:], scrow[:])
        ln_apply(0, DCH)

        QCH = QKV_O // 128
        for ec in range(QCH):
            wslot = ec % 2
            for k in range(DCH):
                p.dma("pool", lambda e, ec=ec, k=k, wslot=wslot: e.dma_start(
                    out=wstage[:, wslot * 1024 + k * 128:wslot * 1024 + (k + 1) * 128],
                    in_=qkv_wT[k * 128:(k + 1) * 128, ec * 128:(ec + 1) * 128]))
            for tb in range(NTB):
                for k in range(DCH):
                    p.op("pe", lambda e, ec=ec, tb=tb, k=k, wslot=wslot: e.matmul(
                        ps_mm[:, :],
                        wstage[:, wslot * 1024 + k * 128:wslot * 1024 + (k + 1) * 128],
                        lnch[k][:, tb * TB:(tb + 1) * TB],
                        start=(k == 0), stop=(k == DCH - 1)))
                p.op("act", lambda e: e.activation(stage[:, :], ps_mm[:, :], AF.Copy))
                for j in range(4):
                    tile = tb * 4 + j
                    p.op("pe", lambda e, j=j: e.transpose(
                        ps_tp[:, :], stage[:, j * 128:(j + 1) * 128], identb[:]))
                    p.op("dve", lambda e, tile=tile, ec=ec: e.tensor_copy(
                        qkv_tok[:, tile * QKV_O + ec * 128:tile * QKV_O + (ec + 1) * 128],
                        ps_tp[:, :]))

        def tok_tile(t):
            return qkv_tok[:, t * QKV_O:(t + 1) * QKV_O]

        aoT = lnbf
        for s in range(8):
            for ic in range(2):
                ti = s * 2 + ic
                qv = tok_tile(ti).rearrange("p (g b) -> p g b", b=384)
                for t in range(s + 1):
                    tj = t * 2 + ic
                    kv = tok_tile(tj).rearrange("p (g b) -> p g b", b=384)
                    k_ap = kv[:, :, 256:320].unsqueeze(2).broadcast_to([128, NKV, 4, HD])
                    v_ap = kv[:, :, 320:384].unsqueeze(2).broadcast_to([128, NKV, 4, HD])
                    p.op("dve", lambda e, k_ap=k_ap: e.tensor_copy(
                        kexp[:].rearrange("p (g r d) -> p g r d", g=NKV, r=4), k_ap))
                    p.op("dve", lambda e, v_ap=v_ap: e.tensor_copy(
                        vexp[:].rearrange("p (g r d) -> p g r d", g=NKV, r=4), v_ap))
                    p.op("dve", lambda e, qv=qv: e.tensor_tensor(
                        prod[:].rearrange("p (g b) -> p g b", g=NKV),
                        qv[:, :, 0:256], kexp[:].rearrange("p (g b) -> p g b", g=NKV),
                        ALU.mult))
                    p.op("dve", lambda e, t=t: e.tensor_reduce(
                        scox[:].rearrange("p (h t) -> p h t", t=8)[:, :, t],
                        prod[:].rearrange("p (h d) -> p h d", d=HD), AX.X, ALU.add))
                sc3 = scox[:].rearrange("p (h t) -> p h t", t=8)[:, :, 0:s + 1]
                p.op("act", lambda e, sc3=sc3: e.activation(
                    sc3, sc3, AF.Exp, scale=0.125))
                p.op("dve", lambda e, sc3=sc3: e.tensor_reduce(
                    rsum16[:], sc3, AX.X, ALU.add))
                p.op("dve", lambda e: e.reciprocal(rsum16[:], rsum16[:]))
                p.op("dve", lambda e, sc3=sc3, s=s: e.tensor_tensor(
                    sc3, sc3, rsum16[:].unsqueeze(2).broadcast_to([128, NH, s + 1]),
                    ALU.mult))
                for t in range(s + 1):
                    tj = t * 2 + ic
                    kv = tok_tile(tj).rearrange("p (g b) -> p g b", b=384)
                    v_ap = kv[:, :, 320:384].unsqueeze(2).broadcast_to([128, NKV, 4, HD])
                    p.op("dve", lambda e, v_ap=v_ap: e.tensor_copy(
                        vexp[:].rearrange("p (g r d) -> p g r d", g=NKV, r=4), v_ap))
                    pb = scox[:].rearrange("p (h t) -> p h t", t=8)[:, :, t:t + 1]
                    pb = pb.broadcast_to([128, NH, HD])
                    if t == 0:
                        p.op("dve", lambda e, pb=pb: e.tensor_tensor(
                            aoacc[:].rearrange("p (h d) -> p h d", d=HD),
                            vexp[:].rearrange("p (h d) -> p h d", d=HD), pb, ALU.mult))
                    else:
                        p.op("dve", lambda e, pb=pb: e.tensor_tensor(
                            prod[:].rearrange("p (h d) -> p h d", d=HD),
                            vexp[:].rearrange("p (h d) -> p h d", d=HD), pb, ALU.mult))
                        p.op("dve", lambda e: e.tensor_add(aoacc[:], aoacc[:], prod[:]))
                p.op("dve", lambda e: e.tensor_copy(aobf[:], aoacc[:]))
                for j in range(DCH):
                    p.op("pe", lambda e, j=j: e.transpose(
                        ps_tp[:, :], aobf[:, j * 128:(j + 1) * 128], identb[:]))
                    p.op("dve", lambda e, j=j, ti=ti: e.tensor_copy(
                        aoT[:, j * TOK + ti * 128:j * TOK + ti * 128 + 128],
                        ps_tp[:, :]))

        for ec in range(DCH):
            wslot = ec % 2
            for k in range(DCH):
                p.dma("pool", lambda e, ec=ec, k=k, wslot=wslot: e.dma_start(
                    out=wstage[:, wslot * 1024 + k * 128:wslot * 1024 + (k + 1) * 128],
                    in_=proj_wT[k * 128:(k + 1) * 128, ec * 128:(ec + 1) * 128]))
            for tb in range(NTB):
                for k in range(DCH):
                    p.op("pe", lambda e, ec=ec, tb=tb, k=k, wslot=wslot: e.matmul(
                        ps_mm[:, :],
                        wstage[:, wslot * 1024 + k * 128:wslot * 1024 + (k + 1) * 128],
                        aoT[:, k * TOK + tb * TB:k * TOK + (tb + 1) * TB],
                        start=(k == 0), stop=(k == DCH - 1)))
                p.op("dve", lambda e, ec=ec, tb=tb: e.tensor_add(
                    xch[ec][:, tb * TB:(tb + 1) * TB],
                    xch[ec][:, tb * TB:(tb + 1) * TB], ps_mm[:, :]))
            p.dma("sync", lambda e, ec=ec: e.dma_start(
                out=o_h1[ec * 128:(ec + 1) * 128, :], in_=xch[ec]))

        emit_ln_rows(p, xch, scr2, ps_r1, ps_r2, onec[:], mrow[0:1, 0:TOK],
                     rrow[:], scrow[:])
        ln_apply(2 * DCH, 3 * DCH)
        for ti in range(16):
            for j in range(DCH):
                p.op("pe", lambda e, ti=ti, j=j: e.transpose(
                    ps_tp[:, :], lnch[j][:, ti * 128:(ti + 1) * 128], identb[:]))
                p.op("dve", lambda e, j=j: e.tensor_copy(
                    aobf[:, j * 128:(j + 1) * 128], ps_tp[:, :]))
            p.dma("sync", lambda e, ti=ti: e.dma_start(
                out=o_ln2tok[ti * 128:(ti + 1) * 128, :], in_=aobf[:]))

        for k in range(DCH):
            p.op("dve", lambda e, k=k: e.tensor_scalar_mul(
                rwp_sb[:, k * EXP:(k + 1) * EXP], rw_sb[:, k * EXP:(k + 1) * EXP],
                lncols[:, 2 * DCH + k:2 * DCH + k + 1]))
            p.op("dve", lambda e, k=k: e.tensor_scalar_mul(
                rwb_sb[:, k * EXP:(k + 1) * EXP], rw_sb[:, k * EXP:(k + 1) * EXP],
                lncols[:, 3 * DCH + k:3 * DCH + k + 1]))
        for k in range(DCH):
            p.op("pe", lambda e, k=k: e.matmul(
                ps_c[:, :], onec[:], rwp_sb[:, k * EXP:(k + 1) * EXP],
                start=(k == 0), stop=(k == DCH - 1)))
        p.op("act", lambda e: e.activation(c12[0:1, 0:EXP], ps_c[:, :], AF.Copy))
        for k in range(DCH):
            p.op("pe", lambda e, k=k: e.matmul(
                ps_c[:, :], onec[:], rwb_sb[:, k * EXP:(k + 1) * EXP],
                start=(k == 0), stop=(k == DCH - 1)))
        p.op("act", lambda e: e.activation(c12[0:1, EXP:2 * EXP], ps_c[:, :], AF.Copy))
        p.op("pe", lambda e: e.matmul(
            ps_bc[:, 0:2 * EXP], oner[:], c12[0:1, 0:2 * EXP], start=True, stop=True))
        p.op("dve", lambda e: e.tensor_copy(c12b[:], ps_bc[:, 0:2 * EXP]))
        for tb in range(NTB):
            for k in range(DCH):
                p.op("pe", lambda e, tb=tb, k=k: e.matmul(
                    ps_mm[0:EXP, 0:TB], rwp_sb[:, k * EXP:(k + 1) * EXP],
                    xch[k][:, tb * TB:(tb + 1) * TB],
                    start=(k == 0), stop=(k == DCH - 1)))
            p.op("act", lambda e, tb=tb: e.activation(
                logits[0:EXP, tb * TB:(tb + 1) * TB], ps_mm[0:EXP, 0:TB], AF.Copy))
        ms1 = prod[:, 0:EXP]
        ms2 = prod[:, EXP:2 * EXP]
        tmp8 = prod[:, 2 * EXP:3 * EXP]
        for s in range(8):
            for ic in range(2):
                ti = s * 2 + ic
                p.op("pe", lambda e, ti=ti: e.transpose(
                    ps_tp32[:, 0:EXP], logits[0:EXP, ti * 128:(ti + 1) * 128],
                    identf[0:EXP, 0:EXP]))
                p.op("dve", lambda e: e.tensor_copy(ptile[:], ps_tp32[:, 0:EXP]))
                p.op("pe", lambda e, ti=ti: e.transpose(
                    ps_tp32[:, 0:1], mrow[0:1, ti * 128:(ti + 1) * 128],
                    identf[0:1, 0:1]))
                p.op("dve", lambda e: e.tensor_copy(tcols[:, 4:5], ps_tp32[:, 0:1]))
                p.op("pe", lambda e, ti=ti: e.transpose(
                    ps_tp32[:, 0:1], rrow[0:1, ti * 128:(ti + 1) * 128],
                    identf[0:1, 0:1]))
                p.op("dve", lambda e: e.tensor_copy(tcols[:, 5:6], ps_tp32[:, 0:1]))
                p.op("dve", lambda e: e.tensor_scalar_mul(
                    tmp8, c12b[:, 0:EXP], tcols[:, 4:5]))
                p.op("dve", lambda e: e.tensor_sub(ptile[:], ptile[:], tmp8))
                p.op("dve", lambda e: e.tensor_scalar_mul(
                    ptile[:], ptile[:], tcols[:, 5:6]))
                p.op("dve", lambda e: e.tensor_add(ptile[:], ptile[:], c12b[:, EXP:2 * EXP]))
                p.op("dve", lambda e: e.tensor_reduce(
                    tcols[:, 0:1], ptile[:], AX.X, ALU.max))
                p.op("dve", lambda e: e.tensor_scalar_sub(
                    ptile[:], ptile[:], tcols[:, 0:1]))
                p.op("act", lambda e: e.activation(ptile[:], ptile[:], AF.Exp))
                p.op("dve", lambda e: e.tensor_reduce(
                    tcols[:, 1:2], ptile[:], AX.X, ALU.add))
                p.op("dve", lambda e: e.reciprocal(tcols[:, 1:2], tcols[:, 1:2]))
                p.op("dve", lambda e: e.tensor_scalar_mul(
                    ptile[:], ptile[:], tcols[:, 1:2]))
                p.op("dve", lambda e: e.tensor_reduce(
                    tcols[:, 0:1], ptile[:], AX.X, ALU.max))
                p.op("dve", lambda e: e.tensor_scalar(
                    ms1, ptile[:], tcols[:, 0:1], None, ALU.is_equal))
                p.op("dve", lambda e: e.tensor_mul(tmp8, ms1, iota8_sb[:]))
                p.op("dve", lambda e: e.tensor_reduce(
                    tcols[:, 2:3], tmp8, AX.X, ALU.add))
                p.op("dve", lambda e: e.tensor_scalar(
                    tmp8, ms1, -1.0, 1.0, ALU.mult, ALU.add))
                p.op("dve", lambda e: e.tensor_mul(tmp8, tmp8, ptile[:]))
                p.op("dve", lambda e: e.tensor_reduce(
                    tcols[:, 1:2], tmp8, AX.X, ALU.max))
                p.op("dve", lambda e: e.tensor_scalar(
                    ms2, tmp8, tcols[:, 1:2], None, ALU.is_equal))
                p.op("dve", lambda e: e.tensor_mul(tmp8, ms2, iota8_sb[:]))
                p.op("dve", lambda e: e.tensor_reduce(
                    tcols[:, 3:4], tmp8, AX.X, ALU.add))
                p.op("dve", lambda e: e.tensor_add(
                    tcols[:, 4:5], tcols[:, 0:1], tcols[:, 1:2]))
                p.op("dve", lambda e: e.reciprocal(tcols[:, 5:6], tcols[:, 4:5]))
                p.op("dve", lambda e: e.tensor_mul(
                    tcols[:, 6:7], tcols[:, 0:1], tcols[:, 5:6]))
                p.op("dve", lambda e: e.tensor_mul(
                    tcols[:, 7:8], tcols[:, 1:2], tcols[:, 5:6]))
                p.dma("sync", lambda e, ti=ti: e.dma_start(
                    out=o_topi[ti * 128:(ti + 1) * 128, :], in_=tcols[:, 2:4]))
                p.dma("sync", lambda e, ti=ti: e.dma_start(
                    out=o_topp[ti * 128:(ti + 1) * 128, :], in_=tcols[:, 6:8]))
                p.op("dve", lambda e: e.tensor_add(ms1, ms1, ms2))
                p.op("pe", lambda e, ic=ic: e.matmul(
                    ps_c[:, :], onec[:], ms1, start=(ic == 0), stop=(ic == 1)))
            p.op("act", lambda e, s=s: e.activation(
                cnt_row[0:1, s * EXP:(s + 1) * EXP], ps_c[:, :], AF.Copy))
        p.dma("sync", lambda e: e.dma_start(
            out=o_counts.rearrange("a b -> (a b)").unsqueeze(0), in_=cnt_row[:]))



def build_l0b1():
    nc = bass.Bass("TRN2", target_bir_lowering=False)
    io = {}
    io["xnT"] = nc.dram_tensor("xnT", [D, TOK], F32, kind="ExternalInput").ap()
    io["qkv_wT"] = nc.dram_tensor("qkv_wT", [D, QKV_O], F32, kind="ExternalInput").ap()
    io["proj_wT"] = nc.dram_tensor("proj_wT", [D, D], F32, kind="ExternalInput").ap()
    io["router_w"] = nc.dram_tensor("router_w", [D, EXP], F32, kind="ExternalInput").ap()
    io["ln1w"] = nc.dram_tensor("ln1w", [128, DCH], F32, kind="ExternalInput").ap()
    io["ln1b"] = nc.dram_tensor("ln1b", [128, DCH], F32, kind="ExternalInput").ap()
    io["ln2w"] = nc.dram_tensor("ln2w", [128, DCH], F32, kind="ExternalInput").ap()
    io["ln2b"] = nc.dram_tensor("ln2b", [128, DCH], F32, kind="ExternalInput").ap()
    io["ones_col"] = nc.dram_tensor("ones_col", [128, 1], F32, kind="ExternalInput").ap()
    io["ones_row"] = nc.dram_tensor("ones_row", [1, 128], F32, kind="ExternalInput").ap()
    io["ident_bf"] = nc.dram_tensor("ident_bf", [128, 128], BF16, kind="ExternalInput").ap()
    io["ident_f32"] = nc.dram_tensor("ident_f32", [128, 128], F32, kind="ExternalInput").ap()
    io["iota8"] = nc.dram_tensor("iota8", [128, EXP], F32, kind="ExternalInput").ap()
    io["o_h1"] = nc.dram_tensor("o_h1", [D, TOK], F32, kind="ExternalOutput").ap()
    io["o_ln2tok"] = nc.dram_tensor("o_ln2tok", [TOK, D], BF16, kind="ExternalOutput").ap()
    io["o_topi"] = nc.dram_tensor("o_topi", [TOK, 2], F32, kind="ExternalOutput").ap()
    io["o_topp"] = nc.dram_tensor("o_topp", [TOK, 2], F32, kind="ExternalOutput").ap()
    io["o_counts"] = nc.dram_tensor("o_counts", [EXP, EXP], F32, kind="ExternalOutput").ap()
    import contextlib as _cl
    with _cl.ExitStack() as _c:
        block = _c.enter_context(nc.Block())
        sem = [_c.enter_context(nc.semaphore(f"chain{i}")) for i in range(4)]
        p = Prog(nc)
        emit_l0b1(nc, p, io)
        p.run(block, sem)
    return nc


# version 07 (final)
# speedup vs baseline: 1.0780x; 1.0780x over previous
import contextlib
import math
import numpy as np
import concourse.bass as bass
import concourse.mybir as mybir
from concourse.bass_utils import run_bass_kernel_spmd

F32 = mybir.dt.float32
BF16 = mybir.dt.bfloat16
I32 = mybir.dt.int32
U8 = mybir.dt.uint8
AF = mybir.ActivationFunctionType
ALU = mybir.AluOpType
AX = mybir.AxisListType

NCORES = 8
B, L, D = 8, 2048, 1024
TOK = 2048
DCH = D // 128
TB = 512
NTB = TOK // TB

LN_EPS = 1e-5


class Prog:
    ENGINES = ("sync", "act", "dve", "pe", "pool")
    BLOCKFN = {"sync": "sync", "act": "scalar", "dve": "vector",
               "pe": "tensor", "pool": "gpsimd"}

    def __init__(self, nc):
        self.nc = nc
        self.ops = []
        self.count = 0

    def op(self, engine, fn, wait=None):
        w = self.count if wait is None else wait
        self.ops.append((engine, fn, 1, w))
        self.count += 1

    def dma(self, engine, fn, wait=None):
        w = self.count if wait is None else wait
        self.ops.append((engine, fn, 16, w))
        self.count += 16

    def mark(self):
        return self.count

    EPOCH = 40000

    def run(self, block, sems):
        if not isinstance(sems, (list, tuple)):
            sems = [sems]
        bounds = [0]
        cur = 0
        starts = []
        for (e, fn, inc, w) in self.ops:
            if cur + inc - bounds[-1] > self.EPOCH:
                bounds.append(cur)
            starts.append(cur)
            cur += inc
        nep = len(bounds)
        assert nep <= len(sems), f"need {nep} semaphores, got {len(sems)}"
        totals = [(bounds[i + 1] if i + 1 < nep else self.count) - bounds[i]
                  for i in range(nep)]

        def epoch_of(cnt):
            ei = 0
            while ei + 1 < nep and cnt >= bounds[ei + 1]:
                ei += 1
            return ei

        for engine_name in self.ENGINES:
            eng_ops = [(fn, inc, w, st) for (e, fn, inc, w), st
                       in zip(self.ops, starts) if e == engine_name]
            if not eng_ops and engine_name != "sync":
                continue

            def section(eng, eng_ops=eng_ops, engine_name=engine_name):
                fenced = -1
                last_w = {}
                for fn, inc, w, st in eng_ops:
                    ep = epoch_of(st)
                    epw = epoch_of(w)
                    for j in range(fenced + 1, epw):
                        eng.wait_ge(sems[j], totals[j])
                    fenced = max(fenced, epw - 1)
                    rel = w - bounds[epw]
                    if rel > 0 and rel > last_w.get(epw, -1):
                        eng.wait_ge(sems[epw], rel)
                        last_w[epw] = rel
                    fn(eng).then_inc(sems[ep], inc)
                if engine_name == "sync":
                    for j in range(nep):
                        if totals[j] > max(last_w.get(j, -1), 0) or j > fenced:
                            eng.wait_ge(sems[j], totals[j])

            getattr(block, self.BLOCKFN[engine_name])(section)


_SB_UID = [0]


class SB:
    def __init__(self, nc, ctx):
        self.nc = nc
        self.ctx = ctx

    def _u(self, name):
        _SB_UID[0] += 1
        return f"{name}_{_SB_UID[0]}"

    def t(self, name, shape, dtype):
        return self.ctx.enter_context(self.nc.sbuf_tensor(self._u(name), shape, dtype))

    def p(self, name, shape, dtype=F32):
        return self.ctx.enter_context(self.nc.psum_tensor(self._u(name), shape, dtype))



def emit_softplus(p, out, in_, scratch, limit=None):
    p.op("act", lambda e: e.activation(scratch, in_, AF.Abs))
    p.op("act", lambda e: e.activation(scratch, scratch, AF.Exp, scale=-1.0))
    p.op("act", lambda e: e.activation(scratch, scratch, AF.Ln, bias=1.0))
    p.op("dve", lambda e: e.tensor_scalar_max(out, in_, 0.0))
    p.op("dve", lambda e: e.tensor_add(out, out, scratch))
    if limit is not None:
        p.op("dve", lambda e: e.tensor_scalar_min(out, out, float(limit)))


def emit_ln_stats(p, x_chunks, sq, ps_s, ps_q, ones_col, mrow, rrow,
                  m_b, r_b, ones_row, ps_bc, scr_row, eps=LN_EPS, n=D,
                  ntb=NTB, tb_sz=TB):
    nch = len(x_chunks)
    for tb in range(ntb):
        sl = slice(tb * tb_sz, (tb + 1) * tb_sz)
        for k in range(nch):
            p.op("pe", lambda e, sl=sl, k=k: e.matmul(
                ps_s[:, :], ones_col, x_chunks[k][:, sl],
                start=(k == 0), stop=(k == nch - 1)))
        p.op("act", lambda e, sl=sl: e.activation(
            mrow[:, sl], ps_s[:, :], AF.Copy, scale=1.0 / n))
        for k in range(nch):
            p.op("act", lambda e, sl=sl, k=k: e.activation(
                sq[:, 0:tb_sz], x_chunks[k][:, sl], AF.Square))
            p.op("pe", lambda e, k=k: e.matmul(
                ps_q[:, :], ones_col, sq[:, 0:tb_sz],
                start=(k == 0), stop=(k == nch - 1)))
        p.op("act", lambda e, sl=sl: e.activation(
            rrow[:, sl], ps_q[:, :], AF.Copy, scale=1.0 / n))
    p.op("dve", lambda e: e.tensor_tensor(scr_row, mrow, mrow, ALU.mult))
    p.op("dve", lambda e: e.tensor_sub(rrow, rrow, scr_row))
    p.op("dve", lambda e: e.tensor_scalar_add(rrow, rrow, float(eps)))
    p.op("act", lambda e: e.activation(rrow, rrow, AF.Sqrt))
    p.op("dve", lambda e: e.reciprocal(rrow, rrow))
    for tb in range(ntb):
        sl = slice(tb * tb_sz, (tb + 1) * tb_sz)
        p.op("pe", lambda e, sl=sl: e.matmul(
            ps_bc[:, :], ones_row, mrow[:, sl], start=True, stop=True))
        p.op("dve", lambda e, sl=sl: e.tensor_copy(m_b[:, sl], ps_bc[:, :]))
        p.op("pe", lambda e, sl=sl: e.matmul(
            ps_bc[:, :], ones_row, rrow[:, sl], start=True, stop=True))
        p.op("dve", lambda e, sl=sl: e.tensor_copy(r_b[:, sl], ps_bc[:, :]))


def emit_ln_rows(p, x_chunks, sq, ps_s, ps_q, ones_col, mrow, rrow, scr_row,
                 eps=LN_EPS, n=D, ntb=NTB, tb_sz=TB):
    nch = len(x_chunks)
    for tb in range(ntb):
        sl = slice(tb * tb_sz, (tb + 1) * tb_sz)
        for k in range(nch):
            p.op("pe", lambda e, sl=sl, k=k: e.matmul(
                ps_s[:, :], ones_col, x_chunks[k][:, sl],
                start=(k == 0), stop=(k == nch - 1)))
        p.op("act", lambda e, sl=sl: e.activation(
            mrow[:, sl], ps_s[:, :], AF.Copy, scale=1.0 / n))
        for k in range(nch):
            p.op("act", lambda e, sl=sl, k=k: e.activation(
                sq[:, 0:tb_sz], x_chunks[k][:, sl], AF.Square))
            p.op("pe", lambda e, k=k: e.matmul(
                ps_q[:, :], ones_col, sq[:, 0:tb_sz],
                start=(k == 0), stop=(k == nch - 1)))
        p.op("act", lambda e, sl=sl: e.activation(
            rrow[:, sl], ps_q[:, :], AF.Copy, scale=1.0 / n))
    p.op("dve", lambda e: e.tensor_tensor(scr_row, mrow, mrow, ALU.mult))
    p.op("dve", lambda e: e.tensor_sub(rrow, rrow, scr_row))
    p.op("dve", lambda e: e.tensor_scalar_add(rrow, rrow, float(eps)))
    p.op("act", lambda e: e.activation(rrow, rrow, AF.Sqrt))
    p.op("dve", lambda e: e.reciprocal(rrow, rrow))



def emit_l0a(nc, p, io):
    ECH = 16

    xT = io["xT"]
    w_inT = io["w_inT"]
    w_stateT = io["w_stateT"]
    w_dtT = io["w_dtT"]
    b_dt = io["b_dt"]
    a_log = io["a_log"]
    w_cT = io["w_cT"]
    dvec = io["dvec"]
    w_outT = io["w_outT"]
    ones_col = io["ones_col"]
    ones_row = io["ones_row"]

    o_xn = io["o_xn"]
    o_s0 = io["o_s0"]

    w0 = p.mark()
    with contextlib.ExitStack() as ctx:
        sb = SB(nc, ctx)
        xt_h = ctx.enter_context(nc.sbuf_tensor("xt_f", [128, DCH * TOK], F32))
        xt = xt_h.ap()
        ug_h = xt_h.bitcast(BF16)
        ug = ug_h.ap()
        xnbf = sb.t("xnbf", [128, DCH * TOK], BF16)
        w_in_sb = sb.t("w_in_sb", [128, DCH * 2048], BF16)
        w_out_sb = sb.t("w_out_sb", [128, DCH * D], BF16)
        w_state_sb = sb.t("w_state_sb", [128, DCH * 128], BF16)
        w_dt_sb = sb.t("w_dt_sb", [128, DCH * 128], BF16)
        w_c_sb = sb.t("w_c_sb", [128, D], BF16)
        onec = sb.t("onec", [128, 1], F32)
        oner = sb.t("oner", [1, 128], F32)
        rrow = sb.t("rrow", [1, TOK], F32)
        scr1 = sb.t("scr1", [128, TOK], F32)
        scr2 = sb.t("scr2", [128, TOK], F32)
        b_t32 = sb.t("b_t32", [128, TOK], F32)
        dt32 = sb.t("dt32", [128, TOK], F32)
        decay = sb.t("decay", [128, TOK], F32)
        hsbf = sb.t("hsbf", [128, TOK], BF16)
        cols = sb.t("cols", [128, 4], F32)
        dcol = sb.t("dcol", [128, DCH], F32)
        ps_s = sb.p("ps_s", [1, TB], F32)
        ps_q = sb.p("ps_q", [1, TB], F32)
        ps_bc = sb.p("ps_bc", [128, TB], F32)
        ps_a = sb.p("ps_a", [128, TB], F32)
        ps_b = sb.p("ps_b", [128, TB], F32)


        xch = [xt[:, k * TOK:(k + 1) * TOK] for k in range(DCH)]
        m_b, r_b = b_t32, dt32
        mrow_t = decay
        mrow = mrow_t
        sout = scr2

        for k in range(DCH):
            p.dma("sync", lambda e, k=k: e.dma_start(
                out=xch[k], in_=xT[k * 128:(k + 1) * 128, :]), wait=w0)
        p.dma("sync", lambda e: e.dma_start(out=onec[:], in_=ones_col), wait=w0)
        p.dma("sync", lambda e: e.dma_start(out=oner[:], in_=ones_row), wait=w0)
        p.dma("sync", lambda e: e.dma_start(out=cols[:, 0:1], in_=b_dt), wait=w0)
        p.dma("sync", lambda e: e.dma_start(out=cols[:, 1:2], in_=a_log), wait=w0)
        p.dma("sync", lambda e: e.dma_start(out=dcol[:], in_=dvec), wait=w0)
        for k in range(DCH):
            p.dma("pool", lambda e, k=k: e.dma_start(
                out=w_in_sb[:, k * 2048:(k + 1) * 2048],
                in_=w_inT[k * 128:(k + 1) * 128, :]), wait=w0)
            p.dma("pool", lambda e, k=k: e.dma_start(
                out=w_out_sb[:, k * D:(k + 1) * D],
                in_=w_outT[k * 128:(k + 1) * 128, :]), wait=w0)
            p.dma("pool", lambda e, k=k: e.dma_start(
                out=w_state_sb[:, k * 128:(k + 1) * 128],
                in_=w_stateT[k * 128:(k + 1) * 128, :]), wait=w0)
            p.dma("pool", lambda e, k=k: e.dma_start(
                out=w_dt_sb[:, k * 128:(k + 1) * 128],
                in_=w_dtT[k * 128:(k + 1) * 128, :]), wait=w0)
        p.dma("pool", lambda e: e.dma_start(out=w_c_sb[:], in_=w_cT), wait=w0)

        emit_ln_stats(p, xch, scr2, ps_s, ps_q, onec[:], mrow[0:1, 0:TOK], rrow[:],
                      m_b[:], r_b[:], oner[:], ps_bc, scr1[0:1, 0:TOK])
        for k in range(DCH):
            p.op("dve", lambda e, k=k: e.tensor_sub(xch[k], xch[k], m_b[:]))
            p.op("dve", lambda e, k=k: e.tensor_mul(xch[k], xch[k], r_b[:]))
            p.op("dve", lambda e, k=k: e.tensor_copy(
                xnbf[:, k * TOK:(k + 1) * TOK], xch[k]))
            p.dma("sync", lambda e, k=k: e.dma_start(
                out=o_xn[k * 128:(k + 1) * 128, :], in_=xch[k]))

        xnch = [xnbf[:, k * TOK:(k + 1) * TOK] for k in range(DCH)]

        for ec in range(ECH):
            for tb in range(NTB):
                for k in range(DCH):
                    p.op("pe", lambda e, ec=ec, tb=tb, k=k: e.matmul(
                        ps_a[:, :],
                        w_in_sb[:, k * 2048 + ec * 128:k * 2048 + (ec + 1) * 128],
                        xnch[k][:, tb * TB:(tb + 1) * TB],
                        start=(k == 0), stop=(k == DCH - 1)))
                p.op("act", lambda e, ec=ec, tb=tb: e.activation(
                    ug[:, ec * TOK + tb * TB:ec * TOK + (tb + 1) * TB],
                    ps_a[:, :], AF.Copy))

        uch = [ug[:, k * TOK:(k + 1) * TOK] for k in range(DCH)]
        gch = [ug[:, (DCH + k) * TOK:(DCH + k + 1) * TOK] for k in range(DCH)]

        for tb in range(NTB):
            sl = slice(tb * TB, (tb + 1) * TB)
            for k in range(DCH):
                p.op("pe", lambda e, sl=sl, k=k: e.matmul(
                    ps_a[:, :], w_state_sb[:, k * 128:(k + 1) * 128],
                    uch[k][:, sl], start=(k == 0), stop=(k == DCH - 1)))
            p.op("act", lambda e, sl=sl: e.activation(
                b_t32[:, sl], ps_a[:, :], AF.Copy))
            for k in range(DCH):
                p.op("pe", lambda e, sl=sl, k=k: e.matmul(
                    ps_b[:, :], w_dt_sb[:, k * 128:(k + 1) * 128],
                    uch[k][:, sl], start=(k == 0), stop=(k == DCH - 1)))
            p.op("act", lambda e, sl=sl: e.activation(
                dt32[:, sl], ps_b[:, :], AF.Identity, bias=cols[:, 0:1]))

        emit_softplus(p, dt32[:], dt32[:], scr1[:], limit=10.0)
        emit_softplus(p, cols[:, 2:3], cols[:, 1:2], cols[:, 3:4])
        p.op("dve", lambda e: e.tensor_scalar_add(cols[:, 2:3], cols[:, 2:3], 1e-4))
        p.op("dve", lambda e: e.tensor_scalar_mul(scr1[:], dt32[:], cols[:, 2:3]))
        p.op("act", lambda e: e.activation(decay[:], scr1[:], AF.Exp, scale=-1.0))
        p.op("dve", lambda e: e.tensor_scalar(scr1[:], decay[:], -1.0, 1.0,
                                              ALU.mult, ALU.add))
        p.op("dve", lambda e: e.tensor_mul(scr1[:], scr1[:], b_t32[:]))
        p.op("dve", lambda e: e.tensor_tensor_scan(
            scr2[:], decay[:], scr1[:], 0.0, ALU.mult, ALU.add))
        p.op("dve", lambda e: e.tensor_copy(hsbf[:], scr2[:]))

        zch = uch
        for k in range(DCH):
            for tb in range(NTB):
                sl = slice(tb * TB, (tb + 1) * TB)
                p.op("pe", lambda e, k=k, sl=sl: e.matmul(
                    ps_a[:, :], w_c_sb[:, k * 128:(k + 1) * 128],
                    hsbf[:, sl], start=True, stop=True))
                p.op("act", lambda e, k=k, sl=sl: e.activation(
                    scr1[:, sl], gch[k][:, sl], AF.Silu))
                p.op("dve", lambda e, k=k, sl=sl: e.tensor_mul(
                    scr2[:, sl], ps_a[:, :], scr1[:, sl]))
            p.op("dve", lambda e, k=k: e.tensor_scalar(
                b_t32[:], xnch[k], dcol[:, k:k + 1], None, ALU.mult))
            p.op("dve", lambda e, k=k: e.tensor_add(scr2[:], scr2[:], b_t32[:]))
            p.op("dve", lambda e, k=k: e.tensor_copy(zch[k], scr2[:]))

        for ec in range(DCH):
            for tb in range(NTB):
                sl = slice(tb * TB, (tb + 1) * TB)
                for k in range(DCH):
                    p.op("pe", lambda e, ec=ec, sl=sl, k=k: e.matmul(
                        ps_b[:, :],
                        w_out_sb[:, k * D + ec * 128:k * D + (ec + 1) * 128],
                        zch[k][:, sl], start=(k == 0), stop=(k == DCH - 1)))
                p.op("act", lambda e, sl=sl: e.activation(
                    sout[:, sl], ps_b[:, :], AF.Copy))
            p.dma("sync", lambda e, ec=ec: e.dma_start(
                out=o_s0[ec * 128:(ec + 1) * 128, :], in_=sout[:]))



def build_l0a():
    nc = bass.Bass("TRN2", target_bir_lowering=False)
    io = {}
    io["xT"] = nc.dram_tensor("xT", [D, TOK], F32, kind="ExternalInput").ap()
    io["w_inT"] = nc.dram_tensor("w_inT", [D, 2048], F32, kind="ExternalInput").ap()
    io["w_stateT"] = nc.dram_tensor("w_stateT", [D, 128], F32, kind="ExternalInput").ap()
    io["w_dtT"] = nc.dram_tensor("w_dtT", [D, 128], F32, kind="ExternalInput").ap()
    io["b_dt"] = nc.dram_tensor("b_dt", [128, 1], F32, kind="ExternalInput").ap()
    io["a_log"] = nc.dram_tensor("a_log", [128, 1], F32, kind="ExternalInput").ap()
    io["w_cT"] = nc.dram_tensor("w_cT", [128, D], F32, kind="ExternalInput").ap()
    io["dvec"] = nc.dram_tensor("dvec", [128, DCH], F32, kind="ExternalInput").ap()
    io["w_outT"] = nc.dram_tensor("w_outT", [D, D], F32, kind="ExternalInput").ap()
    io["ones_col"] = nc.dram_tensor("ones_col", [128, 1], F32, kind="ExternalInput").ap()
    io["ones_row"] = nc.dram_tensor("ones_row", [1, 128], F32, kind="ExternalInput").ap()
    io["o_xn"] = nc.dram_tensor("o_xn", [D, TOK], F32, kind="ExternalOutput").ap()
    io["o_s0"] = nc.dram_tensor("o_s0", [D, TOK], F32, kind="ExternalOutput").ap()
    import contextlib as _cl
    with _cl.ExitStack() as _c:
        block = _c.enter_context(nc.Block())
        sem = [_c.enter_context(nc.semaphore(f"chain{i}")) for i in range(4)]
        p = Prog(nc)
        emit_l0a(nc, p, io)
        p.run(block, sem)
    return nc


def _prep_l0a_inputs(inputs):
    c = np.ascontiguousarray
    w = {}
    w["w_inT"] = c(inputs["L0_attn_W_in"].T)
    w["w_stateT"] = c(inputs["L0_attn_W_state"].T)
    w["w_dtT"] = c(inputs["L0_attn_W_dt"].T)
    w["b_dt"] = c(np.asarray(inputs["L0_attn_b_dt"], np.float32).reshape(128, 1))
    w["a_log"] = c(np.asarray(inputs["L0_attn_A_log"], np.float32).reshape(128, 1))
    w["w_cT"] = c(inputs["L0_attn_W_C"].T)
    w["dvec"] = c(np.asarray(inputs["L0_attn_D"], np.float32).reshape(DCH, 128).T)
    w["w_outT"] = c(inputs["L0_attn_W_out"].T)
    w["ones_col"] = np.ones((128, 1), np.float32)
    w["ones_row"] = np.ones((1, 128), np.float32)
    return {k: np.asarray(v, np.float32) for k, v in w.items()}


def run_l0a(inputs):
    x = np.asarray(inputs["L0_attn_x"], np.float32)
    w = _prep_l0a_inputs(inputs)
    nc = build_l0a()
    in_maps = []
    for b in range(NCORES):
        m = dict(w)
        m["xT"] = np.ascontiguousarray(x[b].T)
        in_maps.append(m)
    res = run_bass_kernel_spmd(nc, in_maps, core_ids=list(range(NCORES)))
    xn = np.stack([r["o_xn"] for r in res.results])
    s0 = np.stack([r["o_s0"] for r in res.results])
    return xn, s0



LCAP = 1536
NGRP = LCAP // 512
TRASH_IDX = 8 * LCAP
TRASH_PARTS = TOK * 2


def emit_l0b2(nc, p, io):

    ln2tok = io["ln2tok"]
    h1T = io["h1T"]
    topi = io["topi"]
    topp = io["topp"]
    counts_all = io["counts_all"]
    sel_g = io["sel_g"]
    sel_l = io["sel_l"]
    moe_w1 = io["moe_w1"]
    moe_w2 = io["moe_w2"]
    ones_row8 = io["ones_row8"]
    eiota = io["eiota"]
    tokhalf = io["tokhalf"]
    jiota = io["jiota"]
    ident_bf = io["ident_bf"]
    ident_f32 = io["ident_f32"]

    o_g = io["o_g"]

    ln2pad = nc.dram_tensor("b2_ln2pad", [TOK + 1, D], BF16).ap()
    idxbuf = nc.dram_tensor("b2_idxbuf", [8 * LCAP + 1, 4], F32).ap()
    parts = nc.dram_tensor("b2_parts", [2 * TOK + 1, D], F32).ap()

    w0 = p.mark()
    with contextlib.ExitStack() as ctx:
        sb = SB(nc, ctx)
        wspace_h = ctx.enter_context(
            nc.sbuf_tensor("wspace_f", [128, 2 * DCH * 2048], BF16))
        w1e = wspace_h.ap()[:, 0:DCH * 2048]
        w2e = wspace_h.ap()[:, DCH * 2048:2 * DCH * 2048]
        h1e = sb.t("h1e", [128, 16 * 512], BF16)
        xg = sb.t("xg", [128, D], BF16)
        xgT = sb.t("xgT", [128, DCH * 512], BF16)
        y2c = sb.t("y2c", [128, 512], F32)
        ytok = sb.t("ytok", [128, 4 * D], F32)
        moT = wspace_h.bitcast(F32).ap()
        zero_t = sb.t("zero_t", [128, D], F32)
        h1st = sb.t("h1st", [128, TOK], F32)
        oh = sb.t("oh", [EXP, 512], F32)
        sc8 = sb.t("sc8", [EXP, 512], F32)
        incl = sb.t("incl", [EXP, 512], F32)
        zero8 = sb.t("zero8", [EXP, 512], F32)
        arow = sb.t("arow", [1, 512], F32)
        prow_r = sb.t("prow_r", [1, 512], F32)
        lrank = sb.t("lrank", [1, 512], F32)
        grow = sb.t("grow", [1, 512], F32)
        lloc = sb.t("lloc", [1, 512], F32)
        erow = sb.t("erow", [1, 512], F32)
        dstrow = sb.t("dstrow", [1, 512], F32)
        vrow = sb.t("vrow", [1, 512], F32)
        tokrow = sb.t("tokrow", [1, 512], F32)
        outrow = sb.t("outrow", [1, 512], F32)
        offsT = sb.t("offsT", [EXP, EXP], F32)
        loffsT = sb.t("loffsT", [EXP, EXP], F32)
        csb = sb.t("csb", [64, EXP], F32)
        selg_sb = sb.t("selg_sb", [64, EXP], F32)
        sell_sb = sb.t("sell_sb", [64, EXP], F32)
        onesr8 = sb.t("onesr8", [1, EXP], F32)
        eic = sb.t("eic", [EXP, 1], F32)
        one8c = sb.t("one8c", [EXP, 1], F32)
        thalf = sb.t("thalf", [1, 512], F32)
        jio = sb.t("jio", [1, 512], F32)
        identb = sb.t("identb", [128, 128], BF16)
        identf = sb.t("identf", [128, 128], F32)
        paycol = sb.t("paycol", [128, 4], F32)
        dsti = sb.t("dsti", [128, 1], I32)
        gcol = sb.t("gcol", [128, 4], F32)
        gidx = sb.t("gidx", [128, 1], I32)
        pidx = sb.t("pidx", [128, 1], I32)
        ps_mm = sb.p("ps_mm", [128, TB], F32)
        ps_tp = sb.p("ps_tp", [128, 128], BF16)
        ps_tp32 = sb.p("ps_tp32", [128, 128], F32)
        ps_row = sb.p("ps_row", [1, 512], F32)
        ps_off = sb.p("ps_off", [EXP, EXP], F32)


        p.dma("sync", lambda e: e.dma_start(out=csb[:], in_=counts_all), wait=w0)
        p.dma("sync", lambda e: e.dma_start(out=selg_sb[:], in_=sel_g), wait=w0)
        p.dma("sync", lambda e: e.dma_start(out=sell_sb[:], in_=sel_l), wait=w0)
        p.dma("sync", lambda e: e.dma_start(out=onesr8[:], in_=ones_row8), wait=w0)
        p.dma("sync", lambda e: e.dma_start(out=eic[:], in_=eiota), wait=w0)
        p.dma("sync", lambda e: e.dma_start(out=thalf[:], in_=tokhalf), wait=w0)
        p.dma("sync", lambda e: e.dma_start(out=jio[:], in_=jiota), wait=w0)
        p.dma("sync", lambda e: e.dma_start(out=identb[:], in_=ident_bf), wait=w0)
        p.dma("sync", lambda e: e.dma_start(out=identf[:], in_=ident_f32), wait=w0)
        p.op("dve", lambda e: e.memset(zero8[:], 0.0), wait=w0)
        p.op("dve", lambda e: e.memset(zero_t[:], 0.0), wait=w0)
        p.op("dve", lambda e: e.memset(one8c[:], 1.0), wait=w0)
        p.dma("sync", lambda e: e.dma_start(out=ln2pad[0:TOK, :], in_=ln2tok), wait=w0)
        zt = p.mark()
        p.dma("sync", lambda e: e.dma_start(
            out=ln2pad[TOK:TOK + 1, :], in_=zero_t[0:1, :].bitcast(BF16)[:, 0:D]),
            wait=zt)
        for r in range(32):
            p.dma("sync", lambda e, r=r: e.dma_start(
                out=parts[r * 128:(r + 1) * 128, :], in_=zero_t[:]), wait=zt)
        p.dma("sync", lambda e: e.dma_start(
            out=parts[2 * TOK:2 * TOK + 1, :], in_=zero_t[0:1, :]), wait=zt)
        p.op("dve", lambda e: e.memset(paycol[:], 0.0), wait=zt)
        p.op("dve", lambda e: e.memset(paycol[:, 0:1], float(TOK)))
        p.op("dve", lambda e: e.memset(paycol[:, 2:3], float(TRASH_PARTS)))
        ii = p.mark()
        for r in range(8 * LCAP // 128 + 1):
            n = 128 if r < 8 * LCAP // 128 else 1
            p.dma("sync", lambda e, r=r, n=n: e.dma_start(
                out=idxbuf[r * 128:r * 128 + n, :], in_=paycol[0:n, :]), wait=ii)

        p.op("pe", lambda e: e.matmul(ps_off[:, :], csb[:], selg_sb[:],
                                      start=True, stop=True))
        p.op("act", lambda e: e.activation(offsT[:], ps_off[:, :], AF.Copy))
        p.op("pe", lambda e: e.matmul(ps_off[:, :], csb[:], sell_sb[:],
                                      start=True, stop=True))
        p.op("act", lambda e: e.activation(loffsT[:], ps_off[:, :], AF.Copy))

        tflat = topi.rearrange("a b -> (a b)").unsqueeze(0)
        pflat = topp.rearrange("a b -> (a b)").unsqueeze(0)
        ps8 = ps_mm[0:EXP, 0:512]
        for s in range(8):
            p.dma("sync", lambda e, s=s: e.dma_start(
                out=arow[:], in_=tflat[:, s * 512:(s + 1) * 512]))
            p.dma("sync", lambda e, s=s: e.dma_start(
                out=prow_r[:], in_=pflat[:, s * 512:(s + 1) * 512]))
            p.op("pe", lambda e: e.matmul(ps8, onesr8[:], arow[:],
                                          start=True, stop=True))
            p.op("dve", lambda e: e.tensor_scalar(
                oh[:], ps8, eic[:], None, ALU.is_equal))
            p.op("dve", lambda e: e.tensor_scalar(
                sc8[:], zero8[:], 0.0, 1.0, ALU.mult, ALU.add))
            p.op("dve", lambda e: e.tensor_tensor_scan(
                incl[:], sc8[:], oh[:], 0.0, ALU.mult, ALU.add))
            p.op("dve", lambda e: e.tensor_sub(incl[:], incl[:], oh[:]))
            p.op("dve", lambda e: e.tensor_mul(sc8[:], oh[:], incl[:]))
            p.op("pe", lambda e: e.matmul(ps_row[:, :], one8c[:], sc8[:],
                                          start=True, stop=True))
            p.op("act", lambda e: e.activation(lrank[:], ps_row[:, :], AF.Copy))
            for colv, dest in ((offsT, grow), (loffsT, lloc)):
                p.op("dve", lambda e, colv=colv, s=s: e.tensor_scalar_mul(
                    sc8[:], oh[:], colv[:, s:s + 1]))
                p.op("pe", lambda e: e.matmul(ps_row[:, :], one8c[:], sc8[:],
                                              start=True, stop=True))
                p.op("act", lambda e, dest=dest: e.activation(
                    dest[:], ps_row[:, :], AF.Copy))
            p.op("dve", lambda e: e.tensor_scalar_mul(sc8[:], oh[:], eic[:]))
            p.op("pe", lambda e: e.matmul(ps_row[:, :], one8c[:], sc8[:],
                                          start=True, stop=True))
            p.op("act", lambda e: e.activation(erow[:], ps_row[:, :], AF.Copy))
            p.op("dve", lambda e: e.tensor_add(grow[:], grow[:], lrank[:]))
            p.op("dve", lambda e: e.tensor_scalar(
                vrow[:], grow[:], float(CAP), None, ALU.is_lt))
            p.op("dve", lambda e: e.tensor_add(lloc[:], lloc[:], lrank[:]))
            p.op("dve", lambda e: e.tensor_scalar(
                sc8[0:1, :], lloc[:], float(LCAP), None, ALU.is_lt))
            p.op("dve", lambda e: e.tensor_mul(vrow[:], vrow[:], sc8[0:1, :]))
            p.op("dve", lambda e: e.tensor_scalar_mul(dstrow[:], erow[:], float(LCAP)))
            p.op("dve", lambda e: e.tensor_add(dstrow[:], dstrow[:], lloc[:]))
            p.op("dve", lambda e: e.tensor_scalar_add(
                dstrow[:], dstrow[:], -float(TRASH_IDX)))
            p.op("dve", lambda e: e.tensor_mul(dstrow[:], dstrow[:], vrow[:]))
            p.op("dve", lambda e: e.tensor_scalar_add(
                dstrow[:], dstrow[:], float(TRASH_IDX)))
            p.op("dve", lambda e, s=s: e.tensor_scalar_add(
                tokrow[:], thalf[:], float(s * 256)))
            p.op("dve", lambda e, s=s: e.tensor_scalar_add(
                outrow[:], jio[:], float(s * 512)))
            for ch in range(4):
                sl = slice(ch * 128, (ch + 1) * 128)
                for row, col in ((tokrow, 0), (prow_r, 1), (outrow, 2),
                                 (dstrow, 3)):
                    p.op("pe", lambda e, row=row, sl=sl: e.transpose(
                        ps_tp32[:, 0:1], row[:, sl], identf[0:1, 0:1]))
                    p.op("dve", lambda e, col=col: e.tensor_copy(
                        paycol[:, col:col + 1], ps_tp32[:, 0:1]))
                p.op("dve", lambda e: e.tensor_copy(dsti[:], paycol[:, 3:4]))
                p.dma("pool", lambda e: e.indirect_dma_start(
                    idxbuf, bass.IndirectOffsetOnAxis(dsti[:], 0),
                    paycol[:, 0:4], None))

        for ex in range(EXP):
            for k in range(DCH):
                p.dma("pool", lambda e, ex=ex, k=k: e.dma_start(
                    out=w1e[:, k * 2048:(k + 1) * 2048],
                    in_=moe_w1[ex, k * 128:(k + 1) * 128, :]))
            for k in range(16):
                p.dma("pool", lambda e, ex=ex, k=k: e.dma_start(
                    out=w2e[:, k * D:(k + 1) * D],
                    in_=moe_w2[ex, k * 128:(k + 1) * 128, :]))
            for g in range(NGRP):
                base = ex * LCAP + g * 512
                for sub in range(4):
                    p.dma("sync", lambda e, base=base, sub=sub: e.dma_start(
                        out=gcol[:, :], in_=idxbuf[base + sub * 128:
                                                   base + (sub + 1) * 128, :]))
                    p.op("dve", lambda e: e.tensor_copy(gidx[:], gcol[:, 0:1]))
                    p.dma("pool", lambda e: e.indirect_dma_start(
                        xg[:], None, ln2pad, bass.IndirectOffsetOnAxis(gidx[:], 0)))
                    for k in range(DCH):
                        p.op("pe", lambda e, k=k: e.transpose(
                            ps_tp[:, :], xg[:, k * 128:(k + 1) * 128], identb[:]))
                        p.op("dve", lambda e, k=k, sub=sub: e.tensor_copy(
                            xgT[:, k * 512 + sub * 128:k * 512 + sub * 128 + 128],
                            ps_tp[:, :]))
                for fc in range(16):
                    for k in range(DCH):
                        p.op("pe", lambda e, fc=fc, k=k: e.matmul(
                            ps_mm[:, :],
                            w1e[:, k * 2048 + fc * 128:k * 2048 + (fc + 1) * 128],
                            xgT[:, k * 512:(k + 1) * 512],
                            start=(k == 0), stop=(k == DCH - 1)))
                    p.op("act", lambda e, fc=fc: e.activation(
                        h1e[:, fc * 512:(fc + 1) * 512], ps_mm[:, :],
                        AF.Gelu_apprx_tanh))
                for hc in range(DCH):
                    for fc in range(16):
                        p.op("pe", lambda e, hc=hc, fc=fc: e.matmul(
                            ps_mm[:, :],
                            w2e[:, fc * D + hc * 128:fc * D + (hc + 1) * 128],
                            h1e[:, fc * 512:(fc + 1) * 512],
                            start=(fc == 0), stop=(fc == 15)))
                    p.op("act", lambda e: e.activation(y2c[:], ps_mm[:, :], AF.Copy))
                    for sub in range(4):
                        p.op("pe", lambda e, sub=sub: e.transpose(
                            ps_tp32[:, :], y2c[:, sub * 128:(sub + 1) * 128],
                            identf[:]))
                        p.op("dve", lambda e, sub=sub, hc=hc: e.tensor_copy(
                            ytok[:, sub * D + hc * 128:sub * D + (hc + 1) * 128],
                            ps_tp32[:, :]))
                for sub in range(4):
                    p.dma("sync", lambda e, base=base, sub=sub: e.dma_start(
                        out=gcol[:, :], in_=idxbuf[base + sub * 128:
                                                   base + (sub + 1) * 128, :]))
                    p.op("dve", lambda e, sub=sub: e.tensor_scalar_mul(
                        ytok[:, sub * D:(sub + 1) * D],
                        ytok[:, sub * D:(sub + 1) * D], gcol[:, 1:2]))
                    p.op("dve", lambda e: e.tensor_copy(pidx[:], gcol[:, 2:3]))
                    p.dma("pool", lambda e, sub=sub: e.indirect_dma_start(
                        parts, bass.IndirectOffsetOnAxis(pidx[:], 0),
                        ytok[:, sub * D:(sub + 1) * D], None))

        for ti in range(16):
            pview = parts[0:2 * TOK, :].rearrange("(t two) d -> t two d", two=2)
            p.dma("sync", lambda e, ti=ti, pview=pview: e.dma_start(
                out=ytok[:, 0:D], in_=pview[ti * 128:(ti + 1) * 128, 0, :]))
            p.dma("sync", lambda e, ti=ti, pview=pview: e.dma_start(
                out=ytok[:, D:2 * D], in_=pview[ti * 128:(ti + 1) * 128, 1, :]))
            p.op("dve", lambda e: e.tensor_add(
                ytok[:, 0:D], ytok[:, 0:D], ytok[:, D:2 * D]))
            for k in range(DCH):
                p.op("pe", lambda e, k=k: e.transpose(
                    ps_tp32[:, :], ytok[:, k * 128:(k + 1) * 128], identf[:]))
                p.op("dve", lambda e, k=k, ti=ti: e.tensor_copy(
                    moT[:, k * TOK + ti * 128:k * TOK + ti * 128 + 128],
                    ps_tp32[:, :]))
        for k in range(DCH):
            p.dma("sync", lambda e, k=k: e.dma_start(
                out=h1st[:], in_=h1T[k * 128:(k + 1) * 128, :]))
            p.op("dve", lambda e, k=k: e.tensor_add(
                h1st[:], h1st[:], moT[:, k * TOK:(k + 1) * TOK]))
            p.op("act", lambda e: e.activation(h1st[:], h1st[:], AF.Silu))
            p.dma("sync", lambda e, k=k: e.dma_start(
                out=o_g[k * 128:(k + 1) * 128, :], in_=h1st[:]))



def build_l0b2():
    nc = bass.Bass("TRN2", target_bir_lowering=False)
    io = {}
    io["ln2tok"] = nc.dram_tensor("ln2tok", [TOK, D], BF16, kind="ExternalInput").ap()
    io["h1T"] = nc.dram_tensor("h1T", [D, TOK], F32, kind="ExternalInput").ap()
    io["topi"] = nc.dram_tensor("topi", [TOK, 2], F32, kind="ExternalInput").ap()
    io["topp"] = nc.dram_tensor("topp", [TOK, 2], F32, kind="ExternalInput").ap()
    io["counts_all"] = nc.dram_tensor("counts_all", [64, EXP], F32, kind="ExternalInput").ap()
    io["sel_g"] = nc.dram_tensor("sel_g", [64, EXP], F32, kind="ExternalInput").ap()
    io["sel_l"] = nc.dram_tensor("sel_l", [64, EXP], F32, kind="ExternalInput").ap()
    io["moe_w1"] = nc.dram_tensor("moe_w1", [EXP, D, 2048], F32, kind="ExternalInput").ap()
    io["moe_w2"] = nc.dram_tensor("moe_w2", [EXP, 2048, D], F32, kind="ExternalInput").ap()
    io["ones_row8"] = nc.dram_tensor("ones_row8", [1, EXP], F32, kind="ExternalInput").ap()
    io["eiota"] = nc.dram_tensor("eiota", [EXP, 1], F32, kind="ExternalInput").ap()
    io["tokhalf"] = nc.dram_tensor("tokhalf", [1, 512], F32, kind="ExternalInput").ap()
    io["jiota"] = nc.dram_tensor("jiota", [1, 512], F32, kind="ExternalInput").ap()
    io["ident_bf"] = nc.dram_tensor("ident_bf", [128, 128], BF16, kind="ExternalInput").ap()
    io["ident_f32"] = nc.dram_tensor("ident_f32", [128, 128], F32, kind="ExternalInput").ap()
    io["o_g"] = nc.dram_tensor("o_g", [D, TOK], F32, kind="ExternalOutput").ap()
    io["ln2pad"] = nc.dram_tensor("ln2pad", [TOK + 1, D], BF16).ap()
    io["idxbuf"] = nc.dram_tensor("idxbuf", [8 * LCAP + 1, 4], F32).ap()
    io["parts"] = nc.dram_tensor("parts", [2 * TOK + 1, D], F32).ap()
    import contextlib as _cl
    with _cl.ExitStack() as _c:
        block = _c.enter_context(nc.Block())
        sem = [_c.enter_context(nc.semaphore(f"chain{i}")) for i in range(4)]
        p = Prog(nc)
        emit_l0b2(nc, p, io)
        p.run(block, sem)
    return nc


def _prep_l0b1_inputs(inputs):
    c = np.ascontiguousarray
    f = np.float32

    def cols(v):
        return c(np.asarray(v, f).reshape(DCH, 128).T)

    w = {
        "qkv_wT": c(inputs["L0_ffn_qkv_weight"].T),
        "proj_wT": c(inputs["L0_ffn_proj_weight"].T),
        "router_w": c(np.asarray(inputs["L0_ffn_router_weight"], f)),
        "ln1w": cols(inputs["L0_ffn_ln1_weight"]),
        "ln1b": cols(inputs["L0_ffn_ln1_bias"]),
        "ln2w": cols(inputs["L0_ffn_ln2_weight"]),
        "ln2b": cols(inputs["L0_ffn_ln2_bias"]),
        "ones_col": np.ones((128, 1), f),
        "ones_row": np.ones((1, 128), f),
        "ident_f32": np.eye(128, dtype=f),
        "iota8": np.tile(np.arange(EXP, dtype=f), (128, 1)),
    }
    import ml_dtypes
    w["ident_bf"] = np.eye(128).astype(ml_dtypes.bfloat16)
    return {k: (np.asarray(v, f) if k != "ident_bf" else v) for k, v in w.items()}


def _reshard_to_axis1(xT_stack):
    out = []
    for cc in range(NCORES):
        cols = [xT_stack[s][:, cc * 256:(cc + 1) * 256] for s in range(8)]
        out.append(np.ascontiguousarray(np.concatenate(cols, axis=1)))
    return out


def run_l0b1(inputs, xnT_stack):
    w = _prep_l0b1_inputs(inputs)
    shards = _reshard_to_axis1(xnT_stack)
    nc = build_l0b1()
    in_maps = []
    for cc in range(NCORES):
        m = dict(w)
        m["xnT"] = shards[cc]
        in_maps.append(m)
    res = run_bass_kernel_spmd(nc, in_maps, core_ids=list(range(NCORES)))
    return res.results



def _prep_l0b2_inputs(inputs, core_id):
    f = np.float32
    sel_g = np.zeros((64, EXP), f)
    sel_l = np.zeros((64, EXP), f)
    for c in range(8):
        for sp in range(8):
            r = c * 8 + sp
            for s in range(8):
                if sp < s or (sp == s and c < core_id):
                    sel_g[r, s] = 1.0
                if c == core_id and sp < s:
                    sel_l[r, s] = 1.0
    import ml_dtypes
    return {
        "sel_g": sel_g, "sel_l": sel_l,
        "moe_w1": np.asarray(inputs["L0_ffn_moe_w1"], f),
        "moe_w2": np.asarray(inputs["L0_ffn_moe_w2"], f),
        "ones_row8": np.ones((1, EXP), f),
        "eiota": np.arange(EXP, dtype=f).reshape(EXP, 1),
        "tokhalf": (np.arange(512) // 2).astype(f).reshape(1, 512),
        "jiota": np.arange(512, dtype=f).reshape(1, 512),
        "ident_bf": np.eye(128).astype(ml_dtypes.bfloat16),
        "ident_f32": np.eye(128, dtype=f),
    }


def run_l0b2(inputs, b1_results):
    counts_all = np.stack([np.asarray(r["o_counts"], np.float32)
                           for r in b1_results]).reshape(64, EXP)
    nc = build_l0b2()
    in_maps = []
    for cc in range(NCORES):
        m = _prep_l0b2_inputs(inputs, cc)
        m["counts_all"] = counts_all
        m["ln2tok"] = np.asarray(b1_results[cc]["o_ln2tok"])
        m["h1T"] = np.asarray(b1_results[cc]["o_h1"], np.float32)
        m["topi"] = np.asarray(b1_results[cc]["o_topi"], np.float32)
        m["topp"] = np.asarray(b1_results[cc]["o_topp"], np.float32)
        in_maps.append(m)
    res = run_bass_kernel_spmd(nc, in_maps, core_ids=list(range(NCORES)))
    return res.results



NHEADS = 16
C1 = math.sqrt(64.0 * 3.0)
BSC = math.sqrt(64.0 * 2.0)
WJ = 2048
SLABR = 2048 * WJ


def emit_l1a(nc, p, io):

    xT = io["xT"]
    s0T = io["s0T"]
    gT = io.get("gT")
    wq = io["wq"]
    wk = io["wk"]
    wv = io["wv"]
    wo = io["wo"]
    wpk = io["wpk"]
    wpq = io["wpq"]
    bq_c = io["bq_c"]
    bk_c = io["bk_c"]
    bo_c = io["bo_c"]
    bv_r = io["bv_r"]
    bpk_r = io["bpk_r"]
    bpq_r = io["bpq_r"]
    rel_emb = io["rel_emb"]
    lng_r = io["lng_r"]
    lnb_r = io["lnb_r"]
    expand1 = io["expand1"]
    expand2 = io["expand2"]
    ones_col = io["ones_col"]
    ones_row = io["ones_row"]
    ones_rowb = io["ones_rowb"]
    ident_bf = io["ident_bf"]

    o_x1 = io["o_x1"]
    o_xn1 = io["o_xn1"]
    o_s1 = io["o_s1"]

    slab1 = nc.dram_tensor("slab1", [NHEADS, 2048, WJ], BF16)
    slab2 = nc.dram_tensor("slab2", [NHEADS, 2048, WJ], BF16)

    w0 = p.mark()
    with contextlib.ExitStack() as ctx:
        sb = SB(nc, ctx)
        regA_h = ctx.enter_context(
            nc.sbuf_tensor("regA_f", [128, 2 * DCH * TOK], BF16))
        x1f = regA_h.bitcast(F32).ap()
        qT = regA_h.ap()[:, 0:DCH * TOK]
        kT = regA_h.ap()[:, DCH * TOK:2 * DCH * TOK]
        xnbf_h = ctx.enter_context(
            nc.sbuf_tensor("xnbf_f", [128, DCH * TOK], BF16))
        xnbf = xnbf_h.ap()
        relst = xnbf_h.bitcast(F32).ap()[:, 0:1024]
        rebf = xnbf_h.ap()[:, 2048:2048 + 4 * 1024]
        reT = xnbf_h.ap()[:, 6144:6144 + 8 * 512]
        gbc = xnbf_h.bitcast(F32).ap()[:, 5120:5120 + 1024]
        bbc = xnbf_h.bitcast(F32).ap()[:, 6144:6144 + 1024]
        vtok_h = ctx.enter_context(
            nc.sbuf_tensor("vtok_f", [128, 16 * (NHEADS * 65)], BF16))
        vtok = vtok_h.ap()
        mrow = vtok_h.bitcast(F32).ap()[:, 0:TOK]
        rrow = vtok_h.bitcast(F32).ap()[:, TOK:2 * TOK]
        scrow = vtok_h.bitcast(F32).ap()[:, 2 * TOK:3 * TOK]
        regB = sb.t("regB", [128, DCH * TOK], BF16)
        ph23_h = ctx.enter_context(
            nc.sbuf_tensor("ph23_f", [128, 4 * WJ], BF16))
        exp_sb = ph23_h.ap()
        wstage = sb.t("wstage", [128, 2 * 1024], BF16)
        pkw = sb.t("pkw", [128, 4 * 1024], BF16)
        stage = sb.t("stage", [128, WJ], BF16)
        m_bc = sb.t("m_bc", [128, TB], F32)
        r_bc = sb.t("r_bc", [128, TB], F32)
        scr1 = sb.t("scr1", [128, TB], F32)
        pkedgec = sb.t("pkedgec", [128, 2 * NHEADS], BF16)
        pqedgec = sb.t("pqedgec", [128, 2 * NHEADS], BF16)
        kpec = sb.t("kpec", [128, 32], F32)
        qpe0 = ph23_h.ap()[0:1, 0:TOK]
        qpe1 = ph23_h.ap()[0:1, TOK:2 * TOK]
        qpbc = ph23_h.ap()[:, 2 * TOK:2 * TOK + 1024]
        e1t = ph23_h.ap()[:, 5120:5120 + 1024]
        e2t = ph23_h.ap()[:, 6144:6144 + 1024]
        ptile2 = [ph23_h.ap()[:, 7168:7168 + 512],
                  stage[:, 0:512]]
        recv = scr1
        onec = sb.t("onec", [128, 1], F32)
        oner = sb.t("oner", [1, 128], F32)
        onerb = sb.t("onerb", [1, 128], BF16)
        identb = sb.t("identb", [128, 128], BF16)
        bcols = sb.t("bcols", [128, 3 * DCH], F32)
        brows = sb.t("brows", [1, 3 * D], BF16)
        ps_a = sb.p("ps_a", [128, TB], F32)
        ps_b = sb.p("ps_b", [128, TB], F32)
        ps_pv = sb.p("ps_pv", [128, TB], F32)
        ps_qk = [sb.p("ps_qk0", [128, TB], F32), sb.p("ps_qk1", [128, TB], F32)]
        ps_tp = sb.p("ps_tp", [128, 128], BF16)
        ps_r1 = sb.p("ps_r1", [1, TB], F32)
        ps_r2 = sb.p("ps_r2", [1, TB], F32)


        x1ch = [x1f[:, k * TOK:(k + 1) * TOK] for k in range(DCH)]
        xnch = [xnbf[:, k * TOK:(k + 1) * TOK] for k in range(DCH)]

        p.dma("sync", lambda e: e.dma_start(out=onec[:], in_=ones_col), wait=w0)
        p.dma("sync", lambda e: e.dma_start(out=oner[:], in_=ones_row), wait=w0)
        p.dma("sync", lambda e: e.dma_start(out=onerb[:], in_=ones_rowb), wait=w0)
        p.dma("sync", lambda e: e.dma_start(out=identb[:], in_=ident_bf), wait=w0)
        p.dma("sync", lambda e: e.dma_start(out=bcols[:, 0:DCH], in_=bq_c), wait=w0)
        p.dma("sync", lambda e: e.dma_start(out=bcols[:, DCH:2 * DCH], in_=bk_c), wait=w0)
        p.dma("sync", lambda e: e.dma_start(out=bcols[:, 2 * DCH:3 * DCH], in_=bo_c), wait=w0)
        p.dma("sync", lambda e: e.dma_start(out=brows[:, 0:D], in_=bv_r), wait=w0)
        p.dma("sync", lambda e: e.dma_start(out=brows[:, D:2 * D], in_=bpk_r), wait=w0)
        p.dma("sync", lambda e: e.dma_start(out=brows[:, 2 * D:3 * D], in_=bpq_r), wait=w0)

        for k in range(DCH):
            p.dma("sync", lambda e, k=k: e.dma_start(
                out=x1ch[k], in_=s0T[k * 128:(k + 1) * 128, :]), wait=w0)
        if "ag_g" in io:
            ridx_sb = sb.t("ridx_sb", [128, 1], I32)
            p.dma("sync", lambda e: e.dma_start(out=ridx_sb[:], in_=io["ridx"]))
            agf = io["ag_g"]
            for k in range(DCH):
                for cc in range(8):
                    p.dma("pool", lambda e, k=k, cc=cc: e.indirect_dma_start(
                        scr1[:, 0:256], None, agf,
                        bass.IndirectOffsetOnAxis(ridx_sb[:], 0),
                        element_offset=(cc * 1024 + k * 128) * 2048))
                    p.op("dve", lambda e, k=k, cc=cc: e.tensor_mul(
                        x1ch[k][:, cc * 256:(cc + 1) * 256],
                        x1ch[k][:, cc * 256:(cc + 1) * 256], scr1[:, 0:256]))
        else:
            for k in range(DCH):
                for tb in range(NTB):
                    p.dma("sync", lambda e, k=k, tb=tb: e.dma_start(
                        out=relst[:, 0:TB],
                        in_=gT[k * 128:(k + 1) * 128, tb * TB:(tb + 1) * TB]))
                    p.op("dve", lambda e, k=k, tb=tb: e.tensor_mul(
                        x1ch[k][:, tb * TB:(tb + 1) * TB],
                        x1ch[k][:, tb * TB:(tb + 1) * TB], relst[:, 0:TB]))
        for k in range(DCH):
            for tb in range(NTB):
                p.dma("sync", lambda e, k=k, tb=tb: e.dma_start(
                    out=relst[:, 0:TB],
                    in_=xT[k * 128:(k + 1) * 128, tb * TB:(tb + 1) * TB]))
                p.op("dve", lambda e, k=k, tb=tb: e.tensor_add(
                    x1ch[k][:, tb * TB:(tb + 1) * TB],
                    x1ch[k][:, tb * TB:(tb + 1) * TB], relst[:, 0:TB]))
            p.dma("sync", lambda e, k=k: e.dma_start(
                out=o_x1[k * 128:(k + 1) * 128, :], in_=x1ch[k]))
        emit_ln_rows(p, x1ch, scr1, ps_r1, ps_r2, onec[:], mrow[0:1, 0:TOK],
                     rrow[0:1, 0:TOK], scrow[0:1, 0:TOK])
        for tb in range(NTB):
            sl = slice(tb * TB, (tb + 1) * TB)
            p.op("pe", lambda e, sl=sl: e.matmul(
                ps_a[:, :], oner[:], mrow[0:1, sl], start=True, stop=True))
            p.op("dve", lambda e: e.tensor_copy(m_bc[:], ps_a[:, :]))
            p.op("pe", lambda e, sl=sl: e.matmul(
                ps_a[:, :], oner[:], rrow[0:1, sl], start=True, stop=True))
            p.op("dve", lambda e: e.tensor_copy(r_bc[:], ps_a[:, :]))
            for k in range(DCH):
                p.op("dve", lambda e, k=k, sl=sl: e.tensor_sub(
                    scr1[:], x1ch[k][:, sl], m_bc[:]))
                p.op("dve", lambda e, k=k: e.tensor_mul(scr1[:], scr1[:], r_bc[:]))
                p.dma("sync", lambda e, k=k, sl=sl: e.dma_start(
                    out=o_xn1[k * 128:(k + 1) * 128, sl], in_=scr1[:]))
                p.op("dve", lambda e, k=k, sl=sl: e.tensor_copy(
                    xnch[k][:, sl], scr1[:]))

        for name_w, dst, bco in ((wq, qT, 0), (wk, kT, DCH)):
            for ec in range(DCH):
                for kk in range(DCH):
                    p.dma("pool", lambda e, name_w=name_w, ec=ec, kk=kk: e.dma_start(
                        out=wstage[:, kk * 128:(kk + 1) * 128],
                        in_=name_w[kk * 128:(kk + 1) * 128, ec * 128:(ec + 1) * 128]))
                for tb in range(NTB):
                    for kk in range(DCH):
                        p.op("pe", lambda e, ec=ec, tb=tb, kk=kk: e.matmul(
                            ps_a[:, :], wstage[:, kk * 128:(kk + 1) * 128],
                            xnch[kk][:, tb * TB:(tb + 1) * TB],
                            start=(kk == 0), stop=(kk == DCH - 1)))
                    p.op("act", lambda e, dst=dst, ec=ec, tb=tb, bco=bco: e.activation(
                        dst[:, ec * TOK + tb * TB:ec * TOK + (tb + 1) * TB],
                        ps_a[:, :], AF.Identity,
                        bias=bcols[:, bco + ec:bco + ec + 1]))
        for kk in range(DCH):
            p.dma("pool", lambda e, kk=kk: e.dma_start(
                out=regB[:, kk * 1024:(kk + 1) * 1024],
                in_=wv[kk * 128:(kk + 1) * 128, :]), wait=None)
        for ti in range(16):
            p.op("dve", lambda e, ti=ti: e.memset(
                vtok[:, ti * 1040:(ti + 1) * 1040].rearrange(
                    "p (h c) -> p h c", c=65)[:, :, 64:65], 1.0))
        for ti in range(16):
            for blk2 in range(2):
                for kk in range(DCH):
                    p.op("pe", lambda e, kk=kk, ti=ti, blk2=blk2: e.matmul(
                        ps_b[:, :], xnch[kk][:, ti * 128:(ti + 1) * 128],
                        regB[:, kk * 1024 + blk2 * 512:kk * 1024 + (blk2 + 1) * 512],
                        start=(kk == 0), stop=False, skip_group_check=True))
                p.op("pe", lambda e, blk2=blk2: e.matmul(
                    ps_b[:, :], onerb[:],
                    brows[0:1, blk2 * 512:(blk2 + 1) * 512],
                    start=False, stop=True, skip_group_check=True))
                p.op("act", lambda e, ti=ti, blk2=blk2: e.activation(
                    vtok[:, ti * 1040:(ti + 1) * 1040].rearrange(
                        "p (h c) -> p h c", c=65)[:, blk2 * 8:(blk2 + 1) * 8, 0:64],
                    ps_b[:, :].rearrange("p (h c) -> p h c", c=64), AF.Copy))

        for srcdram, dst in ((lng_r, gbc), (lnb_r, bbc)):
            p.dma("sync", lambda e, srcdram=srcdram: e.dma_start(
                out=relst[0:1, 0:D], in_=srcdram))
            for bl in range(2):
                p.op("pe", lambda e, bl=bl: e.matmul(
                    ps_a[:, :], oner[:],
                    relst[0:1, bl * 512:(bl + 1) * 512],
                    start=True, stop=True))
                p.op("dve", lambda e, dst=dst, bl=bl: e.tensor_copy(
                    dst[:, bl * 512:(bl + 1) * 512], ps_a[:, :]))
        for rt in range(4):
            p.dma("sync", lambda e, rt=rt: e.dma_start(
                out=relst[:, :], in_=rel_emb[rt * 128:(rt + 1) * 128, :]))
            for hf in range(2):
                p.op("dve", lambda e, hf=hf: e.bn_stats(
                    kpec[:, hf * 6:(hf + 1) * 6], relst[:, hf * 512:(hf + 1) * 512]))
            p.op("dve", lambda e: e.bn_aggr(kpec[:, 12:14], kpec[:, 0:12]))
            p.op("dve", lambda e: e.tensor_scalar_add(
                kpec[:, 13:14], kpec[:, 13:14], LN_EPS))
            p.op("act", lambda e: e.activation(kpec[:, 13:14], kpec[:, 13:14], AF.Sqrt))
            p.op("dve", lambda e: e.reciprocal(kpec[:, 13:14], kpec[:, 13:14]))
            p.op("dve", lambda e: e.tensor_scalar(
                relst[:, :], relst[:, :], kpec[:, 12:13], kpec[:, 13:14],
                ALU.subtract, ALU.mult))
            p.op("dve", lambda e: e.tensor_mul(relst[:, :], relst[:, :], gbc[:, :]))
            p.op("dve", lambda e, rt=rt: e.tensor_tensor(
                rebf[:, rt * 1024:(rt + 1) * 1024], relst[:, :], bbc[:, :], ALU.add))
            for kk in range(DCH):
                p.op("pe", lambda e, rt=rt, kk=kk: e.transpose(
                    ps_tp[:, :], rebf[:, rt * 1024 + kk * 128:rt * 1024 + (kk + 1) * 128],
                    identb[:]))
                p.op("dve", lambda e, rt=rt, kk=kk: e.tensor_copy(
                    reT[:, kk * 512 + rt * 128:kk * 512 + (rt + 1) * 128], ps_tp[:, :]))

        for side in range(2):
            w_side = wpk if side == 0 else wpq
            brow_off = D if side == 0 else 2 * D
            slab = slab1 if side == 0 else slab2
            expnd = expand1 if side == 0 else expand2
            actT = qT if side == 0 else kT
            edgec = pkedgec if side == 0 else pqedgec
            for wc in range(4):
                p.dma("pool", lambda e, expnd=expnd, wc=wc: e.dma_start(
                    out=exp_sb[:, wc * WJ:(wc + 1) * WJ],
                    in_=expnd[wc * 128:(wc + 1) * 128, :]))
            for kk in range(DCH):
                p.dma("pool", lambda e, w_side=w_side, kk=kk: e.dma_start(
                    out=regB[:, kk * 1024:(kk + 1) * 1024],
                    in_=w_side[kk * 128:(kk + 1) * 128, :]))
            for wt in range(4):
                for bl in range(2):
                    for kk in range(DCH):
                        p.op("pe", lambda e, wt=wt, bl=bl, kk=kk: e.matmul(
                            ps_a[:, :],
                            reT[:, kk * 512 + wt * 128:kk * 512 + (wt + 1) * 128],
                            regB[:, kk * 1024 + bl * 512:kk * 1024 + (bl + 1) * 512],
                            start=(kk == 0), stop=False, skip_group_check=True))
                    p.op("pe", lambda e, bl=bl, brow_off=brow_off: e.matmul(
                        ps_a[:, :], onerb[:],
                        brows[0:1, brow_off + bl * 512:brow_off + (bl + 1) * 512],
                        start=False, stop=True, skip_group_check=True))
                    p.op("act", lambda e, wt=wt, bl=bl: e.activation(
                        pkw[:, wt * 1024 + bl * 512:wt * 1024 + (bl + 1) * 512],
                        ps_a[:, :], AF.Copy))
            for jt in range(16):
                for dbl in range(2):
                    for wc in range(4):
                        p.op("pe", lambda e, jt=jt, dbl=dbl, wc=wc: e.matmul(
                            ps_a[:, :],
                            exp_sb[:, wc * WJ + jt * 128:wc * WJ + (jt + 1) * 128],
                            pkw[:, wc * 1024 + dbl * 512:wc * 1024 + (dbl + 1) * 512],
                            start=(wc == 0), stop=(wc == 3)))
                    p.op("act", lambda e: e.activation(
                        stage[:, 0:512], ps_a[:, :], AF.Copy))
                    for sub in range(4):
                        p.op("pe", lambda e, sub=sub: e.transpose(
                            ps_tp[:, :], stage[:, sub * 128:(sub + 1) * 128],
                            identb[:]))
                        dchunk = dbl * 4 + sub
                        p.op("dve", lambda e, dchunk=dchunk, jt=jt: e.tensor_copy(
                            regB[:, dchunk * WJ + jt * 128:dchunk * WJ + (jt + 1) * 128]
                            if False else
                            regB[:, dchunk * 2048 + jt * 128:dchunk * 2048 + (jt + 1) * 128],
                            ps_tp[:, :]))
            for dchunk in range(DCH):
                for eix, jcol in ((0, 0), (1, WJ - 1)):
                    p.op("dve", lambda e, dchunk=dchunk, eix=eix, jcol=jcol,
                         edgec=edgec: e.tensor_copy(
                        edgec[:, dchunk * 2 + eix:dchunk * 2 + eix + 1],
                        regB[:, dchunk * 2048 + jcol:dchunk * 2048 + jcol + 1]))
            for h in range(NHEADS):
                ch, ro = h // 2, (h % 2) * 64
                for st in range(16):
                    lhs = actT[ro:ro + 64,
                               ch * TOK + st * 128:ch * TOK + (st + 1) * 128]
                    for jb in range(4):
                        p.op("pe", lambda e, lhs=lhs, ch=ch, ro=ro, jb=jb: e.matmul(
                            ps_a[:, :], lhs,
                            regB[ro:ro + 64,
                                 ch * 2048 + jb * 512:ch * 2048 + (jb + 1) * 512],
                            start=True, stop=True))
                        p.op("act", lambda e, jb=jb: e.activation(
                            stage[:, jb * 512:(jb + 1) * 512], ps_a[:, :],
                            AF.Exp, scale=1.0 / BSC))
                    p.dma("sync", lambda e, slab=slab, h=h, st=st: e.dma_start(
                        out=slab[h, st * 128:(st + 1) * 128, :], in_=stage[:, :]))

        slab1f = slab1.ap().rearrange("h a b -> (h a b)")
        slab2f = slab2.ap().rearrange("h a b -> (h a b)")
        for h in range(NHEADS):
            ch, ro = h // 2, (h % 2) * 64

            def qsl(c0, w, ch=ch, ro=ro):
                return qT[ro:ro + 64, ch * TOK + c0:ch * TOK + c0 + w]

            def ksl(c0, w, ch=ch, ro=ro):
                return kT[ro:ro + 64, ch * TOK + c0:ch * TOK + c0 + w]

            for eix, dstrow in ((0, qpe1), (1, qpe0)):
                for tb in range(NTB):
                    p.op("pe", lambda e, eix=eix, tb=tb, ch=ch, ro=ro, qsl=qsl:
                         e.matmul(
                             ps_r1[:, :],
                             pkedgec[ro:ro + 64, ch * 2 + eix:ch * 2 + eix + 1],
                             qsl(tb * TB, TB), start=True, stop=True))
                    p.op("act", lambda e, dstrow=dstrow, tb=tb: e.activation(
                        dstrow[0:1, tb * TB:(tb + 1) * TB], ps_r1[:, :],
                        AF.Exp, scale=1.0 / BSC))
            for tt in range(16):
                p.op("pe", lambda e, tt=tt, ch=ch, ro=ro, ksl=ksl: e.matmul(
                    ps_b[:, 0:2],
                    ksl(tt * 128, 128),
                    pqedgec[ro:ro + 64, ch * 2:ch * 2 + 2],
                    start=True, stop=True))
                p.op("act", lambda e, tt=tt: e.activation(
                    kpec[:, tt * 2:tt * 2 + 2], ps_b[:, 0:2],
                    AF.Copy, scale=1.0 / BSC))
            for sc in range(4):
                s0 = sc * 512
                p.op("pe", lambda e, s0=s0: e.matmul(
                    ps_b[:, :], onerb[:], qpe0[0:1, s0:s0 + 512],
                    start=True, stop=True))
                p.op("dve", lambda e: e.tensor_copy(qpbc[:, 0:512], ps_b[:, :]))
                p.op("pe", lambda e, s0=s0: e.matmul(
                    ps_b[:, :], onerb[:], qpe1[0:1, s0:s0 + 512],
                    start=True, stop=True))
                p.op("dve", lambda e: e.tensor_copy(qpbc[:, 512:1024], ps_b[:, :]))
                w_phase = p.mark()
                w_exp = {0: w_phase, 1: w_phase}
                w_mul = {0: w_phase, 1: w_phase}
                w_mul_t = {}
                w_dma_t = {}

                def emit_etile_dmas(tt, wait):
                    t0 = tt * 128
                    DDiag = t0 - s0
                    if not (-640 < DDiag < 1024):
                        w_dma_t[tt] = p.mark()
                        return
                    par = tt % 2
                    off1 = h * SLABR + s0 * (WJ - 1) + t0 + (WJ // 2 - 1)
                    src1 = bass.AP(slab1f.tensor, off1, [[WJ - 1, 512], [1, 128]])
                    p.dma("sync", lambda e, src1=src1, par=par:
                          e.dma_start_transpose(
                              e1t[:, par * TB:(par + 1) * TB], src1), wait=wait)
                    off2 = h * SLABR + t0 * (WJ - 1) + s0 + (WJ // 2 - 1)
                    src2 = bass.AP(slab2f.tensor, off2, [[WJ - 1, 128], [1, 512]])
                    p.dma("pool", lambda e, src2=src2, par=par: e.dma_start(
                        out=e2t[:, par * TB:(par + 1) * TB], in_=src2), wait=wait)
                    w_dma_t[tt] = p.mark()

                emit_etile_dmas(0, w_phase)
                emit_etile_dmas(1, w_phase)
                for tt in range(16):
                    par = tt % 2
                    t0 = tt * 128
                    DDiag = t0 - s0
                    band = -640 < DDiag < 1024
                    p.op("pe", lambda e, tt=tt, s0=s0, ksl=ksl, qsl=qsl, par=par:
                         e.matmul(ps_qk[par][:, :], ksl(tt * 128, 128),
                                  qsl(s0, 512), start=True, stop=True),
                         wait=w_exp[par])
                    w_qk = p.mark()
                    if tt > 0:
                        p.op("pe", lambda e, tt=tt, h=h, pv=1 - par: e.matmul(
                            ps_pv[0:65, :],
                            vtok[:, (tt - 1) * 1040 + h * 65:(tt - 1) * 1040 + (h + 1) * 65],
                            ptile2[pv][:], start=(tt == 1), stop=False,
                            skip_group_check=True), wait=w_mul_t[tt - 1])
                    if DDiag >= 1024:
                        p.op("act", lambda e, tt=tt, par=par: e.activation(
                            ptile2[par][:], ps_qk[par][:, :], AF.Exp,
                            bias=kpec[:, tt * 2:tt * 2 + 1], scale=1.0 / C1),
                            wait=w_qk)
                        mults = [lambda e, par=par: e.tensor_mul(
                            ptile2[par][:], ptile2[par][:], qpbc[:, 0:512])]
                    elif DDiag <= -640:
                        p.op("act", lambda e, tt=tt, par=par: e.activation(
                            ptile2[par][:], ps_qk[par][:, :], AF.Exp,
                            bias=kpec[:, tt * 2 + 1:tt * 2 + 2], scale=1.0 / C1),
                            wait=w_qk)
                        mults = [lambda e, par=par: e.tensor_mul(
                            ptile2[par][:], ptile2[par][:], qpbc[:, 512:1024])]
                    else:
                        p.op("act", lambda e, par=par: e.activation(
                            ptile2[par][:], ps_qk[par][:, :], AF.Exp,
                            scale=1.0 / C1), wait=w_qk)
                        mults = [lambda e, par=par: e.tensor_mul(
                            ptile2[par][:], ptile2[par][:],
                            e1t[:, par * TB:(par + 1) * TB]),
                            lambda e, par=par: e.tensor_mul(
                            ptile2[par][:], ptile2[par][:],
                            e2t[:, par * TB:(par + 1) * TB])]
                    p.op("act", lambda e: e.activation(
                        kpec[32:33, 0:1], kpec[32:33, 0:1], AF.Copy))
                    w_exp[par] = p.mark()
                    w_first = max(w_exp[par], w_dma_t[tt]) if band else w_exp[par]
                    for mi, mf in enumerate(mults):
                        p.op("dve", mf, wait=(w_first if mi == 0 else None))
                    p.op("dve", lambda e: e.engine_nop())
                    w_mul[par] = w_mul_t[tt] = p.mark()
                    if tt + 2 < 16:
                        emit_etile_dmas(tt + 2, w_mul[par])
                p.op("pe", lambda e, h=h: e.matmul(
                    ps_pv[0:65, :],
                    vtok[:, 15 * 1040 + h * 65:15 * 1040 + (h + 1) * 65],
                    ptile2[1][:], start=False, stop=True,
                    skip_group_check=True), wait=w_mul_t[15])
                p.op("dve", lambda e: e.reciprocal(recv[0:1, 0:512], ps_pv[64:65, :]))
                p.op("pe", lambda e: e.matmul(
                    ps_b[0:64, :], oner[0:1, 0:64], recv[0:1, 0:512],
                    start=True, stop=True))
                p.op("dve", lambda e: e.tensor_copy(recv[0:64, :], ps_b[0:64, :]))
                p.op("dve", lambda e, sc=sc, ch=ch, ro=ro: e.tensor_tensor(
                    regB[ro:ro + 64, ch * TOK + sc * 512:ch * TOK + (sc + 1) * 512],
                    ps_pv[0:64, :], recv[0:64, :], ALU.mult))

        for ec in range(DCH):
            for kk in range(DCH):
                p.dma("pool", lambda e, ec=ec, kk=kk: e.dma_start(
                    out=wstage[:, kk * 128:(kk + 1) * 128],
                    in_=wo[kk * 128:(kk + 1) * 128, ec * 128:(ec + 1) * 128]))
            for tb in range(NTB):
                for kk in range(DCH):
                    p.op("pe", lambda e, ec=ec, tb=tb, kk=kk: e.matmul(
                        ps_a[:, :], wstage[:, kk * 128:(kk + 1) * 128],
                        regB[:, kk * TOK + tb * TB:kk * TOK + (tb + 1) * TB],
                        start=(kk == 0), stop=(kk == DCH - 1)))
                p.op("act", lambda e, ec=ec, tb=tb: e.activation(
                    scr1[:], ps_a[:, :], AF.Identity,
                    bias=bcols[:, 2 * DCH + ec:2 * DCH + ec + 1]))
                p.dma("sync", lambda e, ec=ec, tb=tb: e.dma_start(
                    out=o_s1[ec * 128:(ec + 1) * 128, tb * TB:(tb + 1) * TB],
                    in_=scr1[:]))



def build_l1a():
    nc = bass.Bass("TRN2", target_bir_lowering=False)
    io = {}
    io["xT"] = nc.dram_tensor("xT", [D, TOK], F32, kind="ExternalInput").ap()
    io["s0T"] = nc.dram_tensor("s0T", [D, TOK], F32, kind="ExternalInput").ap()
    io["gT"] = nc.dram_tensor("gT", [D, TOK], F32, kind="ExternalInput").ap()
    io["wq"] = nc.dram_tensor("wq", [D, D], F32, kind="ExternalInput").ap()
    io["wk"] = nc.dram_tensor("wk", [D, D], F32, kind="ExternalInput").ap()
    io["wv"] = nc.dram_tensor("wv", [D, D], F32, kind="ExternalInput").ap()
    io["wo"] = nc.dram_tensor("wo", [D, D], F32, kind="ExternalInput").ap()
    io["wpk"] = nc.dram_tensor("wpk", [D, D], F32, kind="ExternalInput").ap()
    io["wpq"] = nc.dram_tensor("wpq", [D, D], F32, kind="ExternalInput").ap()
    io["bq_c"] = nc.dram_tensor("bq_c", [128, DCH], F32, kind="ExternalInput").ap()
    io["bk_c"] = nc.dram_tensor("bk_c", [128, DCH], F32, kind="ExternalInput").ap()
    io["bo_c"] = nc.dram_tensor("bo_c", [128, DCH], F32, kind="ExternalInput").ap()
    io["bv_r"] = nc.dram_tensor("bv_r", [1, D], BF16, kind="ExternalInput").ap()
    io["bpk_r"] = nc.dram_tensor("bpk_r", [1, D], BF16, kind="ExternalInput").ap()
    io["bpq_r"] = nc.dram_tensor("bpq_r", [1, D], BF16, kind="ExternalInput").ap()
    io["rel_emb"] = nc.dram_tensor("rel_emb", [512, D], F32, kind="ExternalInput").ap()
    io["lng_r"] = nc.dram_tensor("lng_r", [1, D], F32, kind="ExternalInput").ap()
    io["lnb_r"] = nc.dram_tensor("lnb_r", [1, D], F32, kind="ExternalInput").ap()
    io["expand1"] = nc.dram_tensor("expand1", [512, WJ], BF16, kind="ExternalInput").ap()
    io["expand2"] = nc.dram_tensor("expand2", [512, WJ], BF16, kind="ExternalInput").ap()
    io["ones_col"] = nc.dram_tensor("ones_col", [128, 1], F32, kind="ExternalInput").ap()
    io["ones_row"] = nc.dram_tensor("ones_row", [1, 128], F32, kind="ExternalInput").ap()
    io["ones_rowb"] = nc.dram_tensor("ones_rowb", [1, 128], BF16, kind="ExternalInput").ap()
    io["ident_bf"] = nc.dram_tensor("ident_bf", [128, 128], BF16, kind="ExternalInput").ap()
    io["o_x1"] = nc.dram_tensor("o_x1", [D, TOK], F32, kind="ExternalOutput").ap()
    io["o_xn1"] = nc.dram_tensor("o_xn1", [D, TOK], F32, kind="ExternalOutput").ap()
    io["o_s1"] = nc.dram_tensor("o_s1", [D, TOK], F32, kind="ExternalOutput").ap()
    io["o_dslab1"] = nc.dram_tensor("o_dslab1", [128, WJ], BF16, kind="ExternalOutput").ap()
    io["o_dslab2"] = nc.dram_tensor("o_dslab2", [128, WJ], BF16, kind="ExternalOutput").ap()
    io["o_det"] = nc.dram_tensor("o_det", [128, 1024], BF16, kind="ExternalOutput").ap()
    io["o_dpt"] = nc.dram_tensor("o_dpt", [128, 512], BF16, kind="ExternalOutput").ap()
    io["o_dctx"] = nc.dram_tensor("o_dctx", [128, TOK], BF16, kind="ExternalOutput").ap()
    io["o_dk"] = nc.dram_tensor("o_dk", [128, TOK], BF16, kind="ExternalOutput").ap()
    io["o_dpv"] = nc.dram_tensor("o_dpv", [128, 512], F32, kind="ExternalOutput").ap()
    import contextlib as _cl
    with _cl.ExitStack() as _c:
        block = _c.enter_context(nc.Block())
        sem = [_c.enter_context(nc.semaphore(f"chain{i}")) for i in range(4)]
        p = Prog(nc)
        emit_l1a(nc, p, io)
        p.run(block, sem)
    return nc



def _bucket_f32(rel):
    rel = np.asarray(rel)
    mid = 128
    sign = np.sign(rel).astype(np.float32)
    abs_pos = np.where((rel < mid) & (rel > -mid), mid - 1,
                       np.abs(rel)).astype(np.float32)
    log_pos = np.ceil(
        np.log(abs_pos / np.float32(mid)) /
        np.float32(np.log(np.float32(511) / np.float32(mid))) *
        np.float32(mid - 1)).astype(np.float32) + mid
    bucket = np.where(abs_pos <= mid, rel.astype(np.float32), log_pos * sign)
    return bucket.astype(np.int32)


def _expand_mats():
    jp = np.arange(WJ)
    r = 1023 - jp
    b = _bucket_f32(r)
    g1 = np.clip(b + 256, 0, 511)
    g2 = np.clip(256 - b, 0, 511)
    e1 = (np.arange(512)[:, None] == g1[None, :])
    e2 = (np.arange(512)[:, None] == g2[None, :])
    import ml_dtypes
    return e1.astype(ml_dtypes.bfloat16), e2.astype(ml_dtypes.bfloat16)


def _prep_l1a_inputs(inputs):
    c = np.ascontiguousarray
    f = np.float32
    import ml_dtypes

    def cols(v):
        return c(np.asarray(v, f).reshape(DCH, 128).T)

    e1, e2 = _expand_mats()
    w = {
        "wq": c(np.asarray(inputs["L1_attn_Wq"], f)),
        "wk": c(np.asarray(inputs["L1_attn_Wk"], f)),
        "wv": c(np.asarray(inputs["L1_attn_Wv"], f)),
        "wo": c(np.asarray(inputs["L1_attn_Wo"], f)),
        "wpk": c(np.asarray(inputs["L1_attn_Wpk"], f)),
        "wpq": c(np.asarray(inputs["L1_attn_Wpq"], f)),
        "bq_c": cols(inputs["L1_attn_bq"]),
        "bk_c": cols(inputs["L1_attn_bk"]),
        "bo_c": cols(inputs["L1_attn_bo"]),
        "bv_r": np.asarray(inputs["L1_attn_bv"], f).reshape(1, D).astype(ml_dtypes.bfloat16),
        "bpk_r": np.asarray(inputs["L1_attn_bpk"], f).reshape(1, D).astype(ml_dtypes.bfloat16),
        "bpq_r": np.asarray(inputs["L1_attn_bpq"], f).reshape(1, D).astype(ml_dtypes.bfloat16),
        "rel_emb": c(np.asarray(inputs["L1_attn_rel_emb"], f)),
        "lng_r": np.asarray(inputs["L1_attn_ln_g"], f).reshape(1, D),
        "lnb_r": np.asarray(inputs["L1_attn_ln_b"], f).reshape(1, D),
        "expand1": c(e1),
        "expand2": c(e2),
        "ones_col": np.ones((128, 1), f),
        "ones_row": np.ones((1, 128), f),
        "ones_rowb": np.ones((1, 128), dtype=ml_dtypes.bfloat16),
        "ident_bf": np.eye(128).astype(ml_dtypes.bfloat16),
    }
    return w


def run_l1a(inputs, xT_stack, s0T_stack, gT_stack):
    w = _prep_l1a_inputs(inputs)
    nc = build_l1a()
    in_maps = []
    for b in range(NCORES):
        m = dict(w)
        m["xT"] = np.ascontiguousarray(xT_stack[b])
        m["s0T"] = np.ascontiguousarray(s0T_stack[b])
        m["gT"] = np.ascontiguousarray(gT_stack[b])
        in_maps.append(m)
    res = run_bass_kernel_spmd(nc, in_maps, core_ids=list(range(NCORES)))
    return res.results



NBLK = 16
FF = 4096
FCH = FF // 128


def emit_l1f(nc, p, io):

    x1T = io["x1T"]
    xn1T = io["xn1T"]
    s1T = io["s1T"]
    w_router = io["w_router"]
    b_router = io["b_router"]
    w1T = io["w1T"]
    b1c = io["b1c"]
    w2T = io["w2T"]
    b2c = io["b2c"]
    ones_row = io["ones_row"]
    ident_f32 = io["ident_f32"]

    o_out = io["o_out"]
    maskdram = nc.dram_tensor("f_maskdram", [NBLK, TOK], F32).ap()

    w0 = p.mark()
    with contextlib.ExitStack() as ctx:
        sb = SB(nc, ctx)
        xnbf = sb.t("xnbf", [128, DCH * TOK], BF16)
        hbuf = sb.t("hbuf", [128, FCH * 1024], BF16)
        mrow0 = sb.t("mrow0", [1, 1024], F32)
        lgs = sb.t("lgs", [128, TOK], F32)
        mbc = sb.t("mbc", [128, 1024], F32)
        stg = sb.t("stg", [128, TOK], F32)
        stg2 = sb.t("stg2", [128, TOK], F32)
        w1st = sb.t("w1st", [128, 1024], BF16)
        w2st = sb.t("w2st", [128, FCH * 128], BF16)
        rw = sb.t("rw", [128, DCH * NBLK], F32)
        brc = sb.t("brc", [128, 1], F32)
        b1cs = sb.t("b1cs", [128, FCH], F32)
        b2cs = sb.t("b2cs", [128, DCH], F32)
        oner = sb.t("oner", [1, 128], F32)
        identf = sb.t("identf", [128, 128], F32)
        ptile = sb.t("ptile", [128, NBLK], F32)
        mk = sb.t("mk", [128, NBLK], F32)
        eq = sb.t("eq", [128, NBLK], F32)
        tc = sb.t("tc", [128, 2], F32)
        sbst = sb.t("sbst", [128, TB], BF16)
        ps_lg = sb.p("ps_lg", [128, TOK], F32)
        ps_a = sb.p("ps_a", [128, TB], F32)
        ps_tp32 = sb.p("ps_tp32", [128, 128], F32)
        ps_bc = sb.p("ps_bc", [128, TB], F32)


        p.dma("sync", lambda e: e.dma_start(out=oner[:], in_=ones_row), wait=w0)
        p.dma("sync", lambda e: e.dma_start(out=identf[:], in_=ident_f32), wait=w0)
        p.dma("sync", lambda e: e.dma_start(out=brc[0:NBLK, :], in_=b_router), wait=w0)
        p.dma("sync", lambda e: e.dma_start(out=b1cs[:], in_=b1c), wait=w0)
        p.dma("sync", lambda e: e.dma_start(out=b2cs[:], in_=b2c), wait=w0)
        for k in range(DCH):
            p.dma("sync", lambda e, k=k: e.dma_start(
                out=rw[:, k * NBLK:(k + 1) * NBLK],
                in_=w_router[k * 128:(k + 1) * 128, :]), wait=w0)

        for k in range(DCH):
            p.dma("sync", lambda e, k=k: e.dma_start(
                out=stg[:], in_=xn1T[k * 128:(k + 1) * 128, :]))
            p.op("dve", lambda e, k=k: e.tensor_copy(
                xnbf[:, k * TOK:(k + 1) * TOK], stg[:]))
            for tb in range(NTB):
                p.op("pe", lambda e, k=k, tb=tb: e.matmul(
                    ps_lg[0:NBLK, tb * TB:(tb + 1) * TB],
                    rw[:, k * NBLK:(k + 1) * NBLK],
                    stg[:, tb * TB:(tb + 1) * TB],
                    start=(k == 0), stop=(k == DCH - 1),
                    skip_group_check=True))
        for tb in range(NTB):
            p.op("act", lambda e, tb=tb: e.activation(
                lgs[0:NBLK, tb * TB:(tb + 1) * TB],
                ps_lg[0:NBLK, tb * TB:(tb + 1) * TB],
                AF.Identity, bias=brc[0:NBLK, :]))
        for ti in range(16):
            p.op("pe", lambda e, ti=ti: e.transpose(
                ps_tp32[:, 0:NBLK], lgs[0:NBLK, ti * 128:(ti + 1) * 128],
                identf[0:NBLK, 0:NBLK]))
            p.op("dve", lambda e: e.tensor_copy(ptile[:], ps_tp32[:, 0:NBLK]))
            p.op("dve", lambda e: e.memset(mk[:], 0.0))
            for rnd in range(4):
                p.op("dve", lambda e: e.tensor_reduce(
                    tc[:, 0:1], ptile[:], AX.X, ALU.max))
                p.op("dve", lambda e: e.tensor_scalar(
                    eq[:], ptile[:], tc[:, 0:1], None, ALU.is_equal))
                p.op("dve", lambda e: e.tensor_add(mk[:], mk[:], eq[:]))
                if rnd < 3:
                    p.op("dve", lambda e: e.tensor_scalar_mul(eq[:], eq[:], -1e30))
                    p.op("dve", lambda e: e.tensor_add(ptile[:], ptile[:], eq[:]))
            p.op("pe", lambda e, ti=ti: e.transpose(
                ps_tp32[0:NBLK, 0:128], mk[:], identf[:]))
            p.op("dve", lambda e, ti=ti: e.tensor_copy(
                lgs[0:NBLK, ti * 128:(ti + 1) * 128], ps_tp32[0:NBLK, 0:128]))
        p.dma("sync", lambda e: e.dma_start(out=maskdram, in_=lgs[0:NBLK, :]))

        for half in range(2):
            hoff = half * 1024
            for blkb in range(NBLK):
                p.dma("sync", lambda e, blkb=blkb, hoff=hoff: e.dma_start(
                    out=mrow0[:], in_=maskdram[blkb:blkb + 1, hoff:hoff + 1024]))
                for bl in range(2):
                    p.op("pe", lambda e, bl=bl: e.matmul(
                        ps_bc[:, :], oner[:], mrow0[0:1, bl * TB:(bl + 1) * TB],
                        start=True, stop=True))
                    p.op("dve", lambda e, bl=bl: e.tensor_copy(
                        mbc[:, bl * TB:(bl + 1) * TB], ps_bc[:, :]))
                for fc in (2 * blkb, 2 * blkb + 1):
                    for kk in range(DCH):
                        p.dma("pool", lambda e, fc=fc, kk=kk: e.dma_start(
                            out=w1st[:, kk * 128:(kk + 1) * 128],
                            in_=w1T[kk * 128:(kk + 1) * 128, fc * 128:(fc + 1) * 128]))
                    for bl in range(2):
                        for kk in range(DCH):
                            p.op("pe", lambda e, fc=fc, kk=kk, hoff=hoff, bl=bl:
                                 e.matmul(
                                     ps_a[:, :], w1st[:, kk * 128:(kk + 1) * 128],
                                     xnbf[:, kk * TOK + hoff + bl * TB:
                                          kk * TOK + hoff + (bl + 1) * TB],
                                     start=(kk == 0), stop=(kk == DCH - 1)))
                        p.op("act", lambda e, fc=fc, bl=bl: e.activation(
                            sbst[:, :], ps_a[:, :], AF.Relu,
                            bias=b1cs[:, fc:fc + 1]))
                        p.op("dve", lambda e, fc=fc, bl=bl: e.tensor_tensor(
                            hbuf[:, fc * 1024 + bl * TB:fc * 1024 + (bl + 1) * TB],
                            sbst[:, :], mbc[:, bl * TB:(bl + 1) * TB], ALU.mult))
            for ec in range(DCH):
                for kk in range(FCH):
                    p.dma("pool", lambda e, ec=ec, kk=kk: e.dma_start(
                        out=w2st[:, kk * 128:(kk + 1) * 128],
                        in_=w2T[kk * 128:(kk + 1) * 128, ec * 128:(ec + 1) * 128]))
                p.dma("sync", lambda e, ec=ec, hoff=hoff: e.dma_start(
                    out=stg[:, 0:1024], in_=x1T[ec * 128:(ec + 1) * 128,
                                                hoff:hoff + 1024]))
                p.dma("sync", lambda e, ec=ec, hoff=hoff: e.dma_start(
                    out=stg2[:, 0:1024], in_=s1T[ec * 128:(ec + 1) * 128,
                                                 hoff:hoff + 1024]))
                for bl in range(2):
                    for kk in range(FCH):
                        p.op("pe", lambda e, ec=ec, kk=kk, bl=bl: e.matmul(
                            ps_a[:, :], w2st[:, kk * 128:(kk + 1) * 128],
                            hbuf[:, kk * 1024 + bl * TB:kk * 1024 + (bl + 1) * TB],
                            start=(kk == 0), stop=(kk == FCH - 1)))
                    p.op("act", lambda e, ec=ec, bl=bl: e.activation(
                        stg2[:, 1024 + bl * TB:1024 + (bl + 1) * TB]
                        if False else stg[:, 1024 + bl * TB:1024 + (bl + 1) * TB],
                        ps_a[:, :], AF.Identity, bias=b2cs[:, ec:ec + 1]))
                    p.op("act", lambda e, bl=bl: e.activation(
                        stg[:, 1024 + bl * TB:1024 + (bl + 1) * TB],
                        stg[:, 1024 + bl * TB:1024 + (bl + 1) * TB], AF.Silu))
                    p.op("dve", lambda e, bl=bl: e.tensor_mul(
                        stg2[:, 1024 + bl * TB:1024 + (bl + 1) * TB]
                        if False else stg2[:, bl * TB:(bl + 1) * TB],
                        stg2[:, bl * TB:(bl + 1) * TB],
                        stg[:, 1024 + bl * TB:1024 + (bl + 1) * TB]))
                    p.op("dve", lambda e, bl=bl: e.tensor_add(
                        stg[:, bl * TB:(bl + 1) * TB],
                        stg[:, bl * TB:(bl + 1) * TB],
                        stg2[:, bl * TB:(bl + 1) * TB]))
                p.dma("sync", lambda e, ec=ec, hoff=hoff: e.dma_start(
                    out=o_out[ec * 128:(ec + 1) * 128, hoff:hoff + 1024],
                    in_=stg[:, 0:1024]))



def build_l1f():
    nc = bass.Bass("TRN2", target_bir_lowering=False)
    io = {}
    io["x1T"] = nc.dram_tensor("x1T", [D, TOK], F32, kind="ExternalInput").ap()
    io["xn1T"] = nc.dram_tensor("xn1T", [D, TOK], F32, kind="ExternalInput").ap()
    io["s1T"] = nc.dram_tensor("s1T", [D, TOK], F32, kind="ExternalInput").ap()
    io["w_router"] = nc.dram_tensor("w_router", [D, NBLK], F32, kind="ExternalInput").ap()
    io["b_router"] = nc.dram_tensor("b_router", [NBLK, 1], F32, kind="ExternalInput").ap()
    io["w1T"] = nc.dram_tensor("w1T", [D, FF], F32, kind="ExternalInput").ap()
    io["b1c"] = nc.dram_tensor("b1c", [128, FCH], F32, kind="ExternalInput").ap()
    io["w2T"] = nc.dram_tensor("w2T", [FF, D], F32, kind="ExternalInput").ap()
    io["b2c"] = nc.dram_tensor("b2c", [128, DCH], F32, kind="ExternalInput").ap()
    io["ones_row"] = nc.dram_tensor("ones_row", [1, 128], F32, kind="ExternalInput").ap()
    io["ident_f32"] = nc.dram_tensor("ident_f32", [128, 128], F32, kind="ExternalInput").ap()
    io["o_out"] = nc.dram_tensor("o_out", [D, TOK], F32, kind="ExternalOutput").ap()
    io["maskdram"] = nc.dram_tensor("maskdram", [NBLK, TOK], F32).ap()
    import contextlib as _cl
    with _cl.ExitStack() as _c:
        block = _c.enter_context(nc.Block())
        sem = [_c.enter_context(nc.semaphore(f"chain{i}")) for i in range(4)]
        p = Prog(nc)
        emit_l1f(nc, p, io)
        p.run(block, sem)
    return nc



def _prep_l1f_inputs(inputs):
    c = np.ascontiguousarray
    f = np.float32

    def cols(v, nch):
        return c(np.asarray(v, f).reshape(nch, 128).T)

    return {
        "w_router": c(np.asarray(inputs["L1_ffn_w_router"], f)),
        "b_router": c(np.asarray(inputs["L1_ffn_b_router"], f).reshape(NBLK, 1)),
        "w1T": c(np.asarray(inputs["L1_ffn_w1"], f).T),
        "b1c": cols(inputs["L1_ffn_b1"], FCH),
        "w2T": c(np.asarray(inputs["L1_ffn_w2"], f).T),
        "b2c": cols(inputs["L1_ffn_b2"], DCH),
        "ones_row": np.ones((1, 128), f),
        "ident_f32": np.eye(128, dtype=f),
    }


def run_l1f(inputs, x1T_stack, xn1T_stack, s1T_stack):
    w = _prep_l1f_inputs(inputs)
    nc = build_l1f()
    in_maps = []
    for b in range(NCORES):
        m = dict(w)
        m["x1T"] = np.ascontiguousarray(x1T_stack[b])
        m["xn1T"] = np.ascontiguousarray(xn1T_stack[b])
        m["s1T"] = np.ascontiguousarray(s1T_stack[b])
        in_maps.append(m)
    res = run_bass_kernel_spmd(nc, in_maps, core_ids=list(range(NCORES)))
    return res.results


RG8 = [list(range(NCORES))]
DEBUG_FUSED = False


def build_fused():
    nc = bass.Bass("TRN2", target_bir_lowering=False)
    E_IN, E_OUT = "ExternalInput", "ExternalOutput"
    d = {}

    def inp(name, shape, dt=F32):
        d[name] = nc.dram_tensor(name, shape, dt, kind=E_IN).ap()
        return d[name]

    inp("xT", [D, TOK])
    inp("ridx", [128, 1], I32)
    inp("sel_g", [64, EXP])
    inp("sel_l", [64, EXP])
    inp("w_inT", [D, 2048]); inp("w_stateT", [D, 128]); inp("w_dtT", [D, 128])
    inp("b_dt", [128, 1]); inp("a_log", [128, 1]); inp("w_cT", [128, D])
    inp("dvec", [128, DCH]); inp("w_outT", [D, D])
    inp("qkv_wT", [D, QKV_O]); inp("proj_wT", [D, D]); inp("router_w", [D, EXP])
    inp("ln1w", [128, DCH]); inp("ln1b", [128, DCH])
    inp("ln2w", [128, DCH]); inp("ln2b", [128, DCH])
    inp("moe_w1", [EXP, D, 2048]); inp("moe_w2", [EXP, 2048, D])
    inp("ones_row8", [1, EXP]); inp("eiota", [EXP, 1])
    inp("tokhalf", [1, 512]); inp("jiota", [1, 512])
    inp("wq", [D, D]); inp("wk", [D, D]); inp("wv", [D, D]); inp("wo", [D, D])
    inp("wpk", [D, D]); inp("wpq", [D, D])
    inp("bq_c", [128, DCH]); inp("bk_c", [128, DCH]); inp("bo_c", [128, DCH])
    inp("bv_r", [1, D], BF16); inp("bpk_r", [1, D], BF16); inp("bpq_r", [1, D], BF16)
    inp("rel_emb", [512, D]); inp("lng_r", [1, D]); inp("lnb_r", [1, D])
    inp("expand1", [512, WJ], BF16); inp("expand2", [512, WJ], BF16)
    inp("w_router", [D, NBLK]); inp("b_router", [NBLK, 1])
    inp("w1T", [D, FF]); inp("b1c", [128, FCH])
    inp("w2T", [FF, D]); inp("b2c", [128, DCH])
    inp("ones_col", [128, 1]); inp("ones_row", [1, 128])
    inp("ones_rowb", [1, 128], BF16)
    inp("ident_bf", [128, 128], BF16); inp("ident_f32", [128, 128])
    inp("iota8", [128, EXP])

    xn_i = nc.dram_tensor("xn_i", [D, TOK], F32)
    s0_i = nc.dram_tensor("s0_i", [D, TOK], F32)
    ag_xn = nc.dram_tensor("ag_xn", [NCORES * D, TOK], F32, addr_space="Shared")
    h1_i = nc.dram_tensor("h1_i", [D, TOK], F32)
    ln2tok_i = nc.dram_tensor("ln2tok_i", [TOK, D], BF16)
    topi_i = nc.dram_tensor("topi_i", [TOK, 2], F32)
    topp_i = nc.dram_tensor("topp_i", [TOK, 2], F32)
    cnt_i = nc.dram_tensor("cnt_i", [EXP, EXP], F32)
    cnt_all = nc.dram_tensor("cnt_all", [64, EXP], F32, addr_space="Shared")
    g_i = nc.dram_tensor("g_i", [D, TOK], F32)
    ag_g = nc.dram_tensor("ag_g", [NCORES * D, TOK], F32, addr_space="Shared")
    x1_i = nc.dram_tensor("x1_i", [D, TOK], F32)
    xn1_i = nc.dram_tensor("xn1_i", [D, TOK], F32)
    s1_i = nc.dram_tensor("s1_i", [D, TOK], F32)
    o_out = nc.dram_tensor("o_out", [D, TOK], F32, kind=E_OUT).ap()

    io_l0a = dict(d)
    io_l0a.update({"o_xn": xn_i.ap(), "o_s0": s0_i.ap()})
    io_b1 = dict(d)
    io_b1.update({"ag_xn": ag_xn.ap().rearrange("a b -> (a b)").unsqueeze(1),
                  "o_h1": h1_i.ap(), "o_ln2tok": ln2tok_i.ap(),
                  "o_topi": topi_i.ap(), "o_topp": topp_i.ap(),
                  "o_counts": cnt_i.ap()})
    io_b2 = dict(d)
    io_b2.update({"ln2tok": ln2tok_i.ap(), "h1T": h1_i.ap(),
                  "topi": topi_i.ap(), "topp": topp_i.ap(),
                  "counts_all": cnt_all.ap(), "o_g": g_i.ap()})
    io_l1a = dict(d)
    io_l1a.update({"s0T": s0_i.ap(), "ag_g": ag_g.ap().rearrange("a b -> (a b)").unsqueeze(1),
                   "o_x1": x1_i.ap(), "o_xn1": xn1_i.ap(), "o_s1": s1_i.ap()})
    io_l1f = dict(d)
    io_l1f.update({"x1T": x1_i.ap(), "xn1T": xn1_i.ap(), "s1T": s1_i.ap(),
                   "o_out": o_out})

    with contextlib.ExitStack() as ctx:
        block = ctx.enter_context(nc.Block())
        sem = [ctx.enter_context(nc.semaphore(f"chain{i}")) for i in range(8)]
        cc_sems = [ctx.enter_context(nc.semaphore(f"ccs{i}")) for i in range(3)]
        p = Prog(nc)

        def coll(ins_t, outs_t, ded):
            def fn(e):
                e.collective_compute(
                    "AllGather", ALU.bypass, RG8,
                    [ins_t.ap().opt()], [outs_t.ap().opt()]).then_inc(ded, 1)
                e.wait_ge(ded, 1)
                return e.nop()
            p.op("pool", fn)

        emit_l0a(nc, p, io_l0a)
        coll(xn_i, ag_xn, cc_sems[0])
        emit_l0b1(nc, p, io_b1)
        coll(cnt_i, cnt_all, cc_sems[1])
        emit_l0b2(nc, p, io_b2)
        coll(g_i, ag_g, cc_sems[2])
        emit_l1a(nc, p, io_l1a)
        emit_l1f(nc, p, io_l1f)
        if DEBUG_FUSED:
            for nm, t in (("d_xn", xn_i), ("d_s0", s0_i), ("d_h1", h1_i),
                          ("d_g", g_i), ("d_x1", x1_i), ("d_s1", s1_i),
                          ("d_xn1", xn1_i)):
                dst = nc.dram_tensor(nm, list(t.ap().shape), F32,
                                     kind="ExternalOutput").ap()
                p.dma("sync", lambda e, dst=dst, t=t: e.dma_start(
                    out=dst, in_=t.ap()))
            dag = nc.dram_tensor("d_agxn", [NCORES * D, TOK], F32,
                                 kind="ExternalOutput").ap()
            p.dma("sync", lambda e, dag=dag: e.dma_start(out=dag, in_=ag_xn.ap()))
        p.run(block, sem)
    return nc


def kernel(**inputs):
    inputs = {k: np.asarray(v) for k, v in inputs.items()}
    w = {}
    w.update(_prep_l0a_inputs(inputs))
    w.update(_prep_l0b1_inputs(inputs))
    w.update(_prep_l1a_inputs(inputs))
    w.update(_prep_l1f_inputs(inputs))
    b2c = _prep_l0b2_inputs(inputs, 0)
    for kk in ("moe_w1", "moe_w2", "ones_row8", "eiota", "tokhalf", "jiota"):
        w[kk] = b2c[kk]
    x = np.asarray(inputs["L0_attn_x"], np.float32)

    nc = build_fused()
    in_maps = []
    for b in range(NCORES):
        m = dict(w)
        sels = _prep_l0b2_inputs(inputs, b)
        m["sel_g"] = sels["sel_g"]
        m["sel_l"] = sels["sel_l"]
        m["xT"] = np.ascontiguousarray(x[b].T)
        m["ridx"] = (np.arange(128, dtype=np.int64) * TOK +
                     b * 256).astype(np.int32).reshape(128, 1)
        in_maps.append(m)
    res = run_bass_kernel_spmd(nc, in_maps, core_ids=list(range(NCORES)))
    out = np.stack([np.asarray(r["o_out"], np.float32).T for r in res.results])
    return np.ascontiguousarray(out)


def kernel_multilaunch(**inputs):
    inputs = {k: np.asarray(v) for k, v in inputs.items()}
    xnT, s0T = run_l0a(inputs)
    b1res = run_l0b1(inputs, xnT)
    b2res = run_l0b2(inputs, b1res)
    gT = np.empty((NCORES, D, TOK), np.float32)
    for cc in range(NCORES):
        gc = np.asarray(b2res[cc]["o_g"], np.float32)
        for s in range(8):
            gT[s][:, cc * 256:(cc + 1) * 256] = gc[:, s * 256:(s + 1) * 256]
    x = np.asarray(inputs["L0_attn_x"], np.float32)
    xT = np.ascontiguousarray(np.swapaxes(x, 1, 2))
    l1a = run_l1a(inputs, xT, s0T, gT)
    x1T = np.stack([np.asarray(r["o_x1"], np.float32) for r in l1a])
    xn1T = np.stack([np.asarray(r["o_xn1"], np.float32) for r in l1a])
    s1T = np.stack([np.asarray(r["o_s1"], np.float32) for r in l1a])
    l1f = run_l1f(inputs, x1T, xn1T, s1T)
    out = np.stack([np.asarray(r["o_out"], np.float32).T for r in l1f])
    return np.ascontiguousarray(out)



NH, NKV, EXP, TOPK = 16, 4, 8, 2
HD = 64
QKV_O = 1536
CAP = 5120


def emit_tp128(p, ps_tp, src, ident, dst):
    p.op("pe", lambda e: e.transpose(ps_tp[:, :], src, ident))
    p.op("dve", lambda e: e.tensor_copy(dst, ps_tp[:, :]))


def emit_l0b1(nc, p, io):
    QCH = QKV_O // 128

    xnT = io.get("xnT")
    qkv_wT = io["qkv_wT"]
    proj_wT = io["proj_wT"]
    router_w = io["router_w"]
    ln1w = io["ln1w"]
    ln1b = io["ln1b"]
    ln2w = io["ln2w"]
    ln2b = io["ln2b"]
    ones_col = io["ones_col"]
    ones_row = io["ones_row"]
    ident_bf = io["ident_bf"]
    ident_f32 = io["ident_f32"]
    iota8 = io["iota8"]

    o_h1 = io["o_h1"]
    o_ln2tok = io["o_ln2tok"]
    o_topi = io["o_topi"]
    o_topp = io["o_topp"]
    o_counts = io["o_counts"]

    w0 = p.mark()
    with contextlib.ExitStack() as ctx:
        sb = SB(nc, ctx)
        xh = sb.t("xh", [128, DCH * TOK], F32)
        lnbf = sb.t("lnbf", [128, DCH * TOK], BF16)
        qkv_tok = sb.t("qkv_tok", [128, 16 * QKV_O], BF16)
        wstage = sb.t("wstage", [128, 2 * 1024], BF16)
        rw_sb = sb.t("rw_sb", [128, DCH * EXP], F32)
        rwp_sb = sb.t("rwp_sb", [128, DCH * EXP], F32)
        rwb_sb = sb.t("rwb_sb", [128, DCH * EXP], F32)
        c12 = sb.t("c12", [1, 2 * EXP], F32)
        c12b = sb.t("c12b", [128, 2 * EXP], F32)
        stage = sb.t("stage", [128, TB], BF16)
        kexp = sb.t("kexp", [128, NH * HD], BF16)
        vexp = sb.t("vexp", [128, NH * HD], BF16)
        prod = sb.t("prod", [128, NH * HD], F32)
        scox = sb.t("scox", [128, NH * 8], F32)
        rsum16 = sb.t("rsum16", [128, NH], F32)
        aoacc = sb.t("aoacc", [128, NH * HD], F32)
        aobf = sb.t("aobf", [128, NH * HD], BF16)
        onec = sb.t("onec", [128, 1], F32)
        oner = sb.t("oner", [1, 128], F32)
        identb = sb.t("identb", [128, 128], BF16)
        identf = sb.t("identf", [128, 128], F32)
        iota8_sb = sb.t("iota8_sb", [128, EXP], F32)
        lncols = sb.t("lncols", [128, 4 * DCH], F32)
        mrow_t = sb.t("mrow_t", [1, TOK], F32)
        rrow = sb.t("rrow", [1, TOK], F32)
        scrow = sb.t("scrow", [1, TOK], F32)
        m_bc = sb.t("m_bc", [128, TB], F32)
        r_bc = sb.t("r_bc", [128, TB], F32)
        scr1 = sb.t("scr1", [128, TB], F32)
        scr2 = sb.t("scr2", [128, TB], F32)
        logits = sb.t("logits", [128, TOK], F32)
        ptile = sb.t("ptile", [128, EXP], F32)
        tcols = sb.t("tcols", [128, 8], F32)
        cnt_row = sb.t("cnt_row", [1, EXP * EXP], F32)
        ps_mm = sb.p("ps_mm", [128, TB], F32)
        ps_tp = sb.p("ps_tp", [128, 128], BF16)
        ps_tp32 = sb.p("ps_tp32", [128, 128], F32)
        ps_bc = sb.p("ps_bc", [128, TB], F32)
        ps_r1 = sb.p("ps_r1", [1, TB], F32)
        ps_r2 = sb.p("ps_r2", [1, TB], F32)
        ps_c = sb.p("ps_c", [1, EXP], F32)


        xch = [xh[:, k * TOK:(k + 1) * TOK] for k in range(DCH)]
        lnch = [lnbf[:, k * TOK:(k + 1) * TOK] for k in range(DCH)]
        mrow = mrow_t

        if "ag_xn" in io:
            ridx_sb = sb.t("ridx_sb", [128, 1], I32)
            p.dma("sync", lambda e: e.dma_start(out=ridx_sb[:], in_=io["ridx"]),
                  wait=w0)
            rmark = p.mark()
            agf = io["ag_xn"]
            for k in range(DCH):
                for s in range(8):
                    p.dma("pool", lambda e, k=k, s=s: e.indirect_dma_start(
                        xch[k][:, s * 256:(s + 1) * 256], None, agf,
                        bass.IndirectOffsetOnAxis(ridx_sb[:], 0),
                        element_offset=(s * 1024 + k * 128) * 2048),
                        wait=rmark)
        else:
            for k in range(DCH):
                p.dma("sync", lambda e, k=k: e.dma_start(
                    out=xch[k], in_=xnT[k * 128:(k + 1) * 128, :]), wait=w0)
        p.dma("sync", lambda e: e.dma_start(out=onec[:], in_=ones_col), wait=w0)
        p.dma("sync", lambda e: e.dma_start(out=oner[:], in_=ones_row), wait=w0)
        p.dma("sync", lambda e: e.dma_start(out=identb[:], in_=ident_bf), wait=w0)
        p.dma("sync", lambda e: e.dma_start(out=identf[:], in_=ident_f32), wait=w0)
        p.dma("sync", lambda e: e.dma_start(out=iota8_sb[:], in_=iota8), wait=w0)
        p.dma("sync", lambda e: e.dma_start(out=lncols[:, 0:DCH], in_=ln1w), wait=w0)
        p.dma("sync", lambda e: e.dma_start(out=lncols[:, DCH:2 * DCH], in_=ln1b), wait=w0)
        p.dma("sync", lambda e: e.dma_start(out=lncols[:, 2 * DCH:3 * DCH], in_=ln2w), wait=w0)
        p.dma("sync", lambda e: e.dma_start(out=lncols[:, 3 * DCH:4 * DCH], in_=ln2b), wait=w0)
        for k in range(DCH):
            p.dma("sync", lambda e, k=k: e.dma_start(
                out=rw_sb[:, k * EXP:(k + 1) * EXP],
                in_=router_w[k * 128:(k + 1) * 128, :]), wait=w0)

        def ln_apply(wcol_off, bcol_off):
            for tb in range(NTB):
                sl = slice(tb * TB, (tb + 1) * TB)
                p.op("pe", lambda e, sl=sl: e.matmul(
                    ps_bc[:, :], oner[:], mrow[0:1, sl], start=True, stop=True))
                p.op("dve", lambda e: e.tensor_copy(m_bc[:], ps_bc[:, :]))
                p.op("pe", lambda e, sl=sl: e.matmul(
                    ps_bc[:, :], oner[:], rrow[0:1, sl], start=True, stop=True))
                p.op("dve", lambda e: e.tensor_copy(r_bc[:], ps_bc[:, :]))
                for k in range(DCH):
                    p.op("dve", lambda e, k=k, sl=sl: e.tensor_sub(
                        scr1[:], xch[k][:, sl], m_bc[:]))
                    p.op("dve", lambda e, k=k: e.tensor_mul(scr1[:], scr1[:], r_bc[:]))
                    p.op("dve", lambda e, k=k, sl=sl: e.tensor_scalar(
                        lnch[k][:, sl], scr1[:],
                        lncols[:, wcol_off + k:wcol_off + k + 1],
                        lncols[:, bcol_off + k:bcol_off + k + 1],
                        ALU.mult, ALU.add))

        emit_ln_rows(p, xch, scr2, ps_r1, ps_r2, onec[:], mrow[0:1, 0:TOK],
                     rrow[:], scrow[:])
        ln_apply(0, DCH)

        QCH = QKV_O // 128
        for ec in range(QCH):
            wslot = ec % 2
            for k in range(DCH):
                p.dma("pool", lambda e, ec=ec, k=k, wslot=wslot: e.dma_start(
                    out=wstage[:, wslot * 1024 + k * 128:wslot * 1024 + (k + 1) * 128],
                    in_=qkv_wT[k * 128:(k + 1) * 128, ec * 128:(ec + 1) * 128]))
            for tb in range(NTB):
                for k in range(DCH):
                    p.op("pe", lambda e, ec=ec, tb=tb, k=k, wslot=wslot: e.matmul(
                        ps_mm[:, :],
                        wstage[:, wslot * 1024 + k * 128:wslot * 1024 + (k + 1) * 128],
                        lnch[k][:, tb * TB:(tb + 1) * TB],
                        start=(k == 0), stop=(k == DCH - 1)))
                p.op("act", lambda e: e.activation(stage[:, :], ps_mm[:, :], AF.Copy))
                for j in range(4):
                    tile = tb * 4 + j
                    p.op("pe", lambda e, j=j: e.transpose(
                        ps_tp[:, :], stage[:, j * 128:(j + 1) * 128], identb[:]))
                    p.op("dve", lambda e, tile=tile, ec=ec: e.tensor_copy(
                        qkv_tok[:, tile * QKV_O + ec * 128:tile * QKV_O + (ec + 1) * 128],
                        ps_tp[:, :]))

        def tok_tile(t):
            return qkv_tok[:, t * QKV_O:(t + 1) * QKV_O]

        aoT = lnbf
        for s in range(8):
            for ic in range(2):
                ti = s * 2 + ic
                qv = tok_tile(ti).rearrange("p (g b) -> p g b", b=384)
                for t in range(s + 1):
                    tj = t * 2 + ic
                    kv = tok_tile(tj).rearrange("p (g b) -> p g b", b=384)
                    k_ap = kv[:, :, 256:320].unsqueeze(2).broadcast_to([128, NKV, 4, HD])
                    v_ap = kv[:, :, 320:384].unsqueeze(2).broadcast_to([128, NKV, 4, HD])
                    p.op("dve", lambda e, k_ap=k_ap: e.tensor_copy(
                        kexp[:].rearrange("p (g r d) -> p g r d", g=NKV, r=4), k_ap))
                    p.op("dve", lambda e, v_ap=v_ap: e.tensor_copy(
                        vexp[:].rearrange("p (g r d) -> p g r d", g=NKV, r=4), v_ap))
                    p.op("dve", lambda e, qv=qv: e.tensor_tensor(
                        prod[:].rearrange("p (g b) -> p g b", g=NKV),
                        qv[:, :, 0:256], kexp[:].rearrange("p (g b) -> p g b", g=NKV),
                        ALU.mult))
                    p.op("dve", lambda e, t=t: e.tensor_reduce(
                        scox[:].rearrange("p (h t) -> p h t", t=8)[:, :, t],
                        prod[:].rearrange("p (h d) -> p h d", d=HD), AX.X, ALU.add))
                sc3 = scox[:].rearrange("p (h t) -> p h t", t=8)[:, :, 0:s + 1]
                p.op("act", lambda e, sc3=sc3: e.activation(
                    sc3, sc3, AF.Exp, scale=0.125))
                p.op("dve", lambda e, sc3=sc3: e.tensor_reduce(
                    rsum16[:], sc3, AX.X, ALU.add))
                p.op("dve", lambda e: e.reciprocal(rsum16[:], rsum16[:]))
                p.op("dve", lambda e, sc3=sc3, s=s: e.tensor_tensor(
                    sc3, sc3, rsum16[:].unsqueeze(2).broadcast_to([128, NH, s + 1]),
                    ALU.mult))
                for t in range(s + 1):
                    tj = t * 2 + ic
                    kv = tok_tile(tj).rearrange("p (g b) -> p g b", b=384)
                    v_ap = kv[:, :, 320:384].unsqueeze(2).broadcast_to([128, NKV, 4, HD])
                    p.op("dve", lambda e, v_ap=v_ap: e.tensor_copy(
                        vexp[:].rearrange("p (g r d) -> p g r d", g=NKV, r=4), v_ap))
                    pb = scox[:].rearrange("p (h t) -> p h t", t=8)[:, :, t:t + 1]
                    pb = pb.broadcast_to([128, NH, HD])
                    if t == 0:
                        p.op("dve", lambda e, pb=pb: e.tensor_tensor(
                            aoacc[:].rearrange("p (h d) -> p h d", d=HD),
                            vexp[:].rearrange("p (h d) -> p h d", d=HD), pb, ALU.mult))
                    else:
                        p.op("dve", lambda e, pb=pb: e.tensor_tensor(
                            prod[:].rearrange("p (h d) -> p h d", d=HD),
                            vexp[:].rearrange("p (h d) -> p h d", d=HD), pb, ALU.mult))
                        p.op("dve", lambda e: e.tensor_add(aoacc[:], aoacc[:], prod[:]))
                p.op("dve", lambda e: e.tensor_copy(aobf[:], aoacc[:]))
                for j in range(DCH):
                    p.op("pe", lambda e, j=j: e.transpose(
                        ps_tp[:, :], aobf[:, j * 128:(j + 1) * 128], identb[:]))
                    p.op("dve", lambda e, j=j, ti=ti: e.tensor_copy(
                        aoT[:, j * TOK + ti * 128:j * TOK + ti * 128 + 128],
                        ps_tp[:, :]))

        for ec in range(DCH):
            wslot = ec % 2
            for k in range(DCH):
                p.dma("pool", lambda e, ec=ec, k=k, wslot=wslot: e.dma_start(
                    out=wstage[:, wslot * 1024 + k * 128:wslot * 1024 + (k + 1) * 128],
                    in_=proj_wT[k * 128:(k + 1) * 128, ec * 128:(ec + 1) * 128]))
            for tb in range(NTB):
                for k in range(DCH):
                    p.op("pe", lambda e, ec=ec, tb=tb, k=k, wslot=wslot: e.matmul(
                        ps_mm[:, :],
                        wstage[:, wslot * 1024 + k * 128:wslot * 1024 + (k + 1) * 128],
                        aoT[:, k * TOK + tb * TB:k * TOK + (tb + 1) * TB],
                        start=(k == 0), stop=(k == DCH - 1)))
                p.op("dve", lambda e, ec=ec, tb=tb: e.tensor_add(
                    xch[ec][:, tb * TB:(tb + 1) * TB],
                    xch[ec][:, tb * TB:(tb + 1) * TB], ps_mm[:, :]))
            p.dma("sync", lambda e, ec=ec: e.dma_start(
                out=o_h1[ec * 128:(ec + 1) * 128, :], in_=xch[ec]))

        emit_ln_rows(p, xch, scr2, ps_r1, ps_r2, onec[:], mrow[0:1, 0:TOK],
                     rrow[:], scrow[:])
        ln_apply(2 * DCH, 3 * DCH)
        for ti in range(16):
            for j in range(DCH):
                p.op("pe", lambda e, ti=ti, j=j: e.transpose(
                    ps_tp[:, :], lnch[j][:, ti * 128:(ti + 1) * 128], identb[:]))
                p.op("dve", lambda e, j=j: e.tensor_copy(
                    aobf[:, j * 128:(j + 1) * 128], ps_tp[:, :]))
            p.dma("sync", lambda e, ti=ti: e.dma_start(
                out=o_ln2tok[ti * 128:(ti + 1) * 128, :], in_=aobf[:]))

        for k in range(DCH):
            p.op("dve", lambda e, k=k: e.tensor_scalar_mul(
                rwp_sb[:, k * EXP:(k + 1) * EXP], rw_sb[:, k * EXP:(k + 1) * EXP],
                lncols[:, 2 * DCH + k:2 * DCH + k + 1]))
            p.op("dve", lambda e, k=k: e.tensor_scalar_mul(
                rwb_sb[:, k * EXP:(k + 1) * EXP], rw_sb[:, k * EXP:(k + 1) * EXP],
                lncols[:, 3 * DCH + k:3 * DCH + k + 1]))
        for k in range(DCH):
            p.op("pe", lambda e, k=k: e.matmul(
                ps_c[:, :], onec[:], rwp_sb[:, k * EXP:(k + 1) * EXP],
                start=(k == 0), stop=(k == DCH - 1)))
        p.op("act", lambda e: e.activation(c12[0:1, 0:EXP], ps_c[:, :], AF.Copy))
        for k in range(DCH):
            p.op("pe", lambda e, k=k: e.matmul(
                ps_c[:, :], onec[:], rwb_sb[:, k * EXP:(k + 1) * EXP],
                start=(k == 0), stop=(k == DCH - 1)))
        p.op("act", lambda e: e.activation(c12[0:1, EXP:2 * EXP], ps_c[:, :], AF.Copy))
        p.op("pe", lambda e: e.matmul(
            ps_bc[:, 0:2 * EXP], oner[:], c12[0:1, 0:2 * EXP], start=True, stop=True))
        p.op("dve", lambda e: e.tensor_copy(c12b[:], ps_bc[:, 0:2 * EXP]))
        for tb in range(NTB):
            for k in range(DCH):
                p.op("pe", lambda e, tb=tb, k=k: e.matmul(
                    ps_mm[0:EXP, 0:TB], rwp_sb[:, k * EXP:(k + 1) * EXP],
                    xch[k][:, tb * TB:(tb + 1) * TB],
                    start=(k == 0), stop=(k == DCH - 1)))
            p.op("act", lambda e, tb=tb: e.activation(
                logits[0:EXP, tb * TB:(tb + 1) * TB], ps_mm[0:EXP, 0:TB], AF.Copy))
        ms1 = prod[:, 0:EXP]
        ms2 = prod[:, EXP:2 * EXP]
        tmp8 = prod[:, 2 * EXP:3 * EXP]
        for s in range(8):
            for ic in range(2):
                ti = s * 2 + ic
                p.op("pe", lambda e, ti=ti: e.transpose(
                    ps_tp32[:, 0:EXP], logits[0:EXP, ti * 128:(ti + 1) * 128],
                    identf[0:EXP, 0:EXP]))
                p.op("dve", lambda e: e.tensor_copy(ptile[:], ps_tp32[:, 0:EXP]))
                p.op("pe", lambda e, ti=ti: e.transpose(
                    ps_tp32[:, 0:1], mrow[0:1, ti * 128:(ti + 1) * 128],
                    identf[0:1, 0:1]))
                p.op("dve", lambda e: e.tensor_copy(tcols[:, 4:5], ps_tp32[:, 0:1]))
                p.op("pe", lambda e, ti=ti: e.transpose(
                    ps_tp32[:, 0:1], rrow[0:1, ti * 128:(ti + 1) * 128],
                    identf[0:1, 0:1]))
                p.op("dve", lambda e: e.tensor_copy(tcols[:, 5:6], ps_tp32[:, 0:1]))
                p.op("dve", lambda e: e.tensor_scalar_mul(
                    tmp8, c12b[:, 0:EXP], tcols[:, 4:5]))
                p.op("dve", lambda e: e.tensor_sub(ptile[:], ptile[:], tmp8))
                p.op("dve", lambda e: e.tensor_scalar_mul(
                    ptile[:], ptile[:], tcols[:, 5:6]))
                p.op("dve", lambda e: e.tensor_add(ptile[:], ptile[:], c12b[:, EXP:2 * EXP]))
                p.op("dve", lambda e: e.tensor_reduce(
                    tcols[:, 0:1], ptile[:], AX.X, ALU.max))
                p.op("dve", lambda e: e.tensor_scalar_sub(
                    ptile[:], ptile[:], tcols[:, 0:1]))
                p.op("act", lambda e: e.activation(ptile[:], ptile[:], AF.Exp))
                p.op("dve", lambda e: e.tensor_reduce(
                    tcols[:, 1:2], ptile[:], AX.X, ALU.add))
                p.op("dve", lambda e: e.reciprocal(tcols[:, 1:2], tcols[:, 1:2]))
                p.op("dve", lambda e: e.tensor_scalar_mul(
                    ptile[:], ptile[:], tcols[:, 1:2]))
                p.op("dve", lambda e: e.tensor_reduce(
                    tcols[:, 0:1], ptile[:], AX.X, ALU.max))
                p.op("dve", lambda e: e.tensor_scalar(
                    ms1, ptile[:], tcols[:, 0:1], None, ALU.is_equal))
                p.op("dve", lambda e: e.tensor_mul(tmp8, ms1, iota8_sb[:]))
                p.op("dve", lambda e: e.tensor_reduce(
                    tcols[:, 2:3], tmp8, AX.X, ALU.add))
                p.op("dve", lambda e: e.tensor_scalar(
                    tmp8, ms1, -1.0, 1.0, ALU.mult, ALU.add))
                p.op("dve", lambda e: e.tensor_mul(tmp8, tmp8, ptile[:]))
                p.op("dve", lambda e: e.tensor_reduce(
                    tcols[:, 1:2], tmp8, AX.X, ALU.max))
                p.op("dve", lambda e: e.tensor_scalar(
                    ms2, tmp8, tcols[:, 1:2], None, ALU.is_equal))
                p.op("dve", lambda e: e.tensor_mul(tmp8, ms2, iota8_sb[:]))
                p.op("dve", lambda e: e.tensor_reduce(
                    tcols[:, 3:4], tmp8, AX.X, ALU.add))
                p.op("dve", lambda e: e.tensor_add(
                    tcols[:, 4:5], tcols[:, 0:1], tcols[:, 1:2]))
                p.op("dve", lambda e: e.reciprocal(tcols[:, 5:6], tcols[:, 4:5]))
                p.op("dve", lambda e: e.tensor_mul(
                    tcols[:, 6:7], tcols[:, 0:1], tcols[:, 5:6]))
                p.op("dve", lambda e: e.tensor_mul(
                    tcols[:, 7:8], tcols[:, 1:2], tcols[:, 5:6]))
                p.dma("sync", lambda e, ti=ti: e.dma_start(
                    out=o_topi[ti * 128:(ti + 1) * 128, :], in_=tcols[:, 2:4]))
                p.dma("sync", lambda e, ti=ti: e.dma_start(
                    out=o_topp[ti * 128:(ti + 1) * 128, :], in_=tcols[:, 6:8]))
                p.op("dve", lambda e: e.tensor_add(ms1, ms1, ms2))
                p.op("pe", lambda e, ic=ic: e.matmul(
                    ps_c[:, :], onec[:], ms1, start=(ic == 0), stop=(ic == 1)))
            p.op("act", lambda e, s=s: e.activation(
                cnt_row[0:1, s * EXP:(s + 1) * EXP], ps_c[:, :], AF.Copy))
        p.dma("sync", lambda e: e.dma_start(
            out=o_counts.rearrange("a b -> (a b)").unsqueeze(0), in_=cnt_row[:]))



def build_l0b1():
    nc = bass.Bass("TRN2", target_bir_lowering=False)
    io = {}
    io["xnT"] = nc.dram_tensor("xnT", [D, TOK], F32, kind="ExternalInput").ap()
    io["qkv_wT"] = nc.dram_tensor("qkv_wT", [D, QKV_O], F32, kind="ExternalInput").ap()
    io["proj_wT"] = nc.dram_tensor("proj_wT", [D, D], F32, kind="ExternalInput").ap()
    io["router_w"] = nc.dram_tensor("router_w", [D, EXP], F32, kind="ExternalInput").ap()
    io["ln1w"] = nc.dram_tensor("ln1w", [128, DCH], F32, kind="ExternalInput").ap()
    io["ln1b"] = nc.dram_tensor("ln1b", [128, DCH], F32, kind="ExternalInput").ap()
    io["ln2w"] = nc.dram_tensor("ln2w", [128, DCH], F32, kind="ExternalInput").ap()
    io["ln2b"] = nc.dram_tensor("ln2b", [128, DCH], F32, kind="ExternalInput").ap()
    io["ones_col"] = nc.dram_tensor("ones_col", [128, 1], F32, kind="ExternalInput").ap()
    io["ones_row"] = nc.dram_tensor("ones_row", [1, 128], F32, kind="ExternalInput").ap()
    io["ident_bf"] = nc.dram_tensor("ident_bf", [128, 128], BF16, kind="ExternalInput").ap()
    io["ident_f32"] = nc.dram_tensor("ident_f32", [128, 128], F32, kind="ExternalInput").ap()
    io["iota8"] = nc.dram_tensor("iota8", [128, EXP], F32, kind="ExternalInput").ap()
    io["o_h1"] = nc.dram_tensor("o_h1", [D, TOK], F32, kind="ExternalOutput").ap()
    io["o_ln2tok"] = nc.dram_tensor("o_ln2tok", [TOK, D], BF16, kind="ExternalOutput").ap()
    io["o_topi"] = nc.dram_tensor("o_topi", [TOK, 2], F32, kind="ExternalOutput").ap()
    io["o_topp"] = nc.dram_tensor("o_topp", [TOK, 2], F32, kind="ExternalOutput").ap()
    io["o_counts"] = nc.dram_tensor("o_counts", [EXP, EXP], F32, kind="ExternalOutput").ap()
    import contextlib as _cl
    with _cl.ExitStack() as _c:
        block = _c.enter_context(nc.Block())
        sem = [_c.enter_context(nc.semaphore(f"chain{i}")) for i in range(4)]
        p = Prog(nc)
        emit_l0b1(nc, p, io)
        p.run(block, sem)
    return nc
